# version 01
# baseline (speedup 1.0000x reference)
import contextlib
import numpy as np
import ml_dtypes
import concourse.bass as bass
import concourse.mybir as mybir
from concourse.bass_utils import run_bass_kernel_spmd

F32 = mybir.dt.float32
BF16 = mybir.dt.bfloat16
AF = mybir.ActivationFunctionType
ALU = mybir.AluOpType
AX = mybir.AxisListType

S = 16384
D = 2048
HCN = 16
NT = 128
OWNT = 16
OWN = 2048
KT = 512
NKT = 32
NQT = 16
NG = 8
INTER = 5632
FCN = 44
OCN = 16
EPS = 1e-5
RSCALE = float(np.float32(1.0) / np.float32(np.sqrt(np.float32(2048.0))))

C_BQ = 0
C_BV = 16
C_BO = 32
C_DS = 48
C_GS = 64
C_US = 108

ST_P0 = 0
ST_MX = 64
ST_NEGM = 128
ST_SE = 130
ST_RS = 132
ST_P3A = 140
ST_P3B = 200
ST_P4 = 260


def build(dbg=False):
    nc = bass.Bass("TRN2", target_bir_lowering=False)

    x_d = nc.dram_tensor("x", [S, D], F32, kind="ExternalInput").ap()
    wq_p = nc.dram_tensor("wq_p", [16, 128, D], F32, kind="ExternalInput").ap()
    wk_p = nc.dram_tensor("wk_p", [16, 128, D], F32, kind="ExternalInput").ap()
    wv_p = nc.dram_tensor("wv_p", [16, 128, D], F32, kind="ExternalInput").ap()
    wo_p = nc.dram_tensor("wo_p", [16, 128, D], F32, kind="ExternalInput").ap()
    wg_p = nc.dram_tensor("wg_p", [44, 128, D], F32, kind="ExternalInput").ap()
    wu_p = nc.dram_tensor("wu_p", [44, 128, D], F32, kind="ExternalInput").ap()
    wd_p = nc.dram_tensor("wd_p", [16, 128, INTER], F32, kind="ExternalInput").ap()
    bq_d = nc.dram_tensor("bq", [128, 16], F32, kind="ExternalInput").ap()
    bv_d = nc.dram_tensor("bv", [128, 16], F32, kind="ExternalInput").ap()
    bo_d = nc.dram_tensor("bo", [128, 16], F32, kind="ExternalInput").ap()
    ds_d = nc.dram_tensor("ds", [128, 16], F32, kind="ExternalInput").ap()
    gs_d = nc.dram_tensor("gs", [128, 44], F32, kind="ExternalInput").ap()
    us_d = nc.dram_tensor("us", [128, 44], F32, kind="ExternalInput").ap()
    id_d = nc.dram_tensor("ident", [128, 128], BF16, kind="ExternalInput").ap()
    out_d = nc.dram_tensor("out", [OWN, D], F32, kind="ExternalOutput").ap()

    Ld = nc.dram_tensor("Ld", [S, D], BF16, kind="Internal").ap()
    LTd = nc.dram_tensor("LTd", [16, 128, S], BF16, kind="Internal").ap()
    QpTd = nc.dram_tensor("QpTd", [NG, 128, HCN * 256], BF16, kind="Internal").ap()
    CTd = nc.dram_tensor("CTd", [2, 128, HCN * 1024], BF16, kind="Internal").ap()
    HDNd = nc.dram_tensor("HDNd", [4, 128, FCN * 512], BF16, kind="Internal").ap()
    yd = nc.dram_tensor("yd", [OWN, D], F32, kind="Internal").ap()

    if dbg:
        dbgL = nc.dram_tensor("dbgL", [256, D], F32, kind="ExternalOutput").ap()
        dbgLT = nc.dram_tensor("dbgLT", [128, 512], F32, kind="ExternalOutput").ap()
        dbgQpT = nc.dram_tensor("dbgQpT", [128, HCN * 256], F32, kind="ExternalOutput").ap()
        dbgCT = nc.dram_tensor("dbgCT", [128, 2048], F32, kind="ExternalOutput").ap()
        dbgY = nc.dram_tensor("dbgY", [256, D], F32, kind="ExternalOutput").ap()
        dbgHDN = nc.dram_tensor("dbgHDN", [128, 2048], F32, kind="ExternalOutput").ap()

    Ld_v = Ld.rearrange("(k p) d -> k p d", p=128)
    x_v = x_d.rearrange("(k p) d -> k p d", p=128)
    yd_v = yd.rearrange("(k p) d -> k p d", p=128)
    out_v = out_d.rearrange("(k p) d -> k p d", p=128)

    es = contextlib.ExitStack()
    with es:
        E = es.enter_context
        BIG0 = E(nc.sbuf_tensor("BIG0", [128, 32768], BF16))
        BIG1 = E(nc.sbuf_tensor("BIG1", [128, 16384], BF16))
        STRM = E(nc.sbuf_tensor("STRM", [128, 16384], BF16))
        WCOL = E(nc.sbuf_tensor("WCOL", [128, 8192], BF16))
        QPT = E(nc.sbuf_tensor("QPT", [128, 4096], BF16))
        FT = E(nc.sbuf_tensor("FT", [128, 4096], BF16))
        SMALL = E(nc.sbuf_tensor("SMALL", [128, 6144], F32))
        SMALLB = E(nc.sbuf_tensor("SMALLB", [128, 4096], BF16))
        LTST = E(nc.sbuf_tensor("LTST", [128, 4096], BF16))
        STATS = E(nc.sbuf_tensor("STATS", [128, 512], F32))
        CONST = E(nc.sbuf_tensor("CONST", [128, 256], F32))
        IDT = E(nc.sbuf_tensor("IDT", [128, 128], BF16))
        pb = [E(nc.psum_tensor("pb%d" % i, [128, 512], F32)) for i in range(8)]

        sem_names = [
            "dCONST", "dX", "vP0", "aP0", "tTR", "vLT", "dLst", "dLTst",
            "dW1", "tM1", "aE1", "tM2", "aE2", "dQP",
            "dQG", "dLTA", "tS", "aS", "vKM", "vFM", "aEX", "vPS",
            "tTP", "vPT", "dLC", "tCk", "vCT", "dCT",
            "dCTl", "dW3", "tM3", "aE3", "tM4", "aE4", "tTR3", "vTR3",
            "vP3", "aP3", "dXo", "dY",
            "dW4", "tG", "tU", "aG", "aU", "vH", "dH",
            "dWd", "dHl", "tD", "aD", "tTF", "vTF", "vP4", "aP4", "dYl", "dOUT",
        ]
        sm = {n: E(nc.semaphore(n)) for n in sem_names}
        block = E(nc.Block())

        bb = [pb[i][:].bitcast(BF16) for i in range(8)]

        BIG0q = BIG0[:].rearrange("p (q j) -> p q j", q=2)

        def st_slot(base, p):
            o = base + p * 30
            return dict(bn=STATS[:, o:o + 24], mv=STATS[:, o + 24:o + 26],
                        t1=STATS[:, o + 26:o + 27], t2=STATS[:, o + 27:o + 28],
                        rstd=STATS[:, o + 28:o + 29], t3=STATS[:, o + 29:o + 30])

        def ln_dve_round(v, src_ap, sl, selfwait):
            raise NotImplementedError

        @block.sync
        def _(q):
            q.dma_start(out=IDT[:], in_=id_d).then_inc(sm["dCONST"], 16)
            q.dma_start(out=CONST[:, C_BQ:C_BQ + 16], in_=bq_d).then_inc(sm["dCONST"], 16)
            q.dma_start(out=CONST[:, C_BV:C_BV + 16], in_=bv_d).then_inc(sm["dCONST"], 16)
            q.dma_start(out=CONST[:, C_BO:C_BO + 16], in_=bo_d).then_inc(sm["dCONST"], 16)
            q.dma_start(out=CONST[:, C_DS:C_DS + 16], in_=ds_d).then_inc(sm["dCONST"], 16)
            q.dma_start(out=CONST[:, C_GS:C_GS + 44], in_=gs_d).then_inc(sm["dCONST"], 16)
            q.dma_start(out=CONST[:, C_US:C_US + 44], in_=us_d).then_inc(sm["dCONST"], 16)
            for t in range(NT):
                p = t % 2
                if t >= 2 and t % 2 == 0:
                    tp = t - 2
                    q.wait_ge(sm["vLT"], 16 * tp + 32)
                    if tp < OWNT:
                        src = BIG0[:].rearrange("p (c i) -> p c i", c=16)[:, :, tp * 128:tp * 128 + 256]
                    else:
                        src = LTST[:].rearrange("p (c j) -> p c j", c=16)
                    q.dma_start(out=LTd[:, :, tp * 128:tp * 128 + 256].rearrange("c p s -> p c s"),
                                in_=src).then_inc(sm["dLTst"], 16)
                if t >= 2:
                    q.wait_ge(sm["aP0"], 2 * (t - 1))
                q.dma_start(out=SMALL[:, p * 2048:(p + 1) * 2048], in_=x_v[t]).then_inc(sm["dX"], 16)
            q.wait_ge(sm["vLT"], 16 * NT)
            q.dma_start(out=LTd[:, :, (NT - 2) * 128:(NT - 2) * 128 + 256].rearrange("c p s -> p c s"),
                        in_=LTST[:].rearrange("p (c j) -> p c j", c=16)).then_inc(sm["dLTst"], 16)
            for ih in range(2):
                for hcp in range(16):
                    for it in range(2):
                        g2 = ih * 32 + hcp * 2 + it
                        r = g2 % 4
                        q.wait_ge(sm["aE2"], g2 + 1)
                        for half in range(2):
                            gabs = ih * 4 + it * 2 + half
                            q.dma_start(out=QpTd[gabs][:, hcp * 256:(hcp + 1) * 256],
                                        in_=FT[:, r * 512 + half * 256: r * 512 + (half + 1) * 256]
                                        ).then_inc(sm["dQP"], 16)
            for g in range(NG):
                if g == 0:
                    q.wait_ge(sm["dQP"], 16 * 128)
                else:
                    q.wait_ge(sm["tS"], g * 64)
                q.dma_start(out=QPT[:], in_=QpTd[g]).then_inc(sm["dQG"], 16)
                for kt in range(NKT):
                    slot = kt % 2
                    if kt >= 2:
                        q.wait_ge(sm["tS"], g * 64 + 2 * kt - 2)
                    elif g > 0:
                        q.wait_ge(sm["tCk"], g * 32)
                    for hq in range(4):
                        q.dma_start(
                            out=STRM[:, slot * 8192 + hq * 4 * 512: slot * 8192 + (hq + 1) * 4 * 512]
                                .rearrange("p (c j) -> p c j", c=4),
                            in_=LTd[hq * 4:(hq + 1) * 4, :, kt * 512:(kt + 1) * 512]
                                .rearrange("c p s -> p c s"),
                        ).then_inc(sm["dLTA"], 16)
            for ih in range(2):
                q.wait_ge(sm["dCT"], 1024 * (ih + 1))
                if ih == 0:
                    q.wait_ge(sm["vPT"], 2048)
                else:
                    q.wait_ge(sm["tM3"], 32)
                q.dma_start(out=BIG1[:], in_=CTd[ih]).then_inc(sm["dCTl"], 16)
                for tt in range(8):
                    T = ih * 8 + tt
                    p = T % 2
                    if T < 2:
                        q.wait_ge(sm["aP0"], 2 * NT)
                    if T >= 2:
                        q.wait_ge(sm["aP3"], 4 * (T - 2) + 4)
                        q.wait_ge(sm["dY"], 16 * (T - 1))
                    q.dma_start(out=SMALL[:, p * 2048:(p + 1) * 2048], in_=x_v[T]).then_inc(sm["dXo"], 16)
                    q.wait_ge(sm["vP3"], 11 * T + 6)
                    q.dma_start(out=yd_v[T], in_=SMALL[:, p * 2048:(p + 1) * 2048]).then_inc(sm["dY"], 16)
            for fc in range(FCN):
                for it in range(4):
                    idx = fc * 4 + it
                    q.wait_ge(sm["vH"], idx + 1)
                    q.dma_start(out=HDNd[it][:, fc * 512:(fc + 1) * 512],
                                in_=LTST[:, (idx % 4) * 512:(idx % 4 + 1) * 512]).then_inc(sm["dH"], 16)
            for it2 in range(4):
                if it2 == 0:
                    q.wait_ge(sm["dH"], 16 * FCN * 4)
                    q.wait_ge(sm["tG"], FCN * 4)
                    q.wait_ge(sm["tU"], FCN * 4)
                    q.wait_ge(sm["vTR3"], 512)
                else:
                    q.wait_ge(sm["tD"], 16 * it2)
                q.dma_start(out=BIG0[:, 0:FCN * 512], in_=HDNd[it2]).then_inc(sm["dHl"], 16)
                for tt in range(4):
                    To = it2 * 4 + tt
                    p = To % 2
                    if To == 0:
                        q.wait_ge(sm["dY"], 16 * 16)
                    if To < 2:
                        q.wait_ge(sm["aP3"], 64)
                    if To >= 2:
                        q.wait_ge(sm["dOUT"], 16 * (To - 1))
                    q.dma_start(out=SMALL[:, p * 2048:(p + 1) * 2048], in_=yd_v[To]).then_inc(sm["dYl"], 16)
                    q.wait_ge(sm["vP4"], 6 * To + 6)
                    q.dma_start(out=out_v[To], in_=SMALL[:, p * 2048:(p + 1) * 2048]).then_inc(sm["dOUT"], 16)

        @block.gpsimd
        def _(g_):
            for t in range(NT):
                p = t % 2
                g_.wait_ge(sm["aP0"], 2 * t + 2)
                g_.dma_start(out=Ld_v[t], in_=SMALLB[:, p * 2048:(p + 1) * 2048]).then_inc(sm["dLst"], 16)
            nw = 0
            for ih in range(2):
                for oc in range(16):
                    w = oc % 2
                    if ih == 1 and oc < 2:
                        g_.wait_ge(sm["tM2"], 32)
                    if oc >= 2:
                        g_.wait_ge(sm["tM1"], ih * 32 + 2 * oc - 2)
                    g_.dma_start(out=WCOL[:, w * 2048:(w + 1) * 2048], in_=wq_p[oc]).then_inc(sm["dW1"], 16)
                    nw += 1
                for hcp in range(16):
                    w = hcp % 2
                    g_.wait_ge(sm["tM1"], (ih + 1) * 32)
                    if hcp >= 2:
                        g_.wait_ge(sm["tM2"], ih * 32 + 2 * (hcp - 1))
                    g_.dma_start(out=WCOL[:, w * 2048:(w + 1) * 2048], in_=wk_p[hcp]).then_inc(sm["dW1"], 16)
                    nw += 1
            for g in range(NG):
                for kt in range(NKT):
                    slot = kt % 2
                    if kt >= 2:
                        g_.wait_ge(sm["tCk"], g * 32 + kt - 1)
                    else:
                        g_.wait_ge(sm["tS"], g * 64 + 62 + 2 * kt)
                    g_.dma_start(out=STRM[:, slot * 8192:(slot + 1) * 8192]
                                 .rearrange("p (c j) -> p c j", c=4),
                                 in_=Ld_v[kt * 4:(kt + 1) * 4, :, :]
                                 .rearrange("c p s -> p c s")).then_inc(sm["dLC"], 16)
                for hc in range(16):
                    g_.wait_ge(sm["vCT"], g * 16 + hc + 1)
                    g_.dma_start(out=CTd[g // 4][:, hc * 1024 + (g % 4) * 256: hc * 1024 + (g % 4) * 256 + 256],
                                 in_=FT[:, hc * 256:(hc + 1) * 256]).then_inc(sm["dCT"], 16)
            for ih in range(2):
                for oc in range(16):
                    w = oc % 2
                    if ih == 0 and oc < 2:
                        g_.wait_ge(sm["tM2"], 64)
                    if ih == 1 and oc < 2:
                        g_.wait_ge(sm["tM4"], 32)
                    if oc >= 2:
                        g_.wait_ge(sm["tM3"], ih * 32 + 2 * oc - 2)
                    g_.dma_start(out=WCOL[:, w * 2048:(w + 1) * 2048], in_=wv_p[oc]).then_inc(sm["dW3"], 16)
                for ocp in range(16):
                    w = ocp % 2
                    g_.wait_ge(sm["tM3"], (ih + 1) * 32)
                    if ocp >= 2:
                        g_.wait_ge(sm["tM4"], ih * 32 + 2 * (ocp - 1))
                    g_.dma_start(out=WCOL[:, w * 2048:(w + 1) * 2048], in_=wo_p[ocp]).then_inc(sm["dW3"], 16)
            for fc in range(FCN):
                wslot = fc % 2
                if fc < 2:
                    g_.wait_ge(sm["tM4"], 64)
                else:
                    g_.wait_ge(sm["tG"], 4 * (fc - 1))
                g_.dma_start(out=WCOL[:, wslot * 2048:(wslot + 1) * 2048], in_=wg_p[fc]).then_inc(sm["dW4"], 16)
                if fc >= 2:
                    g_.wait_ge(sm["tU"], 4 * (fc - 1))
                g_.dma_start(out=WCOL[:, 4096 + wslot * 2048: 4096 + (wslot + 1) * 2048],
                             in_=wu_p[fc]).then_inc(sm["dW4"], 16)
            for it2 in range(4):
                for oc in range(16):
                    jdx = it2 * 16 + oc
                    slot = oc % 2
                    if jdx < 2:
                        g_.wait_ge(sm["tG"], FCN * 4)
                        g_.wait_ge(sm["tU"], FCN * 4)
                    else:
                        g_.wait_ge(sm["tD"], jdx - 1)
                    g_.dma_start(out=STRM[:, slot * 8192: slot * 8192 + INTER],
                                 in_=wd_p[oc]).then_inc(sm["dWd"], 16)
            if dbg:
                g_.wait_ge(sm["dLst"], 16 * NT)
                g_.dma_start(out=dbgL, in_=Ld[0:256, :]).then_inc(sm["dOUT"], 16)
                g_.wait_ge(sm["dLTst"], 16 * 64)
                g_.dma_start(out=dbgLT, in_=LTd[2, 0:128, 0:512]).then_inc(sm["dOUT"], 16)
                g_.wait_ge(sm["dQP"], 16 * 128)
                g_.dma_start(out=dbgQpT, in_=QpTd[1]).then_inc(sm["dOUT"], 16)
                g_.wait_ge(sm["dCT"], 2048)
                g_.dma_start(out=dbgCT, in_=CTd[0][:, 0:2048]).then_inc(sm["dOUT"], 16)
                g_.wait_ge(sm["dY"], 256)
                g_.dma_start(out=dbgY, in_=yd[0:256, :]).then_inc(sm["dOUT"], 16)
                g_.wait_ge(sm["dH"], 16 * FCN * 4)
                g_.dma_start(out=dbgHDN, in_=HDNd[0][:, 0:2048]).then_inc(sm["dOUT"], 16)
                g_.wait_ge(sm["dOUT"], 16 * 16 + 16 * 6)
            else:
                g_.wait_ge(sm["dOUT"], 16 * 16)

        @block.vector
        def _(v):
            for t in range(NT):
                p = t % 2
                sl = st_slot(ST_P0, p)
                xs = SMALL[:, p * 2048:(p + 1) * 2048]
                v.wait_ge(sm["dX"], 16 * (t + 1))
                if t >= 2:
                    v.wait_ge(sm["aP0"], 2 * t - 2)
                for c in range(4):
                    i = v.bn_stats(sl["bn"][:, c * 6:(c + 1) * 6], xs[:, c * 512:(c + 1) * 512])
                i.then_inc(sm["vP0"], 1)
                v.wait_ge(sm["vP0"], 5 * t + 1)
                v.bn_aggr(sl["mv"], sl["bn"]).then_inc(sm["vP0"], 1)
                v.wait_ge(sm["vP0"], 5 * t + 2)
                v.tensor_scalar_add(sl["t1"], sl["mv"][:, 1:2], EPS).then_inc(sm["vP0"], 1)
                v.wait_ge(sm["aP0"], 2 * t + 1)
                v.reciprocal(sl["rstd"], sl["t2"]).then_inc(sm["vP0"], 1)
                v.wait_ge(sm["vP0"], 5 * t + 4)
                v.scalar_tensor_tensor(sl["t3"], sl["mv"][:, 0:1], -1.0, sl["rstd"],
                                       ALU.mult, ALU.mult).then_inc(sm["vP0"], 1)
                for j in range(16):
                    idx = 16 * t + j
                    v.wait_ge(sm["tTR"], idx + 1)
                    if t < OWNT:
                        dest = BIG0[:, j * 2048 + t * 128: j * 2048 + t * 128 + 128]
                    else:
                        if j == 0 and t >= 18:
                            v.wait_ge(sm["dLTst"], 16 * (8 + (t - 18) // 2 + 1))
                        dest = LTST[:, j * 256 + (t % 2) * 128: j * 256 + (t % 2) * 128 + 128]
                    v.tensor_copy(dest, bb[j % 2][:, 0:128]).then_inc(sm["vLT"], 1)
            for g in range(NG):
                for kt in range(NKT):
                    for qi in range(2):
                        idx = g * 64 + kt * 2 + qi
                        v.wait_ge(sm["aS"], idx + 1)
                        v.reduce_max(STATS[:, ST_MX + qi * 32 + kt: ST_MX + qi * 32 + kt + 1],
                                     BIG0[:, qi * S + kt * 512: qi * S + (kt + 1) * 512],
                                     axis=AX.X).then_inc(sm["vKM"], 1)
                for qi in range(2):
                    v.wait_ge(sm["vKM"], g * 64 + 63 + qi)
                    v.reduce_max(STATS[:, ST_NEGM + qi:ST_NEGM + qi + 1],
                                 STATS[:, ST_MX + qi * 32: ST_MX + qi * 32 + 32],
                                 axis=AX.X, negate=True).then_inc(sm["vFM"], 1)
                    v.wait_ge(sm["aEX"], g * 2 + qi + 1)
                    v.reciprocal(STATS[:, ST_RS + qi:ST_RS + qi + 1],
                                 STATS[:, ST_SE + qi:ST_SE + qi + 1]).then_inc(sm["vPS"], 1)
                    v.wait_ge(sm["vPS"], 2 * (g * 2 + qi) + 1)
                    v.tensor_single_scalar(BIG1[:], BIG1[:], STATS[:, ST_RS + qi:ST_RS + qi + 1],
                                           ALU.mult).then_inc(sm["vPS"], 1)
                    for j in range(128):
                        tidx = g * 256 + qi * 128 + j
                        v.wait_ge(sm["tTP"], tidx + 1)
                        v.tensor_copy(BIG0[:, qi * S + j * 128: qi * S + (j + 1) * 128],
                                      bb[j % 2][:, 0:128]).then_inc(sm["vPT"], 1)
                for hc in range(16):
                    v.wait_ge(sm["tCk"], g * 32 + 32)
                    if hc == 0:
                        if g == 0:
                            v.wait_ge(sm["dQP"], 16 * 128)
                        else:
                            v.wait_ge(sm["dCT"], 256 * g)
                    v.tensor_copy(FT[:, hc * 256:(hc + 1) * 256],
                                  pb[hc // 2][:, (hc % 2) * 256:(hc % 2) * 256 + 256]
                                  ).then_inc(sm["vCT"], 1)
            for ih in range(2):
                for tt in range(8):
                    T = ih * 8 + tt
                    p = T % 2
                    for k in range(16):
                        utix = T * 32 + k
                        if utix == 0:
                            v.wait_ge(sm["dLst"], 16 * NT)
                        v.wait_ge(sm["tTR3"], utix + 1)
                        v.tensor_copy(SMALLB[:, p * 2048 + k * 128: p * 2048 + (k + 1) * 128],
                                      bb[utix % 2][:, 0:128]).then_inc(sm["vTR3"], 1)
                    slA = st_slot(ST_P3A, p)
                    v.wait_ge(sm["vTR3"], T * 32 + 16)
                    ats = SMALLB[:, p * 2048:(p + 1) * 2048]
                    for c in range(4):
                        i = v.bn_stats(slA["bn"][:, c * 6:(c + 1) * 6], ats[:, c * 512:(c + 1) * 512])
                    i.then_inc(sm["vP3"], 1)
                    v.wait_ge(sm["vP3"], 11 * T + 1)
                    v.bn_aggr(slA["mv"], slA["bn"]).then_inc(sm["vP3"], 1)
                    v.wait_ge(sm["vP3"], 11 * T + 2)
                    v.tensor_scalar_add(slA["t1"], slA["mv"][:, 1:2], EPS).then_inc(sm["vP3"], 1)
                    v.wait_ge(sm["aP3"], 4 * T + 1)
                    v.reciprocal(slA["rstd"], slA["t2"]).then_inc(sm["vP3"], 1)
                    v.wait_ge(sm["vP3"], 11 * T + 4)
                    v.scalar_tensor_tensor(slA["t3"], slA["mv"][:, 0:1], -1.0, slA["rstd"],
                                           ALU.mult, ALU.mult).then_inc(sm["vP3"], 1)
                    xs3 = SMALL[:, p * 2048:(p + 1) * 2048]
                    v.wait_ge(sm["aP3"], 4 * T + 2)
                    v.wait_ge(sm["dXo"], 16 * (T + 1))
                    v.tensor_tensor(xs3, xs3, SMALL[:, 4096:6144], ALU.add).then_inc(sm["vP3"], 1)
                    slB = st_slot(ST_P3B, p)
                    v.wait_ge(sm["vP3"], 11 * T + 6)
                    for c in range(4):
                        i = v.bn_stats(slB["bn"][:, c * 6:(c + 1) * 6], xs3[:, c * 512:(c + 1) * 512])
                    i.then_inc(sm["vP3"], 1)
                    v.wait_ge(sm["vP3"], 11 * T + 7)
                    v.bn_aggr(slB["mv"], slB["bn"]).then_inc(sm["vP3"], 1)
                    v.wait_ge(sm["vP3"], 11 * T + 8)
                    v.tensor_scalar_add(slB["t1"], slB["mv"][:, 1:2], EPS).then_inc(sm["vP3"], 1)
                    v.wait_ge(sm["aP3"], 4 * T + 3)
                    v.reciprocal(slB["rstd"], slB["t2"]).then_inc(sm["vP3"], 1)
                    v.wait_ge(sm["vP3"], 11 * T + 10)
                    v.scalar_tensor_tensor(slB["t3"], slB["mv"][:, 0:1], -1.0, slB["rstd"],
                                           ALU.mult, ALU.mult).then_inc(sm["vP3"], 1)
                    for k in range(16):
                        utix = T * 32 + 16 + k
                        v.wait_ge(sm["tTR3"], utix + 1)
                        if ih == 0:
                            dest = STRM[:, k * 1024 + T * 128: k * 1024 + T * 128 + 128]
                        else:
                            dest = BIG1[:, k * 1024 + (T - 8) * 128: k * 1024 + (T - 8) * 128 + 128]
                        v.tensor_copy(dest, bb[utix % 2][:, 0:128]).then_inc(sm["vTR3"], 1)
            for fc in range(FCN):
                for it in range(4):
                    idx = fc * 4 + it
                    v.wait_ge(sm["aG"], idx + 1)
                    v.wait_ge(sm["aU"], idx + 1)
                    if idx == 0:
                        v.wait_ge(sm["dLTst"], 16 * 64)
                    if idx >= 4:
                        v.wait_ge(sm["dH"], 16 * (idx - 3))
                    v.tensor_mul(LTST[:, (idx % 4) * 512:(idx % 4 + 1) * 512],
                                 SMALLB[:, (idx % 2) * 512:(idx % 2 + 1) * 512],
                                 SMALLB[:, 1024 + (idx % 2) * 512: 1024 + (idx % 2 + 1) * 512]
                                 ).then_inc(sm["vH"], 1)
            for it2 in range(4):
                for tt in range(4):
                    To = it2 * 4 + tt
                    p = To % 2
                    for oc in range(16):
                        ftix = To * 16 + oc
                        v.wait_ge(sm["tTF"], ftix + 1)
                        if oc == 0:
                            if To == 0:
                                v.wait_ge(sm["aG"], FCN * 4)
                                v.wait_ge(sm["aU"], FCN * 4)
                            if To >= 2:
                                v.wait_ge(sm["aP4"], 2 * To - 2)
                        v.tensor_copy(SMALLB[:, p * 2048 + oc * 128: p * 2048 + (oc + 1) * 128],
                                      bb[ftix % 2][:, 0:128]).then_inc(sm["vTF"], 1)
                    sl4 = st_slot(ST_P4, p)
                    v.wait_ge(sm["vTF"], To * 16 + 16)
                    fts = SMALLB[:, p * 2048:(p + 1) * 2048]
                    for c in range(4):
                        i = v.bn_stats(sl4["bn"][:, c * 6:(c + 1) * 6], fts[:, c * 512:(c + 1) * 512])
                    i.then_inc(sm["vP4"], 1)
                    v.wait_ge(sm["vP4"], 6 * To + 1)
                    v.bn_aggr(sl4["mv"], sl4["bn"]).then_inc(sm["vP4"], 1)
                    v.wait_ge(sm["vP4"], 6 * To + 2)
                    v.tensor_scalar_add(sl4["t1"], sl4["mv"][:, 1:2], EPS).then_inc(sm["vP4"], 1)
                    v.wait_ge(sm["aP4"], 2 * To + 1)
                    v.reciprocal(sl4["rstd"], sl4["t2"]).then_inc(sm["vP4"], 1)
                    v.wait_ge(sm["vP4"], 6 * To + 4)
                    v.scalar_tensor_tensor(sl4["t3"], sl4["mv"][:, 0:1], -1.0, sl4["rstd"],
                                           ALU.mult, ALU.mult).then_inc(sm["vP4"], 1)
                    xs4 = SMALL[:, p * 2048:(p + 1) * 2048]
                    v.wait_ge(sm["aP4"], 2 * To + 2)
                    v.wait_ge(sm["dYl"], 16 * (To + 1))
                    v.tensor_tensor(xs4, xs4, SMALL[:, 4096:6144], ALU.add).then_inc(sm["vP4"], 1)

        @block.scalar
        def _(s):
            for t in range(NT):
                p = t % 2
                sl = st_slot(ST_P0, p)
                s.wait_ge(sm["vP0"], 5 * t + 3)
                s.activation(sl["t2"], sl["t1"], AF.Sqrt).then_inc(sm["aP0"], 1)
                s.wait_ge(sm["vP0"], 5 * t + 5)
                if t >= 2:
                    s.wait_ge(sm["tTR"], 16 * (t - 1))
                    s.wait_ge(sm["dLst"], 16 * (t - 1))
                s.activation(SMALLB[:, p * 2048:(p + 1) * 2048], SMALL[:, p * 2048:(p + 1) * 2048],
                             AF.Identity, bias=sl["t3"], scale=sl["rstd"]).then_inc(sm["aP0"], 1)
            s.wait_ge(sm["dCONST"], 16 * 7)
            for ih in range(2):
                for oc in range(16):
                    for it in range(2):
                        g1 = ih * 32 + oc * 2 + it
                        s.wait_ge(sm["tM1"], g1 + 1)
                        if ih == 1 and g1 == 32:
                            s.wait_ge(sm["tM2"], 32)
                        s.activation(BIG1[:, oc * 1024 + it * 512: oc * 1024 + (it + 1) * 512],
                                     pb[4 + (oc * 2 + it) % 2][:],
                                     AF.Identity, bias=CONST[:, C_BQ + oc:C_BQ + oc + 1]).then_inc(sm["aE1"], 1)
                for hcp in range(16):
                    for it in range(2):
                        g2 = ih * 32 + hcp * 2 + it
                        r = g2 % 4
                        s.wait_ge(sm["tM2"], g2 + 1)
                        if g2 >= 4:
                            s.wait_ge(sm["dQP"], 16 * (2 * g2 - 6))
                        s.activation(FT[:, r * 512:(r + 1) * 512], pb[6 + (hcp * 2 + it) % 2][:],
                                     AF.Identity, scale=RSCALE).then_inc(sm["aE2"], 1)
            for g in range(NG):
                for kt in range(NKT):
                    for qi in range(2):
                        idx = g * 64 + kt * 2 + qi
                        s.wait_ge(sm["tS"], idx + 1)
                        if g > 0 and kt == 0:
                            s.wait_ge(sm["tCk"], g * 32)
                        s.activation(BIG0[:, qi * S + kt * 512: qi * S + (kt + 1) * 512],
                                     pb[(kt * 2 + qi) % 2][:], AF.Copy).then_inc(sm["aS"], 1)
                for qi in range(2):
                    s.wait_ge(sm["vFM"], g * 2 + qi + 1)
                    s.wait_ge(sm["vPT"], g * 256 + qi * 128)
                    s.activation(BIG1[:], BIG0[:, qi * S:(qi + 1) * S], AF.Exp,
                                 bias=STATS[:, ST_NEGM + qi:ST_NEGM + qi + 1], scale=1.0,
                                 accum_out=STATS[:, ST_SE + qi:ST_SE + qi + 1]).then_inc(sm["aEX"], 1)
            for ih in range(2):
                for oc in range(16):
                    for it in range(2):
                        g3 = ih * 32 + oc * 2 + it
                        s.wait_ge(sm["tM3"], g3 + 1)
                        s.activation(BIG0[:, oc * 1024 + it * 512: oc * 1024 + (it + 1) * 512],
                                     pb[4 + (oc * 2 + it) % 2][:], AF.Identity,
                                     bias=CONST[:, C_BV + oc:C_BV + oc + 1]).then_inc(sm["aE3"], 1)
                for ocp in range(16):
                    for it in range(2):
                        g4 = ih * 32 + ocp * 2 + it
                        s.wait_ge(sm["tM4"], g4 + 1)
                        s.activation(BIG0[:, S + ocp * 1024 + it * 512: S + ocp * 1024 + (it + 1) * 512],
                                     pb[6 + (ocp * 2 + it) % 2][:], AF.Identity,
                                     bias=CONST[:, C_BO + ocp:C_BO + ocp + 1]).then_inc(sm["aE4"], 1)
                for tt in range(8):
                    T = ih * 8 + tt
                    p = T % 2
                    slA = st_slot(ST_P3A, p)
                    slB = st_slot(ST_P3B, p)
                    s.wait_ge(sm["vP3"], 11 * T + 3)
                    s.activation(slA["t2"], slA["t1"], AF.Sqrt).then_inc(sm["aP3"], 1)
                    s.wait_ge(sm["vP3"], 11 * T + 5)
                    s.activation(SMALL[:, 4096:6144], SMALLB[:, p * 2048:(p + 1) * 2048],
                                 AF.Identity, bias=slA["t3"], scale=slA["rstd"]).then_inc(sm["aP3"], 1)
                    s.wait_ge(sm["vP3"], 11 * T + 9)
                    s.activation(slB["t2"], slB["t1"], AF.Sqrt).then_inc(sm["aP3"], 1)
                    s.wait_ge(sm["vP3"], 11 * T + 11)
                    s.activation(SMALLB[:, p * 2048:(p + 1) * 2048], SMALL[:, p * 2048:(p + 1) * 2048],
                                 AF.Identity, bias=slB["t3"], scale=slB["rstd"]).then_inc(sm["aP3"], 1)
            for fc in range(FCN):
                for it in range(4):
                    idx = fc * 4 + it
                    s.wait_ge(sm["tG"], idx + 1)
                    if idx == 0:
                        s.wait_ge(sm["vTR3"], 512)
                    if idx >= 2:
                        s.wait_ge(sm["vH"], idx - 1)
                    s.activation(SMALLB[:, (idx % 2) * 512:(idx % 2 + 1) * 512],
                                 pb[4 + idx % 2][:], AF.Silu,
                                 scale=CONST[:, C_GS + fc:C_GS + fc + 1]).then_inc(sm["aG"], 1)
                    s.wait_ge(sm["tU"], idx + 1)
                    s.activation(SMALLB[:, 1024 + (idx % 2) * 512: 1024 + (idx % 2 + 1) * 512],
                                 pb[6 + idx % 2][:], AF.Identity,
                                 scale=CONST[:, C_US + fc:C_US + fc + 1]).then_inc(sm["aU"], 1)
            for it2 in range(4):
                for oc in range(16):
                    jdx = it2 * 16 + oc
                    s.wait_ge(sm["tD"], jdx + 1)
                    if jdx == 0:
                        s.wait_ge(sm["vTR3"], 512)
                    if oc == 0 and it2 >= 1:
                        s.wait_ge(sm["tTF"], it2 * 64)
                    s.activation(BIG0[:, 24576 + oc * 512: 24576 + (oc + 1) * 512],
                                 pb[4 + jdx % 2][:], AF.Identity,
                                 scale=CONST[:, C_DS + oc:C_DS + oc + 1]).then_inc(sm["aD"], 1)
                for tt in range(4):
                    To = it2 * 4 + tt
                    p = To % 2
                    sl4 = st_slot(ST_P4, p)
                    s.wait_ge(sm["vP4"], 6 * To + 3)
                    s.activation(sl4["t2"], sl4["t1"], AF.Sqrt).then_inc(sm["aP4"], 1)
                    s.wait_ge(sm["vP4"], 6 * To + 5)
                    s.activation(SMALL[:, 4096:6144], SMALLB[:, p * 2048:(p + 1) * 2048],
                                 AF.Identity, bias=sl4["t3"], scale=sl4["rstd"]).then_inc(sm["aP4"], 1)

        @block.tensor
        def _(t_):
            t_.wait_ge(sm["dCONST"], 16)
            for t in range(NT):
                p = t % 2
                t_.wait_ge(sm["aP0"], 2 * t + 2)
                for j in range(16):
                    idx = 16 * t + j
                    if idx >= 2:
                        t_.wait_ge(sm["vLT"], idx - 1)
                    t_.transpose(bb[j % 2][:, 0:128],
                                 SMALLB[:, p * 2048 + j * 128: p * 2048 + (j + 1) * 128],
                                 IDT[:]).then_inc(sm["tTR"], 1)
            t_.wait_ge(sm["vLT"], 256)
            for ih in range(2):
                for oc in range(16):
                    for it in range(2):
                        g1 = ih * 32 + oc * 2 + it
                        t_.wait_ge(sm["dW1"], 16 * (ih * 32 + oc + 1))
                        if g1 >= 2:
                            t_.wait_ge(sm["aE1"], g1 - 1)
                        w = oc % 2
                        for hc in range(16):
                            i = t_.matmul(pb[4 + (oc * 2 + it) % 2][:],
                                          WCOL[:, w * 2048 + hc * 128: w * 2048 + (hc + 1) * 128],
                                          BIG0[:, hc * 2048 + ih * 1024 + it * 512:
                                               hc * 2048 + ih * 1024 + (it + 1) * 512],
                                          start=(hc == 0), stop=(hc == 15))
                        i.then_inc(sm["tM1"], 1)
                for hcp in range(16):
                    for it in range(2):
                        g2 = ih * 32 + hcp * 2 + it
                        t_.wait_ge(sm["dW1"], 16 * (ih * 32 + 16 + hcp + 1))
                        if g2 == ih * 32:
                            t_.wait_ge(sm["aE1"], (ih + 1) * 32)
                        if g2 >= 2:
                            t_.wait_ge(sm["aE2"], g2 - 1)
                        w = hcp % 2
                        for oc in range(16):
                            i = t_.matmul(pb[6 + (hcp * 2 + it) % 2][:],
                                          WCOL[:, w * 2048 + oc * 128: w * 2048 + (oc + 1) * 128],
                                          BIG1[:, oc * 1024 + it * 512: oc * 1024 + (it + 1) * 512],
                                          start=(oc == 0), stop=(oc == 15))
                        i.then_inc(sm["tM2"], 1)
            for g in range(NG):
                t_.wait_ge(sm["dQG"], 16 * (g + 1))
                for kt in range(NKT):
                    slot = kt % 2
                    t_.wait_ge(sm["dLTA"], 64 * (g * 32 + kt + 1))
                    for qi in range(2):
                        idx = g * 64 + kt * 2 + qi
                        if g == 0 and idx < 2:
                            t_.wait_ge(sm["vLT"], 2048)
                        if g > 0 and kt == 0:
                            t_.wait_ge(sm["vCT"], (g - 1) * 16 + 4)
                        if idx - g * 64 >= 2:
                            t_.wait_ge(sm["aS"], idx - 1)
                        for hc in range(16):
                            i = t_.matmul(pb[(kt * 2 + qi) % 2][:],
                                          QPT[:, hc * 256 + qi * 128: hc * 256 + qi * 128 + 128],
                                          STRM[:, slot * 8192 + hc * 512: slot * 8192 + (hc + 1) * 512],
                                          start=(hc == 0), stop=(hc == 15))
                        i.then_inc(sm["tS"], 1)
                for qi in range(2):
                    t_.wait_ge(sm["vPS"], 2 * (g * 2 + qi) + 2)
                    for j in range(128):
                        tidx = g * 256 + qi * 128 + j
                        if qi == 0 and j < 2:
                            t_.wait_ge(sm["aS"], g * 64 + 64)
                        if tidx >= 2:
                            t_.wait_ge(sm["vPT"], tidx - 1)
                        t_.transpose(bb[j % 2][:, 0:128],
                                     BIG1[:, j * 128:(j + 1) * 128], IDT[:]).then_inc(sm["tTP"], 1)
                for kt in range(NKT):
                    slot = kt % 2
                    t_.wait_ge(sm["dLC"], 16 * (g * 32 + kt + 1))
                    if kt == 0:
                        t_.wait_ge(sm["vPT"], g * 256 + 256)
                        if g > 0:
                            t_.wait_ge(sm["vCT"], g * 16)
                    for jb in range(4):
                        rhs = BIG0q[:, :, (kt * 4 + jb) * 128:(kt * 4 + jb + 1) * 128]
                        for hc in range(16):
                            i = t_.matmul(
                                pb[hc // 2][:, (hc % 2) * 256:(hc % 2) * 256 + 256]
                                .rearrange("p (a b) -> p a b", a=2),
                                STRM[:, slot * 8192 + jb * 2048 + hc * 128:
                                     slot * 8192 + jb * 2048 + (hc + 1) * 128],
                                rhs,
                                start=(kt == 0 and jb == 0 and hc % 2 == 0),
                                stop=(kt == NKT - 1 and jb == 3 and hc % 2 == 1),
                                skip_group_check=True)
                    i.then_inc(sm["tCk"], 1)
            for ih in range(2):
                t_.wait_ge(sm["dCTl"], 16 * (ih + 1))
                for oc in range(16):
                    for it in range(2):
                        g3 = ih * 32 + oc * 2 + it
                        t_.wait_ge(sm["dW3"], 16 * (ih * 32 + oc + 1))
                        if g3 == 0:
                            t_.wait_ge(sm["vCT"], 128)
                        if g3 >= 2:
                            t_.wait_ge(sm["aE3"], g3 - 1)
                        w = oc % 2
                        for hc in range(16):
                            i = t_.matmul(pb[4 + (oc * 2 + it) % 2][:],
                                          WCOL[:, w * 2048 + hc * 128: w * 2048 + (hc + 1) * 128],
                                          BIG1[:, hc * 1024 + it * 512: hc * 1024 + (it + 1) * 512],
                                          start=(hc == 0), stop=(hc == 15))
                        i.then_inc(sm["tM3"], 1)
                for ocp in range(16):
                    for it in range(2):
                        g4 = ih * 32 + ocp * 2 + it
                        t_.wait_ge(sm["dW3"], 16 * (ih * 32 + 16 + ocp + 1))
                        if g4 == ih * 32:
                            t_.wait_ge(sm["aE3"], ih * 32 + 32)
                        if g4 >= 2:
                            t_.wait_ge(sm["aE4"], g4 - 1)
                        w = ocp % 2
                        for oc in range(16):
                            i = t_.matmul(pb[6 + (ocp * 2 + it) % 2][:],
                                          WCOL[:, w * 2048 + oc * 128: w * 2048 + (oc + 1) * 128],
                                          BIG0[:, oc * 1024 + it * 512: oc * 1024 + (it + 1) * 512],
                                          start=(oc == 0), stop=(oc == 15))
                        i.then_inc(sm["tM4"], 1)
                for tt in range(8):
                    T = ih * 8 + tt
                    p = T % 2
                    for k in range(16):
                        utix = T * 32 + k
                        if utix < 2:
                            t_.wait_ge(sm["vCT"], 128)
                        if k == 0:
                            t_.wait_ge(sm["aE4"], ih * 32 + 32)
                        if utix >= 2:
                            t_.wait_ge(sm["vTR3"], utix - 1)
                        t_.transpose(bb[utix % 2][:, 0:128],
                                     BIG0[:, S + k * 1024 + (T - ih * 8) * 128 + ih * 0:
                                          S + k * 1024 + (T - ih * 8) * 128 + 128],
                                     IDT[:]).then_inc(sm["tTR3"], 1)
                    for k in range(16):
                        utix = T * 32 + 16 + k
                        if k == 0:
                            t_.wait_ge(sm["aP3"], 4 * T + 4)
                        if utix >= 2:
                            t_.wait_ge(sm["vTR3"], utix - 1)
                        t_.transpose(bb[utix % 2][:, 0:128],
                                     SMALLB[:, p * 2048 + k * 128: p * 2048 + (k + 1) * 128],
                                     IDT[:]).then_inc(sm["tTR3"], 1)
            for fc in range(FCN):
                for it in range(4):
                    idx = fc * 4 + it
                    t_.wait_ge(sm["dW4"], 32 * fc + 16)
                    if idx == 0:
                        t_.wait_ge(sm["aE3"], 64)
                        t_.wait_ge(sm["vTR3"], 512)
                    if idx >= 2:
                        t_.wait_ge(sm["aG"], idx - 1)
                    wslot = fc % 2
                    rhsbuf = STRM if it < 2 else BIG1
                    itl = it if it < 2 else it - 2
                    for hc in range(16):
                        i = t_.matmul(pb[4 + idx % 2][:],
                                      WCOL[:, wslot * 2048 + hc * 128: wslot * 2048 + (hc + 1) * 128],
                                      rhsbuf[:, hc * 1024 + itl * 512: hc * 1024 + (itl + 1) * 512],
                                      start=(hc == 0), stop=(hc == 15))
                    i.then_inc(sm["tG"], 1)
                    t_.wait_ge(sm["dW4"], 32 * fc + 32)
                    if idx == 0:
                        t_.wait_ge(sm["aE4"], 64)
                    if idx >= 2:
                        t_.wait_ge(sm["aU"], idx - 1)
                    for hc in range(16):
                        i = t_.matmul(pb[6 + idx % 2][:],
                                      WCOL[:, 4096 + wslot * 2048 + hc * 128:
                                           4096 + wslot * 2048 + (hc + 1) * 128],
                                      rhsbuf[:, hc * 1024 + itl * 512: hc * 1024 + (itl + 1) * 512],
                                      start=(hc == 0), stop=(hc == 15))
                    i.then_inc(sm["tU"], 1)
            for it2 in range(4):
                t_.wait_ge(sm["dHl"], 16 * (it2 + 1))
                for oc in range(16):
                    jdx = it2 * 16 + oc
                    t_.wait_ge(sm["dWd"], 16 * (jdx + 1))
                    if jdx == 0:
                        t_.wait_ge(sm["aG"], FCN * 4)
                        t_.wait_ge(sm["aU"], FCN * 4)
                    if jdx >= 2:
                        t_.wait_ge(sm["aD"], jdx - 1)
                    slot = oc % 2
                    for fc in range(FCN):
                        i = t_.matmul(pb[4 + jdx % 2][:],
                                      STRM[:, slot * 8192 + fc * 128: slot * 8192 + (fc + 1) * 128],
                                      BIG0[:, fc * 512:(fc + 1) * 512],
                                      start=(fc == 0), stop=(fc == FCN - 1))
                    i.then_inc(sm["tD"], 1)
                t_.wait_ge(sm["aD"], it2 * 16 + 16)
                for tt in range(4):
                    To = it2 * 4 + tt
                    for oc in range(16):
                        ftix = To * 16 + oc
                        if ftix < 2:
                            t_.wait_ge(sm["vTR3"], 512)
                        if ftix >= 2:
                            t_.wait_ge(sm["vTF"], ftix - 1)
                        t_.transpose(bb[ftix % 2][:, 0:128],
                                     BIG0[:, 24576 + oc * 512 + tt * 128: 24576 + oc * 512 + tt * 128 + 128],
                                     IDT[:]).then_inc(sm["tTF"], 1)

    return nc


def _pack(Wp):
    K, M = Wp.shape
    R, C = K // 128, M // 128
    return np.ascontiguousarray(
        Wp.reshape(R, 128, C, 128).transpose(2, 1, 0, 3).reshape(C, 128, R * 128))


def _prep_shared(i):
    wqkv = np.asarray(i["L0_attn_w_qkv"], np.float32)
    m = {
        "wq_p": _pack(np.ascontiguousarray(wqkv[0:2048].T)),
        "wk_p": _pack(np.ascontiguousarray(wqkv[2048:4096])),
        "wv_p": _pack(np.ascontiguousarray(wqkv[4096:6144].T)),
        "wo_p": _pack(np.ascontiguousarray(np.asarray(i["L0_attn_w_out"], np.float32).T)),
        "wg_p": _pack(np.ascontiguousarray(np.asarray(i["L0_ffn_gate_w"], np.float32).T)),
        "wu_p": _pack(np.ascontiguousarray(np.asarray(i["L0_ffn_up_w"], np.float32).T)),
        "wd_p": _pack(np.ascontiguousarray(np.asarray(i["L0_ffn_down_w"], np.float32).T)),
        "bq": np.ascontiguousarray(np.asarray(i["L0_attn_b_qkv"], np.float32)[0:2048].reshape(16, 128).T),
        "bv": np.ascontiguousarray(np.asarray(i["L0_attn_b_qkv"], np.float32)[4096:6144].reshape(16, 128).T),
        "bo": np.ascontiguousarray(np.asarray(i["L0_attn_b_out"], np.float32).reshape(16, 128).T),
        "ds": np.ascontiguousarray(np.asarray(i["L0_ffn_down_s"], np.float32).reshape(16, 128).T),
        "gs": np.ascontiguousarray(np.asarray(i["L0_ffn_gate_s"], np.float32).reshape(44, 128).T),
        "us": np.ascontiguousarray(np.asarray(i["L0_ffn_up_s"], np.float32).reshape(44, 128).T),
        "ident": np.eye(128, dtype=ml_dtypes.bfloat16),
    }
    return m


def kernel(**inputs):
    x = np.asarray(inputs["L0_attn_hidden_states"], np.float32).reshape(S, D)
    shared = _prep_shared(inputs)
    nc = build(dbg=False)
    in_maps = []
    for c in range(8):
        m = dict(shared)
        m["x"] = np.ascontiguousarray(np.roll(x, -2048 * c, axis=0))
        in_maps.append(m)
    res = run_bass_kernel_spmd(nc, in_maps, core_ids=list(range(8)))
    out = np.concatenate([np.asarray(res.results[c]["out"]) for c in range(8)], axis=0)
    return out.reshape(1, S, D).astype(np.float32)


# version 02
# speedup vs baseline: 1.1565x; 1.1565x over previous
import contextlib
import numpy as np
import ml_dtypes
import concourse.bass as bass
import concourse.mybir as mybir
from concourse.bass_utils import run_bass_kernel_spmd

F32 = mybir.dt.float32
BF16 = mybir.dt.bfloat16
AF = mybir.ActivationFunctionType
ALU = mybir.AluOpType
AX = mybir.AxisListType

S = 16384
D = 2048
HCN = 16
NT = 128
NSL = 32
OWNT = 16
OWN = 2048
KT = 512
NKT = 32
NG = 8
INTER = 5632
FCN = 44
EPS = 1e-5
RSCALE = float(np.float32(1.0) / np.float32(np.sqrt(np.float32(2048.0))))

C_BQ = 0
C_BV = 16
C_BO = 32
C_DS = 48
C_GS = 64
C_US = 108

ST_MX = 256
ST_NEGM = 320
ST_SE4 = 324
ST_SES = 332
ST_RS = 334
ST_P3A = 336
ST_P3B = 396
ST_P4 = 456


def build(dbg=False):
    nc = bass.Bass("TRN2", target_bir_lowering=False)

    x_d = nc.dram_tensor("x", [S, D], F32, kind="ExternalInput").ap()
    wq_p = nc.dram_tensor("wq_p", [16, 128, D], F32, kind="ExternalInput").ap()
    wk_p = nc.dram_tensor("wk_p", [16, 128, D], F32, kind="ExternalInput").ap()
    wv_p = nc.dram_tensor("wv_p", [16, 128, D], F32, kind="ExternalInput").ap()
    wo_p = nc.dram_tensor("wo_p", [16, 128, D], F32, kind="ExternalInput").ap()
    wg_p = nc.dram_tensor("wg_p", [44, 128, D], F32, kind="ExternalInput").ap()
    wu_p = nc.dram_tensor("wu_p", [44, 128, D], F32, kind="ExternalInput").ap()
    wd_p = nc.dram_tensor("wd_p", [16, 128, INTER], F32, kind="ExternalInput").ap()
    bq_d = nc.dram_tensor("bq", [128, 16], F32, kind="ExternalInput").ap()
    bv_d = nc.dram_tensor("bv", [128, 16], F32, kind="ExternalInput").ap()
    bo_d = nc.dram_tensor("bo", [128, 16], F32, kind="ExternalInput").ap()
    ds_d = nc.dram_tensor("ds", [128, 16], F32, kind="ExternalInput").ap()
    gs_d = nc.dram_tensor("gs", [128, 44], F32, kind="ExternalInput").ap()
    us_d = nc.dram_tensor("us", [128, 44], F32, kind="ExternalInput").ap()
    id_d = nc.dram_tensor("ident", [128, 128], BF16, kind="ExternalInput").ap()
    out_d = nc.dram_tensor("out", [OWN, D], F32, kind="ExternalOutput").ap()

    Ld = nc.dram_tensor("Ld", [S, D], BF16, kind="Internal").ap()
    LTd = nc.dram_tensor("LTd", [16, 128, S], BF16, kind="Internal").ap()
    QpTd = nc.dram_tensor("QpTd", [NG, 128, HCN * 256], BF16, kind="Internal").ap()
    CTd = nc.dram_tensor("CTd", [2, 128, HCN * 1024], BF16, kind="Internal").ap()
    HDNd = nc.dram_tensor("HDNd", [4, 128, FCN * 512], BF16, kind="Internal").ap()
    WDB = nc.dram_tensor("WDB", [16, 128, INTER], BF16, kind="Internal").ap()
    yd = nc.dram_tensor("yd", [OWN, D], F32, kind="Internal").ap()

    if dbg:
        dbgL = nc.dram_tensor("dbgL", [256, D], F32, kind="ExternalOutput").ap()
        dbgLT = nc.dram_tensor("dbgLT", [128, 512], F32, kind="ExternalOutput").ap()
        dbgQpT = nc.dram_tensor("dbgQpT", [128, HCN * 256], F32, kind="ExternalOutput").ap()
        dbgCT = nc.dram_tensor("dbgCT", [128, 2048], F32, kind="ExternalOutput").ap()
        dbgY = nc.dram_tensor("dbgY", [256, D], F32, kind="ExternalOutput").ap()
        dbgHDN = nc.dram_tensor("dbgHDN", [128, 2048], F32, kind="ExternalOutput").ap()

    Ld_v = Ld.rearrange("(k p) d -> k p d", p=128)
    x_v = x_d.rearrange("(k p) d -> k p d", p=128)
    yd_v = yd.rearrange("(k p) d -> k p d", p=128)
    out_v = out_d.rearrange("(k p) d -> k p d", p=128)

    es = contextlib.ExitStack()
    with es:
        E = es.enter_context
        BIG0 = E(nc.sbuf_tensor("BIG0", [128, 32768], BF16))
        BIG1 = E(nc.sbuf_tensor("BIG1", [128, 16384], BF16))
        STRM = E(nc.sbuf_tensor("STRM", [128, 16384], BF16))
        WCOL = E(nc.sbuf_tensor("WCOL", [128, 8192], BF16))
        QPT = E(nc.sbuf_tensor("QPT", [128, 4096], BF16))
        FT = E(nc.sbuf_tensor("FT", [128, 4096], BF16))
        SMALL = E(nc.sbuf_tensor("SMALL", [128, 6144], F32))
        SMALLB = E(nc.sbuf_tensor("SMALLB", [128, 4096], BF16))
        LTST = E(nc.sbuf_tensor("LTST", [128, 4096], BF16))
        STATS = E(nc.sbuf_tensor("STATS", [128, 640], F32))
        CONST = E(nc.sbuf_tensor("CONST", [128, 256], F32))
        IDT = E(nc.sbuf_tensor("IDT", [128, 128], BF16))
        pb = [E(nc.psum_tensor("pb%d" % i, [128, 512], F32)) for i in range(8)]

        sem_names = [
            "dCONST", "dXB", "vP0", "aP0", "tTR", "vLT", "dLst", "dLTst",
            "dW1", "tM1", "aE1", "tM2", "aE2", "dQP", "dWB",
            "dQG", "dLTA", "tS", "aS", "vKM", "vFM", "aEX", "vPS",
            "tTP", "vPT", "dLC", "tCk", "vCT", "dCT",
            "dCTl", "dW3", "tM3", "aE3", "tM4", "aE4", "tTR3", "vTR3",
            "vP3", "aP3", "dXo", "dY",
            "dW4", "tG", "tU", "aG", "aU", "vH", "dH",
            "dWd", "dHl", "tD", "aD", "tTF", "vTF", "vP4", "aP4", "dYl", "dOUT",
        ]
        sm = {n: E(nc.semaphore(n)) for n in sem_names}
        block = E(nc.Block())

        bb = [pb[i][:].bitcast(BF16) for i in range(8)]
        BIG0q = BIG0[:].rearrange("p (q j) -> p q j", q=2)
        BIG0_16 = BIG0[:].rearrange("p (c i) -> p c i", c=16)
        LTST_16 = LTST[:].rearrange("p (c j) -> p c j", c=16)

        def st3(base, p):
            o = base + p * 30
            return dict(bn=STATS[:, o:o + 24], mv=STATS[:, o + 24:o + 26],
                        t1=STATS[:, o + 26:o + 27], t2=STATS[:, o + 27:o + 28],
                        rstd=STATS[:, o + 28:o + 29], t3=STATS[:, o + 29:o + 30])

        def st0(sp):
            b = sp * 128
            mv = STATS[:, b + 96:b + 104].rearrange("p (t two) -> p t two", two=2)
            return dict(bn=STATS[:, b:b + 96],
                        mv=mv,
                        t1=STATS[:, b + 104:b + 108], t2=STATS[:, b + 108:b + 112],
                        rstd=STATS[:, b + 112:b + 116], t3=STATS[:, b + 116:b + 120])

        @block.sync
        def _(q):
            q.dma_start(out=IDT[:], in_=id_d).then_inc(sm["dCONST"], 16)
            q.dma_start(out=CONST[:, C_BQ:C_BQ + 16], in_=bq_d).then_inc(sm["dCONST"], 16)
            q.dma_start(out=CONST[:, C_BV:C_BV + 16], in_=bv_d).then_inc(sm["dCONST"], 16)
            q.dma_start(out=CONST[:, C_BO:C_BO + 16], in_=bo_d).then_inc(sm["dCONST"], 16)
            q.dma_start(out=CONST[:, C_DS:C_DS + 16], in_=ds_d).then_inc(sm["dCONST"], 16)
            q.dma_start(out=CONST[:, C_GS:C_GS + 44], in_=gs_d).then_inc(sm["dCONST"], 16)
            q.dma_start(out=CONST[:, C_US:C_US + 44], in_=us_d).then_inc(sm["dCONST"], 16)
            for tp in range(0, NT, 2):
                q.wait_ge(sm["vLT"], 4 * tp + 8)
                if tp < OWNT:
                    src = BIG0_16[:, :, tp * 128:tp * 128 + 256]
                else:
                    src = LTST_16
                q.dma_start(out=LTd[:, :, tp * 128:tp * 128 + 256].rearrange("c p s -> p c s"),
                            in_=src).then_inc(sm["dLTst"], 16)
            for ih in range(2):
                for hcp in range(16):
                    for it in range(2):
                        g2 = ih * 32 + hcp * 2 + it
                        r = g2 % 4
                        q.wait_ge(sm["aE2"], g2 + 1)
                        for half in range(2):
                            gabs = ih * 4 + it * 2 + half
                            q.dma_start(out=QpTd[gabs][:, hcp * 256:(hcp + 1) * 256],
                                        in_=FT[:, r * 512 + half * 256: r * 512 + (half + 1) * 256]
                                        ).then_inc(sm["dQP"], 16)
            for g in range(NG):
                if g == 0:
                    q.wait_ge(sm["dQP"], 16 * 128)
                else:
                    q.wait_ge(sm["tS"], g * 64)
                q.dma_start(out=QPT[:], in_=QpTd[g]).then_inc(sm["dQG"], 16)
                for kt in range(NKT):
                    slot = kt % 2
                    if kt >= 2:
                        q.wait_ge(sm["tS"], g * 64 + 2 * kt - 2)
                    elif g > 0:
                        q.wait_ge(sm["tCk"], g * 32)
                    for hq in range(4):
                        q.dma_start(
                            out=STRM[:, slot * 8192 + hq * 4 * 512: slot * 8192 + (hq + 1) * 4 * 512]
                                .rearrange("p (c j) -> p c j", c=4),
                            in_=LTd[hq * 4:(hq + 1) * 4, :, kt * 512:(kt + 1) * 512]
                                .rearrange("c p s -> p c s"),
                        ).then_inc(sm["dLTA"], 16)
            for ih in range(2):
                q.wait_ge(sm["dCT"], 1024 * (ih + 1))
                if ih == 0:
                    q.wait_ge(sm["vPT"], 512)
                else:
                    q.wait_ge(sm["tM3"], 32)
                q.dma_start(out=BIG1[:], in_=CTd[ih]).then_inc(sm["dCTl"], 16)
                for tt in range(8):
                    T = ih * 8 + tt
                    p = T % 2
                    if T >= 2:
                        q.wait_ge(sm["aP3"], 4 * (T - 2) + 4)
                        q.wait_ge(sm["dY"], 16 * (T - 1))
                    q.dma_start(out=SMALL[:, p * 2048:(p + 1) * 2048], in_=x_v[T]).then_inc(sm["dXo"], 16)
                    q.wait_ge(sm["vP3"], 11 * T + 6)
                    q.dma_start(out=yd_v[T], in_=SMALL[:, p * 2048:(p + 1) * 2048]).then_inc(sm["dY"], 16)
            for fc in range(FCN):
                for it in range(4):
                    idx = fc * 4 + it
                    q.wait_ge(sm["vH"], idx + 1)
                    q.dma_start(out=HDNd[it][:, fc * 512:(fc + 1) * 512],
                                in_=LTST[:, (idx % 4) * 512:(idx % 4 + 1) * 512]).then_inc(sm["dH"], 16)
            for it2 in range(4):
                if it2 == 0:
                    q.wait_ge(sm["dH"], 16 * FCN * 4)
                    q.wait_ge(sm["tG"], FCN * 4)
                    q.wait_ge(sm["tU"], FCN * 4)
                    q.wait_ge(sm["vTR3"], 128)
                else:
                    q.wait_ge(sm["tD"], 16 * it2)
                for ch in range(4):
                    f0 = ch * 11
                    q.dma_start(out=BIG0[:, f0 * 512:(f0 + 11) * 512],
                                in_=HDNd[it2][:, f0 * 512:(f0 + 11) * 512]).then_inc(sm["dHl"], 16)
                for tt in range(4):
                    To = it2 * 4 + tt
                    p = To % 2
                    if To == 0:
                        q.wait_ge(sm["dY"], 16 * 16)
                        q.wait_ge(sm["aP3"], 64)
                    if To >= 2:
                        q.wait_ge(sm["dOUT"], 16 * (To - 1))
                    q.dma_start(out=SMALL[:, p * 2048:(p + 1) * 2048], in_=yd_v[To]).then_inc(sm["dYl"], 16)
                    q.wait_ge(sm["vP4"], 6 * To + 6)
                    q.dma_start(out=out_v[To], in_=SMALL[:, p * 2048:(p + 1) * 2048]).then_inc(sm["dOUT"], 16)

        @block.gpsimd
        def _(g_):
            for s in range(NSL):
                sp = s % 2
                if s >= 2:
                    g_.wait_ge(sm["aP0"], 5 * (s - 2) + 5)
                g_.dma_start(out=BIG1[:, sp * 8192:(sp + 1) * 8192]
                             .rearrange("p (c d) -> p c d", c=4),
                             in_=x_v[4 * s:4 * s + 4].rearrange("c p d -> p c d")
                             ).then_inc(sm["dXB"], 16)
                for i in range(4):
                    t = 4 * s + i
                    g_.wait_ge(sm["aP0"], 5 * s + i + 2)
                    g_.dma_start(out=Ld_v[t], in_=SMALLB[:, (t % 2) * 2048:(t % 2 + 1) * 2048]
                                 ).then_inc(sm["dLst"], 16)
            for ih in range(2):
                for oc in range(16):
                    w = oc % 2
                    if ih == 1 and oc < 2:
                        g_.wait_ge(sm["tM2"], 32)
                    if oc >= 2:
                        g_.wait_ge(sm["tM1"], ih * 32 + 2 * oc - 2)
                    g_.dma_start(out=WCOL[:, w * 2048:(w + 1) * 2048], in_=wq_p[oc]).then_inc(sm["dW1"], 16)
                for hcp in range(16):
                    w = hcp % 2
                    g_.wait_ge(sm["tM1"], (ih + 1) * 32)
                    if hcp >= 2:
                        g_.wait_ge(sm["tM2"], ih * 32 + 2 * (hcp - 1))
                    g_.dma_start(out=WCOL[:, w * 2048:(w + 1) * 2048], in_=wk_p[hcp]).then_inc(sm["dW1"], 16)
            for oc in range(16):
                g_.dma_start(out=WDB[oc], in_=wd_p[oc]).then_inc(sm["dWB"], 16)
            for g in range(NG):
                for kt in range(NKT):
                    slot = kt % 2
                    if kt >= 2:
                        g_.wait_ge(sm["tCk"], g * 32 + kt - 1)
                    else:
                        g_.wait_ge(sm["tS"], g * 64 + 62 + 2 * kt)
                    g_.dma_start(out=STRM[:, slot * 8192:(slot + 1) * 8192]
                                 .rearrange("p (c j) -> p c j", c=4),
                                 in_=Ld_v[kt * 4:(kt + 1) * 4, :, :]
                                 .rearrange("c p s -> p c s")).then_inc(sm["dLC"], 16)
                for hc in range(16):
                    g_.wait_ge(sm["vCT"], g * 16 + hc + 1)
                    g_.dma_start(out=CTd[g // 4][:, hc * 1024 + (g % 4) * 256: hc * 1024 + (g % 4) * 256 + 256],
                                 in_=FT[:, hc * 256:(hc + 1) * 256]).then_inc(sm["dCT"], 16)
            for ih in range(2):
                for oc in range(16):
                    w = oc % 2
                    if ih == 0 and oc < 2:
                        g_.wait_ge(sm["tM2"], 64)
                    if ih == 1 and oc < 2:
                        g_.wait_ge(sm["tM4"], 32)
                    if oc >= 2:
                        g_.wait_ge(sm["tM3"], ih * 32 + 2 * oc - 2)
                    g_.dma_start(out=WCOL[:, w * 2048:(w + 1) * 2048], in_=wv_p[oc]).then_inc(sm["dW3"], 16)
                for ocp in range(16):
                    w = ocp % 2
                    g_.wait_ge(sm["tM3"], (ih + 1) * 32)
                    if ocp >= 2:
                        g_.wait_ge(sm["tM4"], ih * 32 + 2 * (ocp - 1))
                    g_.dma_start(out=WCOL[:, w * 2048:(w + 1) * 2048], in_=wo_p[ocp]).then_inc(sm["dW3"], 16)
            for fc in range(FCN):
                wslot = fc % 2
                if fc < 2:
                    g_.wait_ge(sm["tM4"], 64)
                else:
                    g_.wait_ge(sm["tG"], 4 * (fc - 1))
                g_.dma_start(out=WCOL[:, wslot * 2048:(wslot + 1) * 2048], in_=wg_p[fc]).then_inc(sm["dW4"], 16)
                if fc >= 2:
                    g_.wait_ge(sm["tU"], 4 * (fc - 1))
                g_.dma_start(out=WCOL[:, 4096 + wslot * 2048: 4096 + (wslot + 1) * 2048],
                             in_=wu_p[fc]).then_inc(sm["dW4"], 16)
            for it2 in range(4):
                for oc in range(16):
                    jdx = it2 * 16 + oc
                    slot = oc % 2
                    if jdx < 2:
                        g_.wait_ge(sm["dWB"], 256)
                        g_.wait_ge(sm["tG"], FCN * 4)
                        g_.wait_ge(sm["tU"], FCN * 4)
                    else:
                        g_.wait_ge(sm["tD"], jdx - 1)
                    g_.dma_start(out=STRM[:, slot * 8192: slot * 8192 + INTER],
                                 in_=WDB[oc]).then_inc(sm["dWd"], 16)
            if dbg:
                g_.wait_ge(sm["dLst"], 16 * NT)
                g_.dma_start(out=dbgL, in_=Ld[0:256, :]).then_inc(sm["dOUT"], 16)
                g_.wait_ge(sm["dLTst"], 16 * 64)
                g_.dma_start(out=dbgLT, in_=LTd[2, 0:128, 0:512]).then_inc(sm["dOUT"], 16)
                g_.wait_ge(sm["dQP"], 16 * 128)
                g_.dma_start(out=dbgQpT, in_=QpTd[1]).then_inc(sm["dOUT"], 16)
                g_.wait_ge(sm["dCT"], 2048)
                g_.dma_start(out=dbgCT, in_=CTd[0][:, 0:2048]).then_inc(sm["dOUT"], 16)
                g_.wait_ge(sm["dY"], 256)
                g_.dma_start(out=dbgY, in_=yd[0:256, :]).then_inc(sm["dOUT"], 16)
                g_.wait_ge(sm["dH"], 16 * FCN * 4)
                g_.dma_start(out=dbgHDN, in_=HDNd[0][:, 0:2048]).then_inc(sm["dOUT"], 16)
                g_.wait_ge(sm["dOUT"], 16 * 16 + 16 * 6)
            else:
                g_.wait_ge(sm["dOUT"], 16 * 16)

        @block.vector
        def _(v):
            for s in range(NSL):
                sp = s % 2
                sl = st0(sp)
                xs = BIG1[:, sp * 8192:(sp + 1) * 8192]
                v.wait_ge(sm["dXB"], 16 * (s + 1))
                if s >= 2:
                    v.wait_ge(sm["aP0"], 5 * (s - 2) + 5)
                for i4 in range(4):
                    for c in range(4):
                        i = v.bn_stats(sl["bn"][:, (i4 * 4 + c) * 6:(i4 * 4 + c + 1) * 6],
                                       xs[:, i4 * 2048 + c * 512: i4 * 2048 + (c + 1) * 512])
                i.then_inc(sm["vP0"], 1)
                v.wait_ge(sm["vP0"], 5 * s + 1)
                for i4 in range(4):
                    i = v.bn_aggr(sl["mv"][:, i4], sl["bn"][:, i4 * 24:(i4 + 1) * 24])
                i.then_inc(sm["vP0"], 1)
                v.wait_ge(sm["vP0"], 5 * s + 2)
                v.tensor_scalar_add(sl["t1"], sl["mv"][:, :, 1], EPS).then_inc(sm["vP0"], 1)
                v.wait_ge(sm["aP0"], 5 * s + 1)
                v.reciprocal(sl["rstd"], sl["t2"]).then_inc(sm["vP0"], 1)
                v.wait_ge(sm["vP0"], 5 * s + 4)
                v.scalar_tensor_tensor(sl["t3"], sl["mv"][:, :, 0], -1.0, sl["rstd"],
                                       ALU.mult, ALU.mult).then_inc(sm["vP0"], 1)
                for i4 in range(4):
                    t = 4 * s + i4
                    for jg in range(4):
                        gidx = t * 4 + jg
                        v.wait_ge(sm["tTR"], 16 * t + 4 * jg + 4)
                        src = bb[gidx % 2][:, 0:512].rearrange("p (a b) -> p a b", a=4)
                        if t < OWNT:
                            dest = BIG0_16[:, jg * 4:(jg + 1) * 4, t * 128:t * 128 + 128]
                        else:
                            if jg == 0 and t >= 18:
                                v.wait_ge(sm["dLTst"], 16 * (8 + (t - 18) // 2 + 1))
                            dest = LTST_16[:, jg * 4:(jg + 1) * 4, (t % 2) * 128:(t % 2) * 128 + 128]
                        v.tensor_copy(dest, src).then_inc(sm["vLT"], 1)
            for g in range(NG):
                for kt in range(NKT):
                    for qi in range(2):
                        idx = g * 64 + kt * 2 + qi
                        v.wait_ge(sm["aS"], idx + 1)
                        v.reduce_max(STATS[:, ST_MX + qi * 32 + kt: ST_MX + qi * 32 + kt + 1],
                                     BIG0[:, qi * S + kt * 512: qi * S + (kt + 1) * 512],
                                     axis=AX.X).then_inc(sm["vKM"], 1)
                for qi in range(2):
                    b = g * 2 + qi
                    v.wait_ge(sm["vKM"], g * 64 + 63 + qi)
                    v.reduce_max(STATS[:, ST_NEGM + qi:ST_NEGM + qi + 1],
                                 STATS[:, ST_MX + qi * 32: ST_MX + qi * 32 + 32],
                                 axis=AX.X, negate=True).then_inc(sm["vFM"], 1)
                    v.wait_ge(sm["aEX"], 4 * b + 4)
                    v.reduce_sum(STATS[:, ST_SES + qi:ST_SES + qi + 1],
                                 STATS[:, ST_SE4 + qi * 4: ST_SE4 + qi * 4 + 4],
                                 axis=AX.X).then_inc(sm["vPS"], 1)
                    v.wait_ge(sm["vPS"], 6 * b + 1)
                    v.reciprocal(STATS[:, ST_RS + qi:ST_RS + qi + 1],
                                 STATS[:, ST_SES + qi:ST_SES + qi + 1]).then_inc(sm["vPS"], 1)
                    v.wait_ge(sm["vPS"], 6 * b + 2)
                    for cch in range(4):
                        v.tensor_single_scalar(BIG1[:, cch * 4096:(cch + 1) * 4096],
                                               BIG1[:, cch * 4096:(cch + 1) * 4096],
                                               STATS[:, ST_RS + qi:ST_RS + qi + 1],
                                               ALU.mult).then_inc(sm["vPS"], 1)
                    for jg in range(32):
                        gpt = b * 32 + jg
                        v.wait_ge(sm["tTP"], gpt * 4 + 4)
                        v.tensor_copy(BIG0[:, qi * S + jg * 512: qi * S + (jg + 1) * 512],
                                      bb[gpt % 2][:, 0:512]).then_inc(sm["vPT"], 1)
                for hc in range(16):
                    v.wait_ge(sm["tCk"], g * 32 + 32)
                    if hc == 0:
                        if g == 0:
                            v.wait_ge(sm["dQP"], 16 * 128)
                        else:
                            v.wait_ge(sm["dCT"], 256 * g)
                    v.tensor_copy(FT[:, hc * 256:(hc + 1) * 256],
                                  pb[hc // 2][:, (hc % 2) * 256:(hc % 2) * 256 + 256]
                                  ).then_inc(sm["vCT"], 1)
            for ih in range(2):
                for tt in range(8):
                    T = ih * 8 + tt
                    p = T % 2
                    for kg in range(4):
                        uix = T * 8 + kg
                        if uix == 0:
                            v.wait_ge(sm["dLst"], 16 * NT)
                        v.wait_ge(sm["tTR3"], (T * 32) + 4 * kg + 4)
                        v.tensor_copy(SMALLB[:, p * 2048 + kg * 512: p * 2048 + (kg + 1) * 512],
                                      bb[uix % 2][:, 0:512]).then_inc(sm["vTR3"], 1)
                    slA = st3(ST_P3A, p)
                    v.wait_ge(sm["vTR3"], T * 8 + 4)
                    ats = SMALLB[:, p * 2048:(p + 1) * 2048]
                    for c in range(4):
                        i = v.bn_stats(slA["bn"][:, c * 6:(c + 1) * 6], ats[:, c * 512:(c + 1) * 512])
                    i.then_inc(sm["vP3"], 1)
                    v.wait_ge(sm["vP3"], 11 * T + 1)
                    v.bn_aggr(slA["mv"], slA["bn"]).then_inc(sm["vP3"], 1)
                    v.wait_ge(sm["vP3"], 11 * T + 2)
                    v.tensor_scalar_add(slA["t1"], slA["mv"][:, 1:2], EPS).then_inc(sm["vP3"], 1)
                    v.wait_ge(sm["aP3"], 4 * T + 1)
                    v.reciprocal(slA["rstd"], slA["t2"]).then_inc(sm["vP3"], 1)
                    v.wait_ge(sm["vP3"], 11 * T + 4)
                    v.scalar_tensor_tensor(slA["t3"], slA["mv"][:, 0:1], -1.0, slA["rstd"],
                                           ALU.mult, ALU.mult).then_inc(sm["vP3"], 1)
                    xs3 = SMALL[:, p * 2048:(p + 1) * 2048]
                    v.wait_ge(sm["aP3"], 4 * T + 2)
                    v.wait_ge(sm["dXo"], 16 * (T + 1))
                    v.tensor_tensor(xs3, xs3, SMALL[:, 4096:6144], ALU.add).then_inc(sm["vP3"], 1)
                    slB = st3(ST_P3B, p)
                    v.wait_ge(sm["vP3"], 11 * T + 6)
                    for c in range(4):
                        i = v.bn_stats(slB["bn"][:, c * 6:(c + 1) * 6], xs3[:, c * 512:(c + 1) * 512])
                    i.then_inc(sm["vP3"], 1)
                    v.wait_ge(sm["vP3"], 11 * T + 7)
                    v.bn_aggr(slB["mv"], slB["bn"]).then_inc(sm["vP3"], 1)
                    v.wait_ge(sm["vP3"], 11 * T + 8)
                    v.tensor_scalar_add(slB["t1"], slB["mv"][:, 1:2], EPS).then_inc(sm["vP3"], 1)
                    v.wait_ge(sm["aP3"], 4 * T + 3)
                    v.reciprocal(slB["rstd"], slB["t2"]).then_inc(sm["vP3"], 1)
                    v.wait_ge(sm["vP3"], 11 * T + 10)
                    v.scalar_tensor_tensor(slB["t3"], slB["mv"][:, 0:1], -1.0, slB["rstd"],
                                           ALU.mult, ALU.mult).then_inc(sm["vP3"], 1)
                    for kg in range(4):
                        uix = T * 8 + 4 + kg
                        v.wait_ge(sm["tTR3"], (T * 32) + 16 + 4 * kg + 4)
                        if ih == 0:
                            dest = STRM[:].rearrange("p (c i) -> p c i", c=16)[
                                :, kg * 4:(kg + 1) * 4, T * 128:T * 128 + 128]
                        else:
                            dest = BIG1[:].rearrange("p (c i) -> p c i", c=16)[
                                :, kg * 4:(kg + 1) * 4, (T - 8) * 128:(T - 8) * 128 + 128]
                        v.tensor_copy(dest, bb[uix % 2][:, 0:512].rearrange("p (a b) -> p a b", a=4)
                                      ).then_inc(sm["vTR3"], 1)
            for fc in range(FCN):
                for it in range(4):
                    idx = fc * 4 + it
                    v.wait_ge(sm["aG"], idx + 1)
                    v.wait_ge(sm["aU"], idx + 1)
                    if idx == 0:
                        v.wait_ge(sm["dLTst"], 16 * 64)
                    if idx >= 4:
                        v.wait_ge(sm["dH"], 16 * (idx - 3))
                    v.tensor_mul(LTST[:, (idx % 4) * 512:(idx % 4 + 1) * 512],
                                 SMALLB[:, (idx % 2) * 512:(idx % 2 + 1) * 512],
                                 SMALLB[:, 1024 + (idx % 2) * 512: 1024 + (idx % 2 + 1) * 512]
                                 ).then_inc(sm["vH"], 1)
            for it2 in range(4):
                for tt in range(4):
                    To = it2 * 4 + tt
                    p = To % 2
                    for og in range(4):
                        fgi = To * 4 + og
                        v.wait_ge(sm["tTF"], fgi * 4 + 4)
                        if og == 0:
                            if To == 0:
                                v.wait_ge(sm["aG"], FCN * 4)
                                v.wait_ge(sm["aU"], FCN * 4)
                            if To >= 2:
                                v.wait_ge(sm["aP4"], 2 * To - 2)
                        v.tensor_copy(SMALLB[:, p * 2048 + og * 512: p * 2048 + (og + 1) * 512],
                                      bb[fgi % 2][:, 0:512]).then_inc(sm["vTF"], 1)
                    sl4 = st3(ST_P4, p)
                    v.wait_ge(sm["vTF"], To * 4 + 4)
                    fts = SMALLB[:, p * 2048:(p + 1) * 2048]
                    for c in range(4):
                        i = v.bn_stats(sl4["bn"][:, c * 6:(c + 1) * 6], fts[:, c * 512:(c + 1) * 512])
                    i.then_inc(sm["vP4"], 1)
                    v.wait_ge(sm["vP4"], 6 * To + 1)
                    v.bn_aggr(sl4["mv"], sl4["bn"]).then_inc(sm["vP4"], 1)
                    v.wait_ge(sm["vP4"], 6 * To + 2)
                    v.tensor_scalar_add(sl4["t1"], sl4["mv"][:, 1:2], EPS).then_inc(sm["vP4"], 1)
                    v.wait_ge(sm["aP4"], 2 * To + 1)
                    v.reciprocal(sl4["rstd"], sl4["t2"]).then_inc(sm["vP4"], 1)
                    v.wait_ge(sm["vP4"], 6 * To + 4)
                    v.scalar_tensor_tensor(sl4["t3"], sl4["mv"][:, 0:1], -1.0, sl4["rstd"],
                                           ALU.mult, ALU.mult).then_inc(sm["vP4"], 1)
                    xs4 = SMALL[:, p * 2048:(p + 1) * 2048]
                    v.wait_ge(sm["aP4"], 2 * To + 2)
                    v.wait_ge(sm["dYl"], 16 * (To + 1))
                    v.tensor_tensor(xs4, xs4, SMALL[:, 4096:6144], ALU.add).then_inc(sm["vP4"], 1)

        @block.scalar
        def _(s):
            for sab in range(NSL):
                sp = sab % 2
                sl = st0(sp)
                s.wait_ge(sm["vP0"], 5 * sab + 3)
                s.activation(sl["t2"], sl["t1"], AF.Sqrt).then_inc(sm["aP0"], 1)
                for i4 in range(4):
                    t = 4 * sab + i4
                    s.wait_ge(sm["vP0"], 5 * sab + 5)
                    if t >= 2:
                        s.wait_ge(sm["tTR"], 16 * (t - 1))
                        s.wait_ge(sm["dLst"], 16 * (t - 1))
                    s.activation(SMALLB[:, (t % 2) * 2048:(t % 2 + 1) * 2048],
                                 BIG1[:, sp * 8192 + i4 * 2048: sp * 8192 + (i4 + 1) * 2048],
                                 AF.Identity, bias=sl["t3"][:, i4:i4 + 1],
                                 scale=sl["rstd"][:, i4:i4 + 1]).then_inc(sm["aP0"], 1)
            s.wait_ge(sm["dCONST"], 16 * 7)
            s.wait_ge(sm["vP0"], 5 * NSL)
            for ih in range(2):
                for oc in range(16):
                    for it in range(2):
                        g1 = ih * 32 + oc * 2 + it
                        s.wait_ge(sm["tM1"], g1 + 1)
                        if ih == 1 and g1 == 32:
                            s.wait_ge(sm["tM2"], 32)
                        s.activation(BIG1[:, oc * 1024 + it * 512: oc * 1024 + (it + 1) * 512],
                                     pb[4 + g1 % 4][:],
                                     AF.Identity, bias=CONST[:, C_BQ + oc:C_BQ + oc + 1]).then_inc(sm["aE1"], 1)
                for hcp in range(16):
                    for it in range(2):
                        g2 = ih * 32 + hcp * 2 + it
                        r = g2 % 4
                        s.wait_ge(sm["tM2"], g2 + 1)
                        if g2 >= 4:
                            s.wait_ge(sm["dQP"], 16 * (2 * g2 - 6))
                        s.activation(FT[:, r * 512:(r + 1) * 512], pb[4 + g2 % 4][:],
                                     AF.Identity, scale=RSCALE).then_inc(sm["aE2"], 1)
            for g in range(NG):
                for kt in range(NKT):
                    for qi in range(2):
                        idx = g * 64 + kt * 2 + qi
                        s.wait_ge(sm["tS"], idx + 1)
                        if g > 0 and kt == 0:
                            s.wait_ge(sm["tCk"], g * 32)
                        s.activation(BIG0[:, qi * S + kt * 512: qi * S + (kt + 1) * 512],
                                     pb[(kt * 2 + qi) % 2][:], AF.Copy).then_inc(sm["aS"], 1)
                for qi in range(2):
                    b = g * 2 + qi
                    for cch in range(4):
                        if cch == 0:
                            s.wait_ge(sm["vFM"], b + 1)
                        if b >= 1:
                            s.wait_ge(sm["vPT"], (b - 1) * 32 + 8 * (cch + 1))
                        s.activation(BIG1[:, cch * 4096:(cch + 1) * 4096],
                                     BIG0[:, qi * S + cch * 4096: qi * S + (cch + 1) * 4096],
                                     AF.Exp,
                                     bias=STATS[:, ST_NEGM + qi:ST_NEGM + qi + 1], scale=1.0,
                                     accum_out=STATS[:, ST_SE4 + qi * 4 + cch: ST_SE4 + qi * 4 + cch + 1]
                                     ).then_inc(sm["aEX"], 1)
            for ih in range(2):
                for oc in range(16):
                    for it in range(2):
                        g3 = ih * 32 + oc * 2 + it
                        s.wait_ge(sm["tM3"], g3 + 1)
                        s.activation(BIG0[:, oc * 1024 + it * 512: oc * 1024 + (it + 1) * 512],
                                     pb[4 + g3 % 4][:], AF.Identity,
                                     bias=CONST[:, C_BV + oc:C_BV + oc + 1]).then_inc(sm["aE3"], 1)
                for ocp in range(16):
                    for it in range(2):
                        g4 = ih * 32 + ocp * 2 + it
                        s.wait_ge(sm["tM4"], g4 + 1)
                        s.activation(BIG0[:, S + ocp * 1024 + it * 512: S + ocp * 1024 + (it + 1) * 512],
                                     pb[4 + g4 % 4][:], AF.Identity,
                                     bias=CONST[:, C_BO + ocp:C_BO + ocp + 1]).then_inc(sm["aE4"], 1)
                for tt in range(8):
                    T = ih * 8 + tt
                    p = T % 2
                    slA = st3(ST_P3A, p)
                    slB = st3(ST_P3B, p)
                    s.wait_ge(sm["vP3"], 11 * T + 3)
                    s.activation(slA["t2"], slA["t1"], AF.Sqrt).then_inc(sm["aP3"], 1)
                    s.wait_ge(sm["vP3"], 11 * T + 5)
                    s.activation(SMALL[:, 4096:6144], SMALLB[:, p * 2048:(p + 1) * 2048],
                                 AF.Identity, bias=slA["t3"], scale=slA["rstd"]).then_inc(sm["aP3"], 1)
                    s.wait_ge(sm["vP3"], 11 * T + 9)
                    s.activation(slB["t2"], slB["t1"], AF.Sqrt).then_inc(sm["aP3"], 1)
                    s.wait_ge(sm["vP3"], 11 * T + 11)
                    s.activation(SMALLB[:, p * 2048:(p + 1) * 2048], SMALL[:, p * 2048:(p + 1) * 2048],
                                 AF.Identity, bias=slB["t3"], scale=slB["rstd"]).then_inc(sm["aP3"], 1)
            for fc in range(FCN):
                for it in range(4):
                    idx = fc * 4 + it
                    s.wait_ge(sm["tG"], idx + 1)
                    if idx == 0:
                        s.wait_ge(sm["vTR3"], 128)
                    if idx >= 2:
                        s.wait_ge(sm["vH"], idx - 1)
                    s.activation(SMALLB[:, (idx % 2) * 512:(idx % 2 + 1) * 512],
                                 pb[4 + idx % 2][:], AF.Silu,
                                 scale=CONST[:, C_GS + fc:C_GS + fc + 1]).then_inc(sm["aG"], 1)
                    s.wait_ge(sm["tU"], idx + 1)
                    s.activation(SMALLB[:, 1024 + (idx % 2) * 512: 1024 + (idx % 2 + 1) * 512],
                                 pb[6 + idx % 2][:], AF.Identity,
                                 scale=CONST[:, C_US + fc:C_US + fc + 1]).then_inc(sm["aU"], 1)
            for it2 in range(4):
                for oc in range(16):
                    jdx = it2 * 16 + oc
                    s.wait_ge(sm["tD"], jdx + 1)
                    if jdx == 0:
                        s.wait_ge(sm["vTR3"], 128)
                    if oc == 0 and it2 >= 1:
                        s.wait_ge(sm["tTF"], it2 * 64)
                    s.activation(BIG0[:, 24576 + oc * 512: 24576 + (oc + 1) * 512],
                                 pb[4 + jdx % 2][:], AF.Identity,
                                 scale=CONST[:, C_DS + oc:C_DS + oc + 1]).then_inc(sm["aD"], 1)
                for tt in range(4):
                    To = it2 * 4 + tt
                    p = To % 2
                    sl4 = st3(ST_P4, p)
                    s.wait_ge(sm["vP4"], 6 * To + 3)
                    s.activation(sl4["t2"], sl4["t1"], AF.Sqrt).then_inc(sm["aP4"], 1)
                    s.wait_ge(sm["vP4"], 6 * To + 5)
                    s.activation(SMALL[:, 4096:6144], SMALLB[:, p * 2048:(p + 1) * 2048],
                                 AF.Identity, bias=sl4["t3"], scale=sl4["rstd"]).then_inc(sm["aP4"], 1)

        @block.tensor
        def _(t_):
            t_.wait_ge(sm["dCONST"], 16)
            for t in range(NT):
                t_.wait_ge(sm["aP0"], 5 * (t // 4) + (t % 4) + 2)
                for jg in range(4):
                    gidx = t * 4 + jg
                    if gidx >= 2:
                        t_.wait_ge(sm["vLT"], gidx - 1)
                    for k in range(4):
                        j = jg * 4 + k
                        t_.transpose(bb[gidx % 2][:, k * 128:(k + 1) * 128],
                                     SMALLB[:, (t % 2) * 2048 + j * 128: (t % 2) * 2048 + (j + 1) * 128],
                                     IDT[:]).then_inc(sm["tTR"], 1)
            t_.wait_ge(sm["vLT"], 64)
            for ih in range(2):
                for oc in range(16):
                    for it in range(2):
                        g1 = ih * 32 + oc * 2 + it
                        t_.wait_ge(sm["dW1"], 16 * (ih * 32 + oc + 1))
                        if g1 >= 4:
                            t_.wait_ge(sm["aE1"], g1 - 3)
                        w = oc % 2
                        for hc in range(16):
                            i = t_.matmul(pb[4 + g1 % 4][:],
                                          WCOL[:, w * 2048 + hc * 128: w * 2048 + (hc + 1) * 128],
                                          BIG0[:, hc * 2048 + ih * 1024 + it * 512:
                                               hc * 2048 + ih * 1024 + (it + 1) * 512],
                                          start=(hc == 0), stop=(hc == 15))
                        i.then_inc(sm["tM1"], 1)
                for hcp in range(16):
                    for it in range(2):
                        g2 = ih * 32 + hcp * 2 + it
                        t_.wait_ge(sm["dW1"], 16 * (ih * 32 + 16 + hcp + 1))
                        if g2 < ih * 32 + 4:
                            t_.wait_ge(sm["aE1"], (ih + 1) * 32)
                        if g2 >= 4:
                            t_.wait_ge(sm["aE2"], g2 - 3)
                        w = hcp % 2
                        for oc in range(16):
                            i = t_.matmul(pb[4 + g2 % 4][:],
                                          WCOL[:, w * 2048 + oc * 128: w * 2048 + (oc + 1) * 128],
                                          BIG1[:, oc * 1024 + it * 512: oc * 1024 + (it + 1) * 512],
                                          start=(oc == 0), stop=(oc == 15))
                        i.then_inc(sm["tM2"], 1)
            for g in range(NG):
                t_.wait_ge(sm["dQG"], 16 * (g + 1))
                for kt in range(NKT):
                    slot = kt % 2
                    t_.wait_ge(sm["dLTA"], 64 * (g * 32 + kt + 1))
                    for qi in range(2):
                        idx = g * 64 + kt * 2 + qi
                        if g == 0 and idx < 2:
                            t_.wait_ge(sm["vLT"], 512)
                        if g > 0 and kt == 0:
                            t_.wait_ge(sm["vCT"], (g - 1) * 16 + 4)
                        if idx - g * 64 >= 2:
                            t_.wait_ge(sm["aS"], idx - 1)
                        for hc in range(16):
                            i = t_.matmul(pb[(kt * 2 + qi) % 2][:],
                                          QPT[:, hc * 256 + qi * 128: hc * 256 + qi * 128 + 128],
                                          STRM[:, slot * 8192 + hc * 512: slot * 8192 + (hc + 1) * 512],
                                          start=(hc == 0), stop=(hc == 15))
                        i.then_inc(sm["tS"], 1)
                for qi in range(2):
                    b = g * 2 + qi
                    for jg in range(32):
                        gpt = b * 32 + jg
                        t_.wait_ge(sm["vPS"], 6 * b + 3 + jg // 8)
                        if qi == 0 and jg < 2:
                            t_.wait_ge(sm["aS"], g * 64 + 64)
                        if gpt >= 2:
                            t_.wait_ge(sm["vPT"], gpt - 1)
                        for k in range(4):
                            j = jg * 4 + k
                            t_.transpose(bb[gpt % 2][:, k * 128:(k + 1) * 128],
                                         BIG1[:, j * 128:(j + 1) * 128], IDT[:]).then_inc(sm["tTP"], 1)
                for kt in range(NKT):
                    slot = kt % 2
                    t_.wait_ge(sm["dLC"], 16 * (g * 32 + kt + 1))
                    if kt == 0:
                        t_.wait_ge(sm["vPT"], g * 64 + 64)
                        if g > 0:
                            t_.wait_ge(sm["vCT"], g * 16)
                    for jb in range(4):
                        rhs = BIG0q[:, :, (kt * 4 + jb) * 128:(kt * 4 + jb + 1) * 128]
                        for hc in range(16):
                            i = t_.matmul(
                                pb[hc // 2][:, (hc % 2) * 256:(hc % 2) * 256 + 256]
                                .rearrange("p (a b) -> p a b", a=2),
                                STRM[:, slot * 8192 + jb * 2048 + hc * 128:
                                     slot * 8192 + jb * 2048 + (hc + 1) * 128],
                                rhs,
                                start=(kt == 0 and jb == 0 and hc % 2 == 0),
                                stop=(kt == NKT - 1 and jb == 3 and hc % 2 == 1),
                                skip_group_check=True)
                    i.then_inc(sm["tCk"], 1)
            for ih in range(2):
                t_.wait_ge(sm["dCTl"], 16 * (ih + 1))
                for oc in range(16):
                    for it in range(2):
                        g3 = ih * 32 + oc * 2 + it
                        t_.wait_ge(sm["dW3"], 16 * (ih * 32 + oc + 1))
                        if g3 == 0:
                            t_.wait_ge(sm["vCT"], 128)
                        if g3 >= 4:
                            t_.wait_ge(sm["aE3"], g3 - 3)
                        w = oc % 2
                        for hc in range(16):
                            i = t_.matmul(pb[4 + g3 % 4][:],
                                          WCOL[:, w * 2048 + hc * 128: w * 2048 + (hc + 1) * 128],
                                          BIG1[:, hc * 1024 + it * 512: hc * 1024 + (it + 1) * 512],
                                          start=(hc == 0), stop=(hc == 15))
                        i.then_inc(sm["tM3"], 1)
                for ocp in range(16):
                    for it in range(2):
                        g4 = ih * 32 + ocp * 2 + it
                        t_.wait_ge(sm["dW3"], 16 * (ih * 32 + 16 + ocp + 1))
                        if g4 < ih * 32 + 4:
                            t_.wait_ge(sm["aE3"], ih * 32 + 32)
                        if g4 >= 4:
                            t_.wait_ge(sm["aE4"], g4 - 3)
                        w = ocp % 2
                        for oc in range(16):
                            i = t_.matmul(pb[4 + g4 % 4][:],
                                          WCOL[:, w * 2048 + oc * 128: w * 2048 + (oc + 1) * 128],
                                          BIG0[:, oc * 1024 + it * 512: oc * 1024 + (it + 1) * 512],
                                          start=(oc == 0), stop=(oc == 15))
                        i.then_inc(sm["tM4"], 1)
                for tt in range(8):
                    T = ih * 8 + tt
                    p = T % 2
                    for kg in range(4):
                        uix = T * 8 + kg
                        if uix < 2:
                            t_.wait_ge(sm["vCT"], 128)
                        if kg == 0:
                            t_.wait_ge(sm["aE4"], ih * 32 + 32)
                        if uix >= 2:
                            t_.wait_ge(sm["vTR3"], uix - 1)
                        for k4 in range(4):
                            k = kg * 4 + k4
                            t_.transpose(bb[uix % 2][:, k4 * 128:(k4 + 1) * 128],
                                         BIG0[:, S + k * 1024 + tt * 128: S + k * 1024 + tt * 128 + 128],
                                         IDT[:]).then_inc(sm["tTR3"], 1)
                    for kg in range(4):
                        uix = T * 8 + 4 + kg
                        if kg == 0:
                            t_.wait_ge(sm["aP3"], 4 * T + 4)
                        if uix >= 2:
                            t_.wait_ge(sm["vTR3"], uix - 1)
                        for k4 in range(4):
                            k = kg * 4 + k4
                            t_.transpose(bb[uix % 2][:, k4 * 128:(k4 + 1) * 128],
                                         SMALLB[:, p * 2048 + k * 128: p * 2048 + (k + 1) * 128],
                                         IDT[:]).then_inc(sm["tTR3"], 1)
            for fc in range(FCN):
                for it in range(4):
                    idx = fc * 4 + it
                    t_.wait_ge(sm["dW4"], 32 * fc + 16)
                    if idx == 0:
                        t_.wait_ge(sm["aE3"], 64)
                        t_.wait_ge(sm["vTR3"], 128)
                    if idx >= 2:
                        t_.wait_ge(sm["aG"], idx - 1)
                    wslot = fc % 2
                    rhsbuf = STRM if it < 2 else BIG1
                    itl = it if it < 2 else it - 2
                    for hc in range(16):
                        i = t_.matmul(pb[4 + idx % 2][:],
                                      WCOL[:, wslot * 2048 + hc * 128: wslot * 2048 + (hc + 1) * 128],
                                      rhsbuf[:, hc * 1024 + itl * 512: hc * 1024 + (itl + 1) * 512],
                                      start=(hc == 0), stop=(hc == 15))
                    i.then_inc(sm["tG"], 1)
                    t_.wait_ge(sm["dW4"], 32 * fc + 32)
                    if idx == 0:
                        t_.wait_ge(sm["aE4"], 64)
                    if idx >= 2:
                        t_.wait_ge(sm["aU"], idx - 1)
                    for hc in range(16):
                        i = t_.matmul(pb[6 + idx % 2][:],
                                      WCOL[:, 4096 + wslot * 2048 + hc * 128:
                                           4096 + wslot * 2048 + (hc + 1) * 128],
                                      rhsbuf[:, hc * 1024 + itl * 512: hc * 1024 + (itl + 1) * 512],
                                      start=(hc == 0), stop=(hc == 15))
                    i.then_inc(sm["tU"], 1)
            for it2 in range(4):
                for oc in range(16):
                    jdx = it2 * 16 + oc
                    t_.wait_ge(sm["dWd"], 16 * (jdx + 1))
                    if jdx == 0:
                        t_.wait_ge(sm["aG"], FCN * 4)
                        t_.wait_ge(sm["aU"], FCN * 4)
                    if jdx >= 2:
                        t_.wait_ge(sm["aD"], jdx - 1)
                    slot = oc % 2
                    for fc in range(FCN):
                        if oc == 0 and fc % 11 == 0:
                            t_.wait_ge(sm["dHl"], 16 * (it2 * 4 + fc // 11 + 1))
                        i = t_.matmul(pb[4 + jdx % 2][:],
                                      STRM[:, slot * 8192 + fc * 128: slot * 8192 + (fc + 1) * 128],
                                      BIG0[:, fc * 512:(fc + 1) * 512],
                                      start=(fc == 0), stop=(fc == FCN - 1))
                    i.then_inc(sm["tD"], 1)
                t_.wait_ge(sm["aD"], it2 * 16 + 16)
                for tt in range(4):
                    To = it2 * 4 + tt
                    for og in range(4):
                        fgi = To * 4 + og
                        if fgi < 2:
                            t_.wait_ge(sm["vTR3"], 128)
                        if fgi >= 2:
                            t_.wait_ge(sm["vTF"], fgi - 1)
                        for k4 in range(4):
                            oc = og * 4 + k4
                            t_.transpose(bb[fgi % 2][:, k4 * 128:(k4 + 1) * 128],
                                         BIG0[:, 24576 + oc * 512 + tt * 128:
                                              24576 + oc * 512 + tt * 128 + 128],
                                         IDT[:]).then_inc(sm["tTF"], 1)

    return nc


def _pack(Wp):
    K, M = Wp.shape
    R, C = K // 128, M // 128
    return np.ascontiguousarray(
        Wp.reshape(R, 128, C, 128).transpose(2, 1, 0, 3).reshape(C, 128, R * 128))


def _prep_shared(i):
    wqkv = np.asarray(i["L0_attn_w_qkv"], np.float32)
    m = {
        "wq_p": _pack(np.ascontiguousarray(wqkv[0:2048].T)),
        "wk_p": _pack(np.ascontiguousarray(wqkv[2048:4096])),
        "wv_p": _pack(np.ascontiguousarray(wqkv[4096:6144].T)),
        "wo_p": _pack(np.ascontiguousarray(np.asarray(i["L0_attn_w_out"], np.float32).T)),
        "wg_p": _pack(np.ascontiguousarray(np.asarray(i["L0_ffn_gate_w"], np.float32).T)),
        "wu_p": _pack(np.ascontiguousarray(np.asarray(i["L0_ffn_up_w"], np.float32).T)),
        "wd_p": _pack(np.ascontiguousarray(np.asarray(i["L0_ffn_down_w"], np.float32).T)),
        "bq": np.ascontiguousarray(np.asarray(i["L0_attn_b_qkv"], np.float32)[0:2048].reshape(16, 128).T),
        "bv": np.ascontiguousarray(np.asarray(i["L0_attn_b_qkv"], np.float32)[4096:6144].reshape(16, 128).T),
        "bo": np.ascontiguousarray(np.asarray(i["L0_attn_b_out"], np.float32).reshape(16, 128).T),
        "ds": np.ascontiguousarray(np.asarray(i["L0_ffn_down_s"], np.float32).reshape(16, 128).T),
        "gs": np.ascontiguousarray(np.asarray(i["L0_ffn_gate_s"], np.float32).reshape(44, 128).T),
        "us": np.ascontiguousarray(np.asarray(i["L0_ffn_up_s"], np.float32).reshape(44, 128).T),
        "ident": np.eye(128, dtype=ml_dtypes.bfloat16),
    }
    return m


def kernel(**inputs):
    x = np.asarray(inputs["L0_attn_hidden_states"], np.float32).reshape(S, D)
    shared = _prep_shared(inputs)
    nc = build(dbg=False)
    in_maps = []
    for c in range(8):
        m = dict(shared)
        m["x"] = np.ascontiguousarray(np.roll(x, -2048 * c, axis=0))
        in_maps.append(m)
    res = run_bass_kernel_spmd(nc, in_maps, core_ids=list(range(8)))
    out = np.concatenate([np.asarray(res.results[c]["out"]) for c in range(8)], axis=0)
    return out.reshape(1, S, D).astype(np.float32)


# version 03
# speedup vs baseline: 1.1598x; 1.0029x over previous
import contextlib
import numpy as np
import ml_dtypes
import concourse.bass as bass
import concourse.mybir as mybir
from concourse.bass_utils import run_bass_kernel_spmd

F32 = mybir.dt.float32
BF16 = mybir.dt.bfloat16
AF = mybir.ActivationFunctionType
ALU = mybir.AluOpType
AX = mybir.AxisListType

S = 16384
D = 2048
HCN = 16
NT = 128
NSL = 32
OWNT = 16
OWN = 2048
KT = 512
NKT = 32
NG = 8
INTER = 5632
FCN = 44
EPS = 1e-5
RSCALE = float(np.float32(1.0) / np.float32(np.sqrt(np.float32(2048.0))))

C_BQ = 0
C_BV = 16
C_BO = 32
C_DS = 48
C_GS = 64
C_US = 108

ST_MX = 256
ST_NEGM = 320
ST_SE4 = 324
ST_SES = 332
ST_RS = 334
ST_P3A = 336
ST_P3B = 396
ST_P4 = 456


def build(dbg=False):
    nc = bass.Bass("TRN2", target_bir_lowering=False)

    x_d = nc.dram_tensor("x", [S, D], F32, kind="ExternalInput").ap()
    wq_p = nc.dram_tensor("wq_p", [16, 128, D], F32, kind="ExternalInput").ap()
    wk_p = nc.dram_tensor("wk_p", [16, 128, D], F32, kind="ExternalInput").ap()
    wv_p = nc.dram_tensor("wv_p", [16, 128, D], F32, kind="ExternalInput").ap()
    wo_p = nc.dram_tensor("wo_p", [16, 128, D], F32, kind="ExternalInput").ap()
    wg_p = nc.dram_tensor("wg_p", [44, 128, D], F32, kind="ExternalInput").ap()
    wu_p = nc.dram_tensor("wu_p", [44, 128, D], F32, kind="ExternalInput").ap()
    wd_p = nc.dram_tensor("wd_p", [16, 128, INTER], F32, kind="ExternalInput").ap()
    bq_d = nc.dram_tensor("bq", [128, 16], F32, kind="ExternalInput").ap()
    bv_d = nc.dram_tensor("bv", [128, 16], F32, kind="ExternalInput").ap()
    bo_d = nc.dram_tensor("bo", [128, 16], F32, kind="ExternalInput").ap()
    ds_d = nc.dram_tensor("ds", [128, 16], F32, kind="ExternalInput").ap()
    gs_d = nc.dram_tensor("gs", [128, 44], F32, kind="ExternalInput").ap()
    us_d = nc.dram_tensor("us", [128, 44], F32, kind="ExternalInput").ap()
    id_d = nc.dram_tensor("ident", [128, 128], BF16, kind="ExternalInput").ap()
    out_d = nc.dram_tensor("out", [OWN, D], F32, kind="ExternalOutput").ap()

    Ld = nc.dram_tensor("Ld", [S, D], BF16, kind="Internal").ap()
    LTd = nc.dram_tensor("LTd", [16, 128, S], BF16, kind="Internal").ap()
    QpTd = nc.dram_tensor("QpTd", [NG, 128, HCN * 256], BF16, kind="Internal").ap()
    CTd = nc.dram_tensor("CTd", [2, 128, HCN * 1024], BF16, kind="Internal").ap()
    HDNd = nc.dram_tensor("HDNd", [4, 128, FCN * 512], BF16, kind="Internal").ap()
    WDB = nc.dram_tensor("WDB", [16, 128, INTER], BF16, kind="Internal").ap()
    yd = nc.dram_tensor("yd", [OWN, D], F32, kind="Internal").ap()

    if dbg:
        dbgL = nc.dram_tensor("dbgL", [256, D], F32, kind="ExternalOutput").ap()
        dbgLT = nc.dram_tensor("dbgLT", [128, 512], F32, kind="ExternalOutput").ap()
        dbgQpT = nc.dram_tensor("dbgQpT", [128, HCN * 256], F32, kind="ExternalOutput").ap()
        dbgCT = nc.dram_tensor("dbgCT", [128, 2048], F32, kind="ExternalOutput").ap()
        dbgY = nc.dram_tensor("dbgY", [256, D], F32, kind="ExternalOutput").ap()
        dbgHDN = nc.dram_tensor("dbgHDN", [128, 2048], F32, kind="ExternalOutput").ap()

    Ld_v = Ld.rearrange("(k p) d -> k p d", p=128)
    x_v = x_d.rearrange("(k p) d -> k p d", p=128)
    yd_v = yd.rearrange("(k p) d -> k p d", p=128)
    out_v = out_d.rearrange("(k p) d -> k p d", p=128)

    es = contextlib.ExitStack()
    with es:
        E = es.enter_context
        BIG0 = E(nc.sbuf_tensor("BIG0", [128, 32768], BF16))
        BIG1 = E(nc.sbuf_tensor("BIG1", [128, 16384], BF16))
        STRM = E(nc.sbuf_tensor("STRM", [128, 16384], BF16))
        WCOL = E(nc.sbuf_tensor("WCOL", [128, 8192], BF16))
        QPT = E(nc.sbuf_tensor("QPT", [128, 4096], BF16))
        FT = E(nc.sbuf_tensor("FT", [128, 4096], BF16))
        SMALL = E(nc.sbuf_tensor("SMALL", [128, 6144], F32))
        SMALLB = E(nc.sbuf_tensor("SMALLB", [128, 4096], BF16))
        LTST = E(nc.sbuf_tensor("LTST", [128, 4096], BF16))
        STATS = E(nc.sbuf_tensor("STATS", [128, 640], F32))
        CONST = E(nc.sbuf_tensor("CONST", [128, 256], F32))
        IDT = E(nc.sbuf_tensor("IDT", [128, 128], BF16))
        pb = [E(nc.psum_tensor("pb%d" % i, [128, 512], F32)) for i in range(8)]

        sem_names = [
            "dCONST", "dXB", "vP0", "aP0", "tTR", "vLT", "aLT", "dLst", "dLTst",
            "dW1", "tM1", "aE1", "tM2", "aE2", "dQP", "dWB",
            "dQG", "dLTA", "tS", "aS", "vKM", "vFM", "aEX", "vPS",
            "tTP", "vPT", "dLC", "tCk", "vCT", "dCT",
            "dCTl", "dW3", "tM3", "aE3", "tM4", "aE4", "tTR3", "vTR3",
            "vP3", "aP3", "dXo", "dY",
            "dW4", "tG", "tU", "aG", "aU", "vH", "dH",
            "dWd", "dHl", "tD", "aD", "tTF", "vTF", "vP4", "aP4", "dYl", "dOUT",
        ]
        sm = {n: E(nc.semaphore(n)) for n in sem_names}
        block = E(nc.Block())

        bb = [pb[i][:].bitcast(BF16) for i in range(8)]
        BIG0q = BIG0[:].rearrange("p (q j) -> p q j", q=2)
        BIG0_16 = BIG0[:].rearrange("p (c i) -> p c i", c=16)
        LTST_16 = LTST[:].rearrange("p (c j) -> p c j", c=16)

        def st3(base, p):
            o = base + p * 30
            return dict(bn=STATS[:, o:o + 24], mv=STATS[:, o + 24:o + 26],
                        t1=STATS[:, o + 26:o + 27], t2=STATS[:, o + 27:o + 28],
                        rstd=STATS[:, o + 28:o + 29], t3=STATS[:, o + 29:o + 30])

        def st0(sp):
            b = sp * 128
            mv = STATS[:, b + 96:b + 104].rearrange("p (t two) -> p t two", two=2)
            return dict(bn=STATS[:, b:b + 96],
                        mv=mv,
                        t1=STATS[:, b + 104:b + 108], t2=STATS[:, b + 108:b + 112],
                        rstd=STATS[:, b + 112:b + 116], t3=STATS[:, b + 116:b + 120])

        PH0EVAC = [None]

        def _mk_ph0_evac():
            def _ph0_evac_tile(eng, sem, half, t):
                eng.wait_ge(sm["tTR"], 16 * t + 8 * half + 8)
                srcv = bb[half][:, 0:1024].rearrange("p (a b) -> p a b", a=8)
                if t < OWNT:
                    dest = BIG0_16[:, half * 8:(half + 1) * 8, t * 128:t * 128 + 128]
                else:
                    if t >= 18:
                        eng.wait_ge(sm["dLTst"], 16 * (8 + (t - 18) // 2 + 1))
                    dest = LTST_16[:, half * 8:(half + 1) * 8, (t % 2) * 128:(t % 2) * 128 + 128]
                if hasattr(eng, "tensor_copy"):
                    eng.tensor_copy(dest, srcv).then_inc(sem, 1)
                else:
                    eng.copy(dest, srcv).then_inc(sem, 1)
            PH0EVAC.append(_ph0_evac_tile)

            def _ph0_evac_slab(eng, sem, half, sv):
                for i4 in range(4):
                    t = 4 * sv + i4
                    eng.wait_ge(sm["tTR"], 16 * t + 8 * half + 8)
                    srcv = bb[half][:, 0:1024].rearrange("p (a b) -> p a b", a=8)
                    if t < OWNT:
                        dest = BIG0_16[:, half * 8:(half + 1) * 8, t * 128:t * 128 + 128]
                    else:
                        if t >= 18:
                            eng.wait_ge(sm["dLTst"], 16 * (8 + (t - 18) // 2 + 1))
                        dest = LTST_16[:, half * 8:(half + 1) * 8, (t % 2) * 128:(t % 2) * 128 + 128]
                    if hasattr(eng, "tensor_copy"):
                        eng.tensor_copy(dest, srcv).then_inc(sem, 1)
                    else:
                        eng.copy(dest, srcv).then_inc(sem, 1)
            PH0EVAC[0] = _ph0_evac_slab
        _mk_ph0_evac()

        @block.sync
        def _(q):
            q.dma_start(out=IDT[:], in_=id_d).then_inc(sm["dCONST"], 16)
            q.dma_start(out=CONST[:, C_BQ:C_BQ + 16], in_=bq_d).then_inc(sm["dCONST"], 16)
            q.dma_start(out=CONST[:, C_BV:C_BV + 16], in_=bv_d).then_inc(sm["dCONST"], 16)
            q.dma_start(out=CONST[:, C_BO:C_BO + 16], in_=bo_d).then_inc(sm["dCONST"], 16)
            q.dma_start(out=CONST[:, C_DS:C_DS + 16], in_=ds_d).then_inc(sm["dCONST"], 16)
            q.dma_start(out=CONST[:, C_GS:C_GS + 44], in_=gs_d).then_inc(sm["dCONST"], 16)
            q.dma_start(out=CONST[:, C_US:C_US + 44], in_=us_d).then_inc(sm["dCONST"], 16)
            for tp in range(0, NT, 2):
                q.wait_ge(sm["vLT"], tp + 2)
                q.wait_ge(sm["aLT"], tp + 2)
                if tp < OWNT:
                    src = BIG0_16[:, :, tp * 128:tp * 128 + 256]
                else:
                    src = LTST_16
                q.dma_start(out=LTd[:, :, tp * 128:tp * 128 + 256].rearrange("c p s -> p c s"),
                            in_=src).then_inc(sm["dLTst"], 16)
            for ih in range(2):
                for hcp in range(16):
                    for it in range(2):
                        g2 = ih * 32 + hcp * 2 + it
                        r = g2 % 4
                        q.wait_ge(sm["aE2"], g2 + 1)
                        for half in range(2):
                            gabs = ih * 4 + it * 2 + half
                            q.dma_start(out=QpTd[gabs][:, hcp * 256:(hcp + 1) * 256],
                                        in_=FT[:, r * 512 + half * 256: r * 512 + (half + 1) * 256]
                                        ).then_inc(sm["dQP"], 16)
            for g in range(NG):
                if g == 0:
                    q.wait_ge(sm["dQP"], 16 * 128)
                else:
                    q.wait_ge(sm["tS"], g * 64)
                q.dma_start(out=QPT[:], in_=QpTd[g]).then_inc(sm["dQG"], 16)
                for kt in range(NKT):
                    slot = kt % 2
                    if kt >= 2:
                        q.wait_ge(sm["tS"], g * 64 + 2 * kt - 2)
                    elif g > 0:
                        q.wait_ge(sm["tCk"], g * 32)
                    for hq in range(4):
                        q.dma_start(
                            out=STRM[:, slot * 8192 + hq * 4 * 512: slot * 8192 + (hq + 1) * 4 * 512]
                                .rearrange("p (c j) -> p c j", c=4),
                            in_=LTd[hq * 4:(hq + 1) * 4, :, kt * 512:(kt + 1) * 512]
                                .rearrange("c p s -> p c s"),
                        ).then_inc(sm["dLTA"], 16)
            for ih in range(2):
                q.wait_ge(sm["dCT"], 1024 * (ih + 1))
                if ih == 0:
                    q.wait_ge(sm["vPT"], 512)
                else:
                    q.wait_ge(sm["tM3"], 32)
                q.dma_start(out=BIG1[:], in_=CTd[ih]).then_inc(sm["dCTl"], 16)
                for tt in range(8):
                    T = ih * 8 + tt
                    p = T % 2
                    if T >= 2:
                        q.wait_ge(sm["aP3"], 4 * (T - 2) + 4)
                        q.wait_ge(sm["dY"], 16 * (T - 1))
                    q.dma_start(out=SMALL[:, p * 2048:(p + 1) * 2048], in_=x_v[T]).then_inc(sm["dXo"], 16)
                    q.wait_ge(sm["vP3"], 11 * T + 6)
                    q.dma_start(out=yd_v[T], in_=SMALL[:, p * 2048:(p + 1) * 2048]).then_inc(sm["dY"], 16)
            for fc in range(FCN):
                for it in range(4):
                    idx = fc * 4 + it
                    q.wait_ge(sm["vH"], idx + 1)
                    q.dma_start(out=HDNd[it][:, fc * 512:(fc + 1) * 512],
                                in_=LTST[:, (idx % 4) * 512:(idx % 4 + 1) * 512]).then_inc(sm["dH"], 16)
            for it2 in range(4):
                if it2 == 0:
                    q.wait_ge(sm["dH"], 16 * FCN * 4)
                    q.wait_ge(sm["tG"], FCN * 4)
                    q.wait_ge(sm["tU"], FCN * 4)
                    q.wait_ge(sm["vTR3"], 128)
                else:
                    q.wait_ge(sm["tD"], 16 * it2)
                for ch in range(4):
                    f0 = ch * 11
                    q.dma_start(out=BIG0[:, f0 * 512:(f0 + 11) * 512],
                                in_=HDNd[it2][:, f0 * 512:(f0 + 11) * 512]).then_inc(sm["dHl"], 16)
                for tt in range(4):
                    To = it2 * 4 + tt
                    p = To % 2
                    if To == 0:
                        q.wait_ge(sm["dY"], 16 * 16)
                        q.wait_ge(sm["aP3"], 64)
                    if To >= 2:
                        q.wait_ge(sm["dOUT"], 16 * (To - 1))
                    q.dma_start(out=SMALL[:, p * 2048:(p + 1) * 2048], in_=yd_v[To]).then_inc(sm["dYl"], 16)
                    q.wait_ge(sm["vP4"], 6 * To + 6)
                    q.dma_start(out=out_v[To], in_=SMALL[:, p * 2048:(p + 1) * 2048]).then_inc(sm["dOUT"], 16)

        @block.gpsimd
        def _(g_):
            g_.dma_start(out=WCOL[:, 0:2048], in_=wq_p[0]).then_inc(sm["dW1"], 16)
            g_.dma_start(out=WCOL[:, 2048:4096], in_=wq_p[1]).then_inc(sm["dW1"], 16)
            def _slab_load(s):
                sp = s % 2
                if s >= 2:
                    g_.wait_ge(sm["aP0"], 5 * (s - 2) + 5)
                g_.dma_start(out=BIG1[:, sp * 8192:(sp + 1) * 8192]
                             .rearrange("p (c d) -> p c d", c=4),
                             in_=x_v[4 * s:4 * s + 4].rearrange("c p d -> p c d")
                             ).then_inc(sm["dXB"], 16)
            _slab_load(0)
            _slab_load(1)
            for s in range(NSL):
                for i in range(4):
                    t = 4 * s + i
                    g_.wait_ge(sm["aP0"], 5 * s + i + 2)
                    g_.dma_start(out=Ld_v[t], in_=SMALLB[:, (t % 2) * 2048:(t % 2 + 1) * 2048]
                                 ).then_inc(sm["dLst"], 16)
                if s + 2 < NSL:
                    _slab_load(s + 2)
            for ih in range(2):
                for oc in range(16):
                    if ih == 0 and oc < 2:
                        continue
                    w = oc % 2
                    if ih == 1 and oc < 2:
                        g_.wait_ge(sm["tM2"], 32)
                    if oc >= 2:
                        g_.wait_ge(sm["tM1"], ih * 32 + 2 * oc - 2)
                    g_.dma_start(out=WCOL[:, w * 2048:(w + 1) * 2048], in_=wq_p[oc]).then_inc(sm["dW1"], 16)
                for hcp in range(16):
                    w = hcp % 2
                    g_.wait_ge(sm["tM1"], (ih + 1) * 32)
                    if hcp >= 2:
                        g_.wait_ge(sm["tM2"], ih * 32 + 2 * (hcp - 1))
                    g_.dma_start(out=WCOL[:, w * 2048:(w + 1) * 2048], in_=wk_p[hcp]).then_inc(sm["dW1"], 16)
            for oc in range(16):
                g_.dma_start(out=WDB[oc], in_=wd_p[oc]).then_inc(sm["dWB"], 16)
            for g in range(NG):
                for kt in range(NKT):
                    slot = kt % 2
                    if kt >= 2:
                        g_.wait_ge(sm["tCk"], g * 32 + kt - 1)
                    else:
                        g_.wait_ge(sm["tS"], g * 64 + 62 + 2 * kt)
                    g_.dma_start(out=STRM[:, slot * 8192:(slot + 1) * 8192]
                                 .rearrange("p (c j) -> p c j", c=4),
                                 in_=Ld_v[kt * 4:(kt + 1) * 4, :, :]
                                 .rearrange("c p s -> p c s")).then_inc(sm["dLC"], 16)
                for hc in range(16):
                    g_.wait_ge(sm["vCT"], g * 16 + hc + 1)
                    g_.dma_start(out=CTd[g // 4][:, hc * 1024 + (g % 4) * 256: hc * 1024 + (g % 4) * 256 + 256],
                                 in_=FT[:, hc * 256:(hc + 1) * 256]).then_inc(sm["dCT"], 16)
            for ih in range(2):
                for oc in range(16):
                    w = oc % 2
                    if ih == 0 and oc < 2:
                        g_.wait_ge(sm["tM2"], 64)
                    if ih == 1 and oc < 2:
                        g_.wait_ge(sm["tM4"], 32)
                    if oc >= 2:
                        g_.wait_ge(sm["tM3"], ih * 32 + 2 * oc - 2)
                    g_.dma_start(out=WCOL[:, w * 2048:(w + 1) * 2048], in_=wv_p[oc]).then_inc(sm["dW3"], 16)
                for ocp in range(16):
                    w = ocp % 2
                    g_.wait_ge(sm["tM3"], (ih + 1) * 32)
                    if ocp >= 2:
                        g_.wait_ge(sm["tM4"], ih * 32 + 2 * (ocp - 1))
                    g_.dma_start(out=WCOL[:, w * 2048:(w + 1) * 2048], in_=wo_p[ocp]).then_inc(sm["dW3"], 16)
            for fc in range(FCN):
                wslot = fc % 2
                if fc < 2:
                    g_.wait_ge(sm["tM4"], 64)
                else:
                    g_.wait_ge(sm["tG"], 4 * (fc - 1))
                g_.dma_start(out=WCOL[:, wslot * 2048:(wslot + 1) * 2048], in_=wg_p[fc]).then_inc(sm["dW4"], 16)
                if fc >= 2:
                    g_.wait_ge(sm["tU"], 4 * (fc - 1))
                g_.dma_start(out=WCOL[:, 4096 + wslot * 2048: 4096 + (wslot + 1) * 2048],
                             in_=wu_p[fc]).then_inc(sm["dW4"], 16)
            for it2 in range(4):
                for oc in range(16):
                    jdx = it2 * 16 + oc
                    slot = oc % 2
                    if jdx < 2:
                        g_.wait_ge(sm["dWB"], 256)
                        g_.wait_ge(sm["tG"], FCN * 4)
                        g_.wait_ge(sm["tU"], FCN * 4)
                    else:
                        g_.wait_ge(sm["tD"], jdx - 1)
                    g_.dma_start(out=STRM[:, slot * 8192: slot * 8192 + INTER],
                                 in_=WDB[oc]).then_inc(sm["dWd"], 16)
            if dbg:
                g_.wait_ge(sm["dLst"], 16 * NT)
                g_.dma_start(out=dbgL, in_=Ld[0:256, :]).then_inc(sm["dOUT"], 16)
                g_.wait_ge(sm["dLTst"], 16 * 64)
                g_.dma_start(out=dbgLT, in_=LTd[2, 0:128, 0:512]).then_inc(sm["dOUT"], 16)
                g_.wait_ge(sm["dQP"], 16 * 128)
                g_.dma_start(out=dbgQpT, in_=QpTd[1]).then_inc(sm["dOUT"], 16)
                g_.wait_ge(sm["dCT"], 2048)
                g_.dma_start(out=dbgCT, in_=CTd[0][:, 0:2048]).then_inc(sm["dOUT"], 16)
                g_.wait_ge(sm["dY"], 256)
                g_.dma_start(out=dbgY, in_=yd[0:256, :]).then_inc(sm["dOUT"], 16)
                g_.wait_ge(sm["dH"], 16 * FCN * 4)
                g_.dma_start(out=dbgHDN, in_=HDNd[0][:, 0:2048]).then_inc(sm["dOUT"], 16)
                g_.wait_ge(sm["dOUT"], 16 * 16 + 16 * 6)
            else:
                g_.wait_ge(sm["dOUT"], 16 * 16)

        @block.vector
        def _(v):
            _ph0_evac_slab = PH0EVAC[0]
            for s in range(NSL):
                sp = s % 2
                sl = st0(sp)
                xs = BIG1[:, sp * 8192:(sp + 1) * 8192]
                v.wait_ge(sm["dXB"], 16 * (s + 1))
                if s >= 2:
                    v.wait_ge(sm["aP0"], 5 * (s - 2) + 5)
                for i4 in range(4):
                    for c in range(4):
                        i = v.bn_stats(sl["bn"][:, (i4 * 4 + c) * 6:(i4 * 4 + c + 1) * 6],
                                       xs[:, i4 * 2048 + c * 512: i4 * 2048 + (c + 1) * 512])
                i.then_inc(sm["vP0"], 1)
                if s >= 1:
                    _ph0_evac_slab(v, sm["vLT"], 0, s - 1)
                v.wait_ge(sm["vP0"], 5 * s + 1)
                for i4 in range(4):
                    i = v.bn_aggr(sl["mv"][:, i4], sl["bn"][:, i4 * 24:(i4 + 1) * 24])
                i.then_inc(sm["vP0"], 1)
                v.wait_ge(sm["vP0"], 5 * s + 2)
                v.tensor_scalar_add(sl["t1"], sl["mv"][:, :, 1], EPS).then_inc(sm["vP0"], 1)
                v.wait_ge(sm["aP0"], 5 * s + 1)
                v.reciprocal(sl["rstd"], sl["t2"]).then_inc(sm["vP0"], 1)
                v.wait_ge(sm["vP0"], 5 * s + 4)
                v.scalar_tensor_tensor(sl["t3"], sl["mv"][:, :, 0], -1.0, sl["rstd"],
                                       ALU.mult, ALU.mult).then_inc(sm["vP0"], 1)
            _ph0_evac_slab(v, sm["vLT"], 0, NSL - 1)
            for g in range(NG):
                for kt in range(NKT):
                    for qi in range(2):
                        idx = g * 64 + kt * 2 + qi
                        v.wait_ge(sm["aS"], idx + 1)
                        v.reduce_max(STATS[:, ST_MX + qi * 32 + kt: ST_MX + qi * 32 + kt + 1],
                                     BIG0[:, qi * S + kt * 512: qi * S + (kt + 1) * 512],
                                     axis=AX.X).then_inc(sm["vKM"], 1)
                for qi in range(2):
                    b = g * 2 + qi
                    v.wait_ge(sm["vKM"], g * 64 + 63 + qi)
                    v.reduce_max(STATS[:, ST_NEGM + qi:ST_NEGM + qi + 1],
                                 STATS[:, ST_MX + qi * 32: ST_MX + qi * 32 + 32],
                                 axis=AX.X, negate=True).then_inc(sm["vFM"], 1)
                    v.wait_ge(sm["aEX"], 4 * b + 4)
                    v.reduce_sum(STATS[:, ST_SES + qi:ST_SES + qi + 1],
                                 STATS[:, ST_SE4 + qi * 4: ST_SE4 + qi * 4 + 4],
                                 axis=AX.X).then_inc(sm["vPS"], 1)
                    v.wait_ge(sm["vPS"], 6 * b + 1)
                    v.reciprocal(STATS[:, ST_RS + qi:ST_RS + qi + 1],
                                 STATS[:, ST_SES + qi:ST_SES + qi + 1]).then_inc(sm["vPS"], 1)
                    v.wait_ge(sm["vPS"], 6 * b + 2)
                    for cch in range(4):
                        v.tensor_single_scalar(BIG1[:, cch * 4096:(cch + 1) * 4096],
                                               BIG1[:, cch * 4096:(cch + 1) * 4096],
                                               STATS[:, ST_RS + qi:ST_RS + qi + 1],
                                               ALU.mult).then_inc(sm["vPS"], 1)
                    for jg in range(32):
                        gpt = b * 32 + jg
                        v.wait_ge(sm["tTP"], gpt * 4 + 4)
                        v.tensor_copy(BIG0[:, qi * S + jg * 512: qi * S + (jg + 1) * 512],
                                      bb[gpt % 2][:, 0:512]).then_inc(sm["vPT"], 1)
                for hc in range(16):
                    v.wait_ge(sm["tCk"], g * 32 + 32)
                    if hc == 0:
                        if g == 0:
                            v.wait_ge(sm["dQP"], 16 * 128)
                        else:
                            v.wait_ge(sm["dCT"], 256 * g)
                    v.tensor_copy(FT[:, hc * 256:(hc + 1) * 256],
                                  pb[hc // 2][:, (hc % 2) * 256:(hc % 2) * 256 + 256]
                                  ).then_inc(sm["vCT"], 1)
            for ih in range(2):
                for tt in range(8):
                    T = ih * 8 + tt
                    p = T % 2
                    for kg in range(4):
                        uix = T * 8 + kg
                        if uix == 0:
                            v.wait_ge(sm["dLst"], 16 * NT)
                        v.wait_ge(sm["tTR3"], (T * 32) + 4 * kg + 4)
                        v.tensor_copy(SMALLB[:, p * 2048 + kg * 512: p * 2048 + (kg + 1) * 512],
                                      bb[uix % 2][:, 0:512]).then_inc(sm["vTR3"], 1)
                    slA = st3(ST_P3A, p)
                    v.wait_ge(sm["vTR3"], T * 8 + 4)
                    ats = SMALLB[:, p * 2048:(p + 1) * 2048]
                    for c in range(4):
                        i = v.bn_stats(slA["bn"][:, c * 6:(c + 1) * 6], ats[:, c * 512:(c + 1) * 512])
                    i.then_inc(sm["vP3"], 1)
                    v.wait_ge(sm["vP3"], 11 * T + 1)
                    v.bn_aggr(slA["mv"], slA["bn"]).then_inc(sm["vP3"], 1)
                    v.wait_ge(sm["vP3"], 11 * T + 2)
                    v.tensor_scalar_add(slA["t1"], slA["mv"][:, 1:2], EPS).then_inc(sm["vP3"], 1)
                    v.wait_ge(sm["aP3"], 4 * T + 1)
                    v.reciprocal(slA["rstd"], slA["t2"]).then_inc(sm["vP3"], 1)
                    v.wait_ge(sm["vP3"], 11 * T + 4)
                    v.scalar_tensor_tensor(slA["t3"], slA["mv"][:, 0:1], -1.0, slA["rstd"],
                                           ALU.mult, ALU.mult).then_inc(sm["vP3"], 1)
                    xs3 = SMALL[:, p * 2048:(p + 1) * 2048]
                    v.wait_ge(sm["aP3"], 4 * T + 2)
                    v.wait_ge(sm["dXo"], 16 * (T + 1))
                    v.tensor_tensor(xs3, xs3, SMALL[:, 4096:6144], ALU.add).then_inc(sm["vP3"], 1)
                    slB = st3(ST_P3B, p)
                    v.wait_ge(sm["vP3"], 11 * T + 6)
                    for c in range(4):
                        i = v.bn_stats(slB["bn"][:, c * 6:(c + 1) * 6], xs3[:, c * 512:(c + 1) * 512])
                    i.then_inc(sm["vP3"], 1)
                    v.wait_ge(sm["vP3"], 11 * T + 7)
                    v.bn_aggr(slB["mv"], slB["bn"]).then_inc(sm["vP3"], 1)
                    v.wait_ge(sm["vP3"], 11 * T + 8)
                    v.tensor_scalar_add(slB["t1"], slB["mv"][:, 1:2], EPS).then_inc(sm["vP3"], 1)
                    v.wait_ge(sm["aP3"], 4 * T + 3)
                    v.reciprocal(slB["rstd"], slB["t2"]).then_inc(sm["vP3"], 1)
                    v.wait_ge(sm["vP3"], 11 * T + 10)
                    v.scalar_tensor_tensor(slB["t3"], slB["mv"][:, 0:1], -1.0, slB["rstd"],
                                           ALU.mult, ALU.mult).then_inc(sm["vP3"], 1)
                    for kg in range(4):
                        uix = T * 8 + 4 + kg
                        v.wait_ge(sm["tTR3"], (T * 32) + 16 + 4 * kg + 4)
                        if ih == 0:
                            dest = STRM[:].rearrange("p (c i) -> p c i", c=16)[
                                :, kg * 4:(kg + 1) * 4, T * 128:T * 128 + 128]
                        else:
                            dest = BIG1[:].rearrange("p (c i) -> p c i", c=16)[
                                :, kg * 4:(kg + 1) * 4, (T - 8) * 128:(T - 8) * 128 + 128]
                        v.tensor_copy(dest, bb[uix % 2][:, 0:512].rearrange("p (a b) -> p a b", a=4)
                                      ).then_inc(sm["vTR3"], 1)
            for fc in range(FCN):
                for it in range(4):
                    idx = fc * 4 + it
                    v.wait_ge(sm["aG"], idx + 1)
                    v.wait_ge(sm["aU"], idx + 1)
                    if idx == 0:
                        v.wait_ge(sm["dLTst"], 16 * 64)
                    if idx >= 4:
                        v.wait_ge(sm["dH"], 16 * (idx - 3))
                    v.tensor_mul(LTST[:, (idx % 4) * 512:(idx % 4 + 1) * 512],
                                 SMALLB[:, (idx % 2) * 512:(idx % 2 + 1) * 512],
                                 SMALLB[:, 1024 + (idx % 2) * 512: 1024 + (idx % 2 + 1) * 512]
                                 ).then_inc(sm["vH"], 1)
            for it2 in range(4):
                for tt in range(4):
                    To = it2 * 4 + tt
                    p = To % 2
                    for og in range(4):
                        fgi = To * 4 + og
                        v.wait_ge(sm["tTF"], fgi * 4 + 4)
                        if og == 0:
                            if To == 0:
                                v.wait_ge(sm["aG"], FCN * 4)
                                v.wait_ge(sm["aU"], FCN * 4)
                            if To >= 2:
                                v.wait_ge(sm["aP4"], 2 * To - 2)
                        v.tensor_copy(SMALLB[:, p * 2048 + og * 512: p * 2048 + (og + 1) * 512],
                                      bb[fgi % 2][:, 0:512]).then_inc(sm["vTF"], 1)
                    sl4 = st3(ST_P4, p)
                    v.wait_ge(sm["vTF"], To * 4 + 4)
                    fts = SMALLB[:, p * 2048:(p + 1) * 2048]
                    for c in range(4):
                        i = v.bn_stats(sl4["bn"][:, c * 6:(c + 1) * 6], fts[:, c * 512:(c + 1) * 512])
                    i.then_inc(sm["vP4"], 1)
                    v.wait_ge(sm["vP4"], 6 * To + 1)
                    v.bn_aggr(sl4["mv"], sl4["bn"]).then_inc(sm["vP4"], 1)
                    v.wait_ge(sm["vP4"], 6 * To + 2)
                    v.tensor_scalar_add(sl4["t1"], sl4["mv"][:, 1:2], EPS).then_inc(sm["vP4"], 1)
                    v.wait_ge(sm["aP4"], 2 * To + 1)
                    v.reciprocal(sl4["rstd"], sl4["t2"]).then_inc(sm["vP4"], 1)
                    v.wait_ge(sm["vP4"], 6 * To + 4)
                    v.scalar_tensor_tensor(sl4["t3"], sl4["mv"][:, 0:1], -1.0, sl4["rstd"],
                                           ALU.mult, ALU.mult).then_inc(sm["vP4"], 1)
                    xs4 = SMALL[:, p * 2048:(p + 1) * 2048]
                    v.wait_ge(sm["aP4"], 2 * To + 2)
                    v.wait_ge(sm["dYl"], 16 * (To + 1))
                    v.tensor_tensor(xs4, xs4, SMALL[:, 4096:6144], ALU.add).then_inc(sm["vP4"], 1)

        @block.scalar
        def _(s):
            _ph0_evac_tile = PH0EVAC[1]
            for sab in range(NSL):
                sp = sab % 2
                sl = st0(sp)
                s.wait_ge(sm["vP0"], 5 * sab + 3)
                s.activation(sl["t2"], sl["t1"], AF.Sqrt).then_inc(sm["aP0"], 1)
                for i4 in range(4):
                    t = 4 * sab + i4
                    if t >= 2:
                        _ph0_evac_tile(s, sm["aLT"], 1, t - 2)
                    s.wait_ge(sm["vP0"], 5 * sab + 5)
                    if t >= 2:
                        s.wait_ge(sm["tTR"], 16 * (t - 1))
                        s.wait_ge(sm["dLst"], 16 * (t - 1))
                    s.activation(SMALLB[:, (t % 2) * 2048:(t % 2 + 1) * 2048],
                                 BIG1[:, sp * 8192 + i4 * 2048: sp * 8192 + (i4 + 1) * 2048],
                                 AF.Identity, bias=sl["t3"][:, i4:i4 + 1],
                                 scale=sl["rstd"][:, i4:i4 + 1]).then_inc(sm["aP0"], 1)
            _ph0_evac_tile(s, sm["aLT"], 1, NT - 2)
            _ph0_evac_tile(s, sm["aLT"], 1, NT - 1)
            s.wait_ge(sm["dCONST"], 16 * 7)
            s.wait_ge(sm["vP0"], 5 * NSL)
            for ih in range(2):
                for oc in range(16):
                    for it in range(2):
                        g1 = ih * 32 + oc * 2 + it
                        s.wait_ge(sm["tM1"], g1 + 1)
                        if ih == 1 and g1 == 32:
                            s.wait_ge(sm["tM2"], 32)
                        s.activation(BIG1[:, oc * 1024 + it * 512: oc * 1024 + (it + 1) * 512],
                                     pb[4 + g1 % 4][:],
                                     AF.Identity, bias=CONST[:, C_BQ + oc:C_BQ + oc + 1]).then_inc(sm["aE1"], 1)
                for hcp in range(16):
                    for it in range(2):
                        g2 = ih * 32 + hcp * 2 + it
                        r = g2 % 4
                        s.wait_ge(sm["tM2"], g2 + 1)
                        if g2 >= 4:
                            s.wait_ge(sm["dQP"], 16 * (2 * g2 - 6))
                        s.activation(FT[:, r * 512:(r + 1) * 512], pb[4 + g2 % 4][:],
                                     AF.Identity, scale=RSCALE).then_inc(sm["aE2"], 1)
            for g in range(NG):
                for kt in range(NKT):
                    for qi in range(2):
                        idx = g * 64 + kt * 2 + qi
                        s.wait_ge(sm["tS"], idx + 1)
                        if g > 0 and kt == 0:
                            s.wait_ge(sm["tCk"], g * 32)
                        s.activation(BIG0[:, qi * S + kt * 512: qi * S + (kt + 1) * 512],
                                     pb[(kt * 2 + qi) % 2][:], AF.Copy).then_inc(sm["aS"], 1)
                for qi in range(2):
                    b = g * 2 + qi
                    for cch in range(4):
                        if cch == 0:
                            s.wait_ge(sm["vFM"], b + 1)
                        if b >= 1:
                            s.wait_ge(sm["vPT"], (b - 1) * 32 + 8 * (cch + 1))
                        s.activation(BIG1[:, cch * 4096:(cch + 1) * 4096],
                                     BIG0[:, qi * S + cch * 4096: qi * S + (cch + 1) * 4096],
                                     AF.Exp,
                                     bias=STATS[:, ST_NEGM + qi:ST_NEGM + qi + 1], scale=1.0,
                                     accum_out=STATS[:, ST_SE4 + qi * 4 + cch: ST_SE4 + qi * 4 + cch + 1]
                                     ).then_inc(sm["aEX"], 1)
            for ih in range(2):
                for oc in range(16):
                    for it in range(2):
                        g3 = ih * 32 + oc * 2 + it
                        s.wait_ge(sm["tM3"], g3 + 1)
                        s.activation(BIG0[:, oc * 1024 + it * 512: oc * 1024 + (it + 1) * 512],
                                     pb[4 + g3 % 4][:], AF.Identity,
                                     bias=CONST[:, C_BV + oc:C_BV + oc + 1]).then_inc(sm["aE3"], 1)
                for ocp in range(16):
                    for it in range(2):
                        g4 = ih * 32 + ocp * 2 + it
                        s.wait_ge(sm["tM4"], g4 + 1)
                        s.activation(BIG0[:, S + ocp * 1024 + it * 512: S + ocp * 1024 + (it + 1) * 512],
                                     pb[4 + g4 % 4][:], AF.Identity,
                                     bias=CONST[:, C_BO + ocp:C_BO + ocp + 1]).then_inc(sm["aE4"], 1)
                for tt in range(8):
                    T = ih * 8 + tt
                    p = T % 2
                    slA = st3(ST_P3A, p)
                    slB = st3(ST_P3B, p)
                    s.wait_ge(sm["vP3"], 11 * T + 3)
                    s.activation(slA["t2"], slA["t1"], AF.Sqrt).then_inc(sm["aP3"], 1)
                    s.wait_ge(sm["vP3"], 11 * T + 5)
                    s.activation(SMALL[:, 4096:6144], SMALLB[:, p * 2048:(p + 1) * 2048],
                                 AF.Identity, bias=slA["t3"], scale=slA["rstd"]).then_inc(sm["aP3"], 1)
                    s.wait_ge(sm["vP3"], 11 * T + 9)
                    s.activation(slB["t2"], slB["t1"], AF.Sqrt).then_inc(sm["aP3"], 1)
                    s.wait_ge(sm["vP3"], 11 * T + 11)
                    s.activation(SMALLB[:, p * 2048:(p + 1) * 2048], SMALL[:, p * 2048:(p + 1) * 2048],
                                 AF.Identity, bias=slB["t3"], scale=slB["rstd"]).then_inc(sm["aP3"], 1)
            for fc in range(FCN):
                for it in range(4):
                    idx = fc * 4 + it
                    s.wait_ge(sm["tG"], idx + 1)
                    if idx == 0:
                        s.wait_ge(sm["vTR3"], 128)
                    if idx >= 2:
                        s.wait_ge(sm["vH"], idx - 1)
                    s.activation(SMALLB[:, (idx % 2) * 512:(idx % 2 + 1) * 512],
                                 pb[4 + idx % 2][:], AF.Silu,
                                 scale=CONST[:, C_GS + fc:C_GS + fc + 1]).then_inc(sm["aG"], 1)
                    s.wait_ge(sm["tU"], idx + 1)
                    s.activation(SMALLB[:, 1024 + (idx % 2) * 512: 1024 + (idx % 2 + 1) * 512],
                                 pb[6 + idx % 2][:], AF.Identity,
                                 scale=CONST[:, C_US + fc:C_US + fc + 1]).then_inc(sm["aU"], 1)
            for it2 in range(4):
                for oc in range(16):
                    jdx = it2 * 16 + oc
                    s.wait_ge(sm["tD"], jdx + 1)
                    if jdx == 0:
                        s.wait_ge(sm["vTR3"], 128)
                    if oc == 0 and it2 >= 1:
                        s.wait_ge(sm["tTF"], it2 * 64)
                    s.activation(BIG0[:, 24576 + oc * 512: 24576 + (oc + 1) * 512],
                                 pb[4 + jdx % 2][:], AF.Identity,
                                 scale=CONST[:, C_DS + oc:C_DS + oc + 1]).then_inc(sm["aD"], 1)
                for tt in range(4):
                    To = it2 * 4 + tt
                    p = To % 2
                    sl4 = st3(ST_P4, p)
                    s.wait_ge(sm["vP4"], 6 * To + 3)
                    s.activation(sl4["t2"], sl4["t1"], AF.Sqrt).then_inc(sm["aP4"], 1)
                    s.wait_ge(sm["vP4"], 6 * To + 5)
                    s.activation(SMALL[:, 4096:6144], SMALLB[:, p * 2048:(p + 1) * 2048],
                                 AF.Identity, bias=sl4["t3"], scale=sl4["rstd"]).then_inc(sm["aP4"], 1)

        @block.tensor
        def _(t_):
            t_.wait_ge(sm["dCONST"], 16)
            for t in range(NT):
                t_.wait_ge(sm["aP0"], 5 * (t // 4) + (t % 4) + 2)
                for half in range(2):
                    if t >= 1:
                        t_.wait_ge(sm["vLT"] if half == 0 else sm["aLT"], t)
                    for k in range(8):
                        j = half * 8 + k
                        t_.transpose(bb[half][:, k * 128:(k + 1) * 128],
                                     SMALLB[:, (t % 2) * 2048 + j * 128: (t % 2) * 2048 + (j + 1) * 128],
                                     IDT[:]).then_inc(sm["tTR"], 1)
            t_.wait_ge(sm["vLT"], 16)
            t_.wait_ge(sm["aLT"], 16)
            for ih in range(2):
                for oc in range(16):
                    for it in range(2):
                        g1 = ih * 32 + oc * 2 + it
                        t_.wait_ge(sm["dW1"], 16 * (ih * 32 + oc + 1))
                        if g1 >= 4:
                            t_.wait_ge(sm["aE1"], g1 - 3)
                        w = oc % 2
                        for hc in range(16):
                            i = t_.matmul(pb[4 + g1 % 4][:],
                                          WCOL[:, w * 2048 + hc * 128: w * 2048 + (hc + 1) * 128],
                                          BIG0[:, hc * 2048 + ih * 1024 + it * 512:
                                               hc * 2048 + ih * 1024 + (it + 1) * 512],
                                          start=(hc == 0), stop=(hc == 15))
                        i.then_inc(sm["tM1"], 1)
                for hcp in range(16):
                    for it in range(2):
                        g2 = ih * 32 + hcp * 2 + it
                        t_.wait_ge(sm["dW1"], 16 * (ih * 32 + 16 + hcp + 1))
                        if g2 < ih * 32 + 4:
                            t_.wait_ge(sm["aE1"], (ih + 1) * 32)
                        if g2 >= 4:
                            t_.wait_ge(sm["aE2"], g2 - 3)
                        w = hcp % 2
                        for oc in range(16):
                            i = t_.matmul(pb[4 + g2 % 4][:],
                                          WCOL[:, w * 2048 + oc * 128: w * 2048 + (oc + 1) * 128],
                                          BIG1[:, oc * 1024 + it * 512: oc * 1024 + (it + 1) * 512],
                                          start=(oc == 0), stop=(oc == 15))
                        i.then_inc(sm["tM2"], 1)
            for g in range(NG):
                t_.wait_ge(sm["dQG"], 16 * (g + 1))
                for kt in range(NKT):
                    slot = kt % 2
                    t_.wait_ge(sm["dLTA"], 64 * (g * 32 + kt + 1))
                    for qi in range(2):
                        idx = g * 64 + kt * 2 + qi
                        if g == 0 and idx < 2:
                            t_.wait_ge(sm["vLT"], 128)
                            t_.wait_ge(sm["aLT"], 128)
                        if g > 0 and kt == 0:
                            t_.wait_ge(sm["vCT"], (g - 1) * 16 + 4)
                        if idx - g * 64 >= 2:
                            t_.wait_ge(sm["aS"], idx - 1)
                        for hc in range(16):
                            i = t_.matmul(pb[(kt * 2 + qi) % 2][:],
                                          QPT[:, hc * 256 + qi * 128: hc * 256 + qi * 128 + 128],
                                          STRM[:, slot * 8192 + hc * 512: slot * 8192 + (hc + 1) * 512],
                                          start=(hc == 0), stop=(hc == 15))
                        i.then_inc(sm["tS"], 1)
                for qi in range(2):
                    b = g * 2 + qi
                    for jg in range(32):
                        gpt = b * 32 + jg
                        t_.wait_ge(sm["vPS"], 6 * b + 3 + jg // 8)
                        if qi == 0 and jg < 2:
                            t_.wait_ge(sm["aS"], g * 64 + 64)
                        if gpt >= 2:
                            t_.wait_ge(sm["vPT"], gpt - 1)
                        for k in range(4):
                            j = jg * 4 + k
                            t_.transpose(bb[gpt % 2][:, k * 128:(k + 1) * 128],
                                         BIG1[:, j * 128:(j + 1) * 128], IDT[:]).then_inc(sm["tTP"], 1)
                for kt in range(NKT):
                    slot = kt % 2
                    t_.wait_ge(sm["dLC"], 16 * (g * 32 + kt + 1))
                    if kt == 0:
                        t_.wait_ge(sm["vPT"], g * 64 + 64)
                        if g > 0:
                            t_.wait_ge(sm["vCT"], g * 16)
                    for jb in range(4):
                        rhs = BIG0q[:, :, (kt * 4 + jb) * 128:(kt * 4 + jb + 1) * 128]
                        for hc in range(16):
                            i = t_.matmul(
                                pb[hc // 2][:, (hc % 2) * 256:(hc % 2) * 256 + 256]
                                .rearrange("p (a b) -> p a b", a=2),
                                STRM[:, slot * 8192 + jb * 2048 + hc * 128:
                                     slot * 8192 + jb * 2048 + (hc + 1) * 128],
                                rhs,
                                start=(kt == 0 and jb == 0 and hc % 2 == 0),
                                stop=(kt == NKT - 1 and jb == 3 and hc % 2 == 1),
                                skip_group_check=True)
                    i.then_inc(sm["tCk"], 1)
            for ih in range(2):
                t_.wait_ge(sm["dCTl"], 16 * (ih + 1))
                for oc in range(16):
                    for it in range(2):
                        g3 = ih * 32 + oc * 2 + it
                        t_.wait_ge(sm["dW3"], 16 * (ih * 32 + oc + 1))
                        if g3 == 0:
                            t_.wait_ge(sm["vCT"], 128)
                        if g3 >= 4:
                            t_.wait_ge(sm["aE3"], g3 - 3)
                        w = oc % 2
                        for hc in range(16):
                            i = t_.matmul(pb[4 + g3 % 4][:],
                                          WCOL[:, w * 2048 + hc * 128: w * 2048 + (hc + 1) * 128],
                                          BIG1[:, hc * 1024 + it * 512: hc * 1024 + (it + 1) * 512],
                                          start=(hc == 0), stop=(hc == 15))
                        i.then_inc(sm["tM3"], 1)
                for ocp in range(16):
                    for it in range(2):
                        g4 = ih * 32 + ocp * 2 + it
                        t_.wait_ge(sm["dW3"], 16 * (ih * 32 + 16 + ocp + 1))
                        if g4 < ih * 32 + 4:
                            t_.wait_ge(sm["aE3"], ih * 32 + 32)
                        if g4 >= 4:
                            t_.wait_ge(sm["aE4"], g4 - 3)
                        w = ocp % 2
                        for oc in range(16):
                            i = t_.matmul(pb[4 + g4 % 4][:],
                                          WCOL[:, w * 2048 + oc * 128: w * 2048 + (oc + 1) * 128],
                                          BIG0[:, oc * 1024 + it * 512: oc * 1024 + (it + 1) * 512],
                                          start=(oc == 0), stop=(oc == 15))
                        i.then_inc(sm["tM4"], 1)
                for tt in range(8):
                    T = ih * 8 + tt
                    p = T % 2
                    for kg in range(4):
                        uix = T * 8 + kg
                        if uix < 2:
                            t_.wait_ge(sm["vCT"], 128)
                        if kg == 0:
                            t_.wait_ge(sm["aE4"], ih * 32 + 32)
                        if uix >= 2:
                            t_.wait_ge(sm["vTR3"], uix - 1)
                        for k4 in range(4):
                            k = kg * 4 + k4
                            t_.transpose(bb[uix % 2][:, k4 * 128:(k4 + 1) * 128],
                                         BIG0[:, S + k * 1024 + tt * 128: S + k * 1024 + tt * 128 + 128],
                                         IDT[:]).then_inc(sm["tTR3"], 1)
                    for kg in range(4):
                        uix = T * 8 + 4 + kg
                        if kg == 0:
                            t_.wait_ge(sm["aP3"], 4 * T + 4)
                        if uix >= 2:
                            t_.wait_ge(sm["vTR3"], uix - 1)
                        for k4 in range(4):
                            k = kg * 4 + k4
                            t_.transpose(bb[uix % 2][:, k4 * 128:(k4 + 1) * 128],
                                         SMALLB[:, p * 2048 + k * 128: p * 2048 + (k + 1) * 128],
                                         IDT[:]).then_inc(sm["tTR3"], 1)
            for fc in range(FCN):
                for it in range(4):
                    idx = fc * 4 + it
                    t_.wait_ge(sm["dW4"], 32 * fc + 16)
                    if idx == 0:
                        t_.wait_ge(sm["aE3"], 64)
                        t_.wait_ge(sm["vTR3"], 128)
                    if idx >= 2:
                        t_.wait_ge(sm["aG"], idx - 1)
                    wslot = fc % 2
                    rhsbuf = STRM if it < 2 else BIG1
                    itl = it if it < 2 else it - 2
                    for hc in range(16):
                        i = t_.matmul(pb[4 + idx % 2][:],
                                      WCOL[:, wslot * 2048 + hc * 128: wslot * 2048 + (hc + 1) * 128],
                                      rhsbuf[:, hc * 1024 + itl * 512: hc * 1024 + (itl + 1) * 512],
                                      start=(hc == 0), stop=(hc == 15))
                    i.then_inc(sm["tG"], 1)
                    t_.wait_ge(sm["dW4"], 32 * fc + 32)
                    if idx == 0:
                        t_.wait_ge(sm["aE4"], 64)
                    if idx >= 2:
                        t_.wait_ge(sm["aU"], idx - 1)
                    for hc in range(16):
                        i = t_.matmul(pb[6 + idx % 2][:],
                                      WCOL[:, 4096 + wslot * 2048 + hc * 128:
                                           4096 + wslot * 2048 + (hc + 1) * 128],
                                      rhsbuf[:, hc * 1024 + itl * 512: hc * 1024 + (itl + 1) * 512],
                                      start=(hc == 0), stop=(hc == 15))
                    i.then_inc(sm["tU"], 1)
            for it2 in range(4):
                for oc in range(16):
                    jdx = it2 * 16 + oc
                    t_.wait_ge(sm["dWd"], 16 * (jdx + 1))
                    if jdx == 0:
                        t_.wait_ge(sm["aG"], FCN * 4)
                        t_.wait_ge(sm["aU"], FCN * 4)
                    if jdx >= 2:
                        t_.wait_ge(sm["aD"], jdx - 1)
                    slot = oc % 2
                    for fc in range(FCN):
                        if oc == 0 and fc % 11 == 0:
                            t_.wait_ge(sm["dHl"], 16 * (it2 * 4 + fc // 11 + 1))
                        i = t_.matmul(pb[4 + jdx % 2][:],
                                      STRM[:, slot * 8192 + fc * 128: slot * 8192 + (fc + 1) * 128],
                                      BIG0[:, fc * 512:(fc + 1) * 512],
                                      start=(fc == 0), stop=(fc == FCN - 1))
                    i.then_inc(sm["tD"], 1)
                t_.wait_ge(sm["aD"], it2 * 16 + 16)
                for tt in range(4):
                    To = it2 * 4 + tt
                    for og in range(4):
                        fgi = To * 4 + og
                        if fgi < 2:
                            t_.wait_ge(sm["vTR3"], 128)
                        if fgi >= 2:
                            t_.wait_ge(sm["vTF"], fgi - 1)
                        for k4 in range(4):
                            oc = og * 4 + k4
                            t_.transpose(bb[fgi % 2][:, k4 * 128:(k4 + 1) * 128],
                                         BIG0[:, 24576 + oc * 512 + tt * 128:
                                              24576 + oc * 512 + tt * 128 + 128],
                                         IDT[:]).then_inc(sm["tTF"], 1)

    return nc


def _pack(Wp):
    K, M = Wp.shape
    R, C = K // 128, M // 128
    return np.ascontiguousarray(
        Wp.reshape(R, 128, C, 128).transpose(2, 1, 0, 3).reshape(C, 128, R * 128))


def _prep_shared(i):
    wqkv = np.asarray(i["L0_attn_w_qkv"], np.float32)
    m = {
        "wq_p": _pack(np.ascontiguousarray(wqkv[0:2048].T)),
        "wk_p": _pack(np.ascontiguousarray(wqkv[2048:4096])),
        "wv_p": _pack(np.ascontiguousarray(wqkv[4096:6144].T)),
        "wo_p": _pack(np.ascontiguousarray(np.asarray(i["L0_attn_w_out"], np.float32).T)),
        "wg_p": _pack(np.ascontiguousarray(np.asarray(i["L0_ffn_gate_w"], np.float32).T)),
        "wu_p": _pack(np.ascontiguousarray(np.asarray(i["L0_ffn_up_w"], np.float32).T)),
        "wd_p": _pack(np.ascontiguousarray(np.asarray(i["L0_ffn_down_w"], np.float32).T)),
        "bq": np.ascontiguousarray(np.asarray(i["L0_attn_b_qkv"], np.float32)[0:2048].reshape(16, 128).T),
        "bv": np.ascontiguousarray(np.asarray(i["L0_attn_b_qkv"], np.float32)[4096:6144].reshape(16, 128).T),
        "bo": np.ascontiguousarray(np.asarray(i["L0_attn_b_out"], np.float32).reshape(16, 128).T),
        "ds": np.ascontiguousarray(np.asarray(i["L0_ffn_down_s"], np.float32).reshape(16, 128).T),
        "gs": np.ascontiguousarray(np.asarray(i["L0_ffn_gate_s"], np.float32).reshape(44, 128).T),
        "us": np.ascontiguousarray(np.asarray(i["L0_ffn_up_s"], np.float32).reshape(44, 128).T),
        "ident": np.eye(128, dtype=ml_dtypes.bfloat16),
    }
    return m


def kernel(**inputs):
    x = np.asarray(inputs["L0_attn_hidden_states"], np.float32).reshape(S, D)
    shared = _prep_shared(inputs)
    nc = build(dbg=False)
    in_maps = []
    for c in range(8):
        m = dict(shared)
        m["x"] = np.ascontiguousarray(np.roll(x, -2048 * c, axis=0))
        in_maps.append(m)
    res = run_bass_kernel_spmd(nc, in_maps, core_ids=list(range(8)))
    out = np.concatenate([np.asarray(res.results[c]["out"]) for c in range(8)], axis=0)
    return out.reshape(1, S, D).astype(np.float32)


# version 04
# speedup vs baseline: 1.2223x; 1.0538x over previous
import contextlib
import numpy as np
import ml_dtypes
import concourse.bass as bass
import concourse.mybir as mybir
from concourse.bass_utils import run_bass_kernel_spmd

F32 = mybir.dt.float32
BF16 = mybir.dt.bfloat16
AF = mybir.ActivationFunctionType
ALU = mybir.AluOpType
AX = mybir.AxisListType

S = 16384
D = 2048
HCN = 16
NT = 128
NSL = 32
OWNT = 16
OWN = 2048
KT = 512
NKT = 32
NG = 8
INTER = 5632
FCN = 44
EPS = 1e-5
RSCALE = float(np.float32(1.0) / np.float32(np.sqrt(np.float32(2048.0))))

C_BQ = 0
C_BV = 16
C_BO = 32
C_DS = 48
C_GS = 64
C_US = 108

ST_MX = 256
ST_NEGM = 320
ST_SE4 = 324
ST_SES = 332
ST_RS = 334
ST_P3A = 336
ST_P3B = 396
ST_P4 = 456


def build(dbg=False):
    nc = bass.Bass("TRN2", target_bir_lowering=False)

    x_d = nc.dram_tensor("x", [S, D], F32, kind="ExternalInput").ap()
    wq_p = nc.dram_tensor("wq_p", [16, 128, D], F32, kind="ExternalInput").ap()
    wk_p = nc.dram_tensor("wk_p", [16, 128, D], F32, kind="ExternalInput").ap()
    wv_p = nc.dram_tensor("wv_p", [16, 128, D], F32, kind="ExternalInput").ap()
    wo_p = nc.dram_tensor("wo_p", [16, 128, D], F32, kind="ExternalInput").ap()
    wg_p = nc.dram_tensor("wg_p", [44, 128, D], F32, kind="ExternalInput").ap()
    wu_p = nc.dram_tensor("wu_p", [44, 128, D], F32, kind="ExternalInput").ap()
    wd_p = nc.dram_tensor("wd_p", [16, 128, INTER], F32, kind="ExternalInput").ap()
    bq_d = nc.dram_tensor("bq", [128, 16], F32, kind="ExternalInput").ap()
    bv_d = nc.dram_tensor("bv", [128, 16], F32, kind="ExternalInput").ap()
    bo_d = nc.dram_tensor("bo", [128, 16], F32, kind="ExternalInput").ap()
    ds_d = nc.dram_tensor("ds", [128, 16], F32, kind="ExternalInput").ap()
    gs_d = nc.dram_tensor("gs", [128, 44], F32, kind="ExternalInput").ap()
    us_d = nc.dram_tensor("us", [128, 44], F32, kind="ExternalInput").ap()
    id_d = nc.dram_tensor("ident", [128, 128], BF16, kind="ExternalInput").ap()
    out_d = nc.dram_tensor("out", [OWN, D], F32, kind="ExternalOutput").ap()

    Ld = nc.dram_tensor("Ld", [S, D], BF16, kind="Internal").ap()
    LTd = nc.dram_tensor("LTd", [16, 128, S], BF16, kind="Internal").ap()
    QpTd = nc.dram_tensor("QpTd", [NG, 128, HCN * 256], BF16, kind="Internal").ap()
    CTd = nc.dram_tensor("CTd", [2, 128, HCN * 1024], BF16, kind="Internal").ap()
    HDNd = nc.dram_tensor("HDNd", [4, 128, FCN * 512], BF16, kind="Internal").ap()
    WDB = nc.dram_tensor("WDB", [16, 128, INTER], BF16, kind="Internal").ap()
    yd = nc.dram_tensor("yd", [OWN, D], F32, kind="Internal").ap()

    if dbg:
        dbgL = nc.dram_tensor("dbgL", [256, D], F32, kind="ExternalOutput").ap()
        dbgLT = nc.dram_tensor("dbgLT", [128, 512], F32, kind="ExternalOutput").ap()
        dbgQpT = nc.dram_tensor("dbgQpT", [128, HCN * 256], F32, kind="ExternalOutput").ap()
        dbgCT = nc.dram_tensor("dbgCT", [128, 2048], F32, kind="ExternalOutput").ap()
        dbgY = nc.dram_tensor("dbgY", [256, D], F32, kind="ExternalOutput").ap()
        dbgHDN = nc.dram_tensor("dbgHDN", [128, 2048], F32, kind="ExternalOutput").ap()

    Ld_v = Ld.rearrange("(k p) d -> k p d", p=128)
    x_v = x_d.rearrange("(k p) d -> k p d", p=128)
    yd_v = yd.rearrange("(k p) d -> k p d", p=128)
    out_v = out_d.rearrange("(k p) d -> k p d", p=128)

    es = contextlib.ExitStack()
    with es:
        E = es.enter_context
        BIG0 = E(nc.sbuf_tensor("BIG0", [128, 32768], BF16))
        BIG1 = E(nc.sbuf_tensor("BIG1", [128, 16384], BF16))
        STRM = E(nc.sbuf_tensor("STRM", [128, 16384], BF16))
        WCOL = E(nc.sbuf_tensor("WCOL", [128, 8192], BF16))
        QPT = E(nc.sbuf_tensor("QPT", [128, 4096], BF16))
        FT = E(nc.sbuf_tensor("FT", [128, 4096], BF16))
        SMALL = E(nc.sbuf_tensor("SMALL", [128, 6144], F32))
        SMALLB = E(nc.sbuf_tensor("SMALLB", [128, 4096], BF16))
        LTST = E(nc.sbuf_tensor("LTST", [128, 4096], BF16))
        STATS = E(nc.sbuf_tensor("STATS", [128, 640], F32))
        CONST = E(nc.sbuf_tensor("CONST", [128, 256], F32))
        IDT = E(nc.sbuf_tensor("IDT", [128, 128], BF16))
        pb = [E(nc.psum_tensor("pb%d" % i, [128, 512], F32)) for i in range(8)]

        sem_names = [
            "dCONST", "dXB", "vP0", "aP0", "tTR", "vLT", "aLT", "dLst", "dLTst",
            "dW1", "tM1", "aE1", "tM2", "aE2", "dQP", "dWB",
            "dQG", "dLTA", "tS", "aS", "vKM", "vFM", "aEX", "vPS",
            "tTP", "vPT", "dLC", "tCk", "vCT", "dCT",
            "dCTl", "dW3", "tM3", "aE3", "tM4", "aE4", "tTR3", "vTR3",
            "vP3", "aP3", "dXo", "dY",
            "dW4", "tG", "tU", "aG", "aU", "vH", "dH",
            "dWd", "dHl", "tD", "aD", "tTF", "vTF", "vP4", "aP4", "dYl", "dOUT",
        ]
        sm = {n: E(nc.semaphore(n)) for n in sem_names}
        block = E(nc.Block())

        bb = [pb[i][:].bitcast(BF16) for i in range(8)]
        BIG0q = BIG0[:].rearrange("p (q j) -> p q j", q=2)
        BIG0_16 = BIG0[:].rearrange("p (c i) -> p c i", c=16)
        LTST_16 = LTST[:].rearrange("p (c j) -> p c j", c=16)

        def st3(base, p):
            o = base + p * 30
            return dict(bn=STATS[:, o:o + 24], mv=STATS[:, o + 24:o + 26],
                        t1=STATS[:, o + 26:o + 27], t2=STATS[:, o + 27:o + 28],
                        rstd=STATS[:, o + 28:o + 29], t3=STATS[:, o + 29:o + 30])

        def st0(sp):
            b = sp * 128
            mv = STATS[:, b + 96:b + 104].rearrange("p (t two) -> p t two", two=2)
            return dict(bn=STATS[:, b:b + 96],
                        mv=mv,
                        t1=STATS[:, b + 104:b + 108], t2=STATS[:, b + 108:b + 112],
                        rstd=STATS[:, b + 112:b + 116], t3=STATS[:, b + 116:b + 120])

        PH0EVAC = [None]

        def _mk_ph0_evac():
            def _ph0_evac_tile(eng, sem, half, t):
                eng.wait_ge(sm["tTR"], 16 * t + 8 * half + 8)
                srcv = bb[half][:, 0:1024].rearrange("p (a b) -> p a b", a=8)
                if t < OWNT:
                    dest = BIG0_16[:, half * 8:(half + 1) * 8, t * 128:t * 128 + 128]
                else:
                    if t >= 18:
                        eng.wait_ge(sm["dLTst"], 16 * (8 + (t - 18) // 2 + 1))
                    dest = LTST_16[:, half * 8:(half + 1) * 8, (t % 2) * 128:(t % 2) * 128 + 128]
                if hasattr(eng, "tensor_copy"):
                    eng.tensor_copy(dest, srcv).then_inc(sem, 1)
                else:
                    eng.copy(dest, srcv).then_inc(sem, 1)
            PH0EVAC.append(_ph0_evac_tile)

            def _ph0_evac_slab(eng, sem, half, sv):
                for i4 in range(4):
                    t = 4 * sv + i4
                    eng.wait_ge(sm["tTR"], 16 * t + 8 * half + 8)
                    srcv = bb[half][:, 0:1024].rearrange("p (a b) -> p a b", a=8)
                    if t < OWNT:
                        dest = BIG0_16[:, half * 8:(half + 1) * 8, t * 128:t * 128 + 128]
                    else:
                        if t >= 18:
                            eng.wait_ge(sm["dLTst"], 16 * (8 + (t - 18) // 2 + 1))
                        dest = LTST_16[:, half * 8:(half + 1) * 8, (t % 2) * 128:(t % 2) * 128 + 128]
                    if hasattr(eng, "tensor_copy"):
                        eng.tensor_copy(dest, srcv).then_inc(sem, 1)
                    else:
                        eng.copy(dest, srcv).then_inc(sem, 1)
            PH0EVAC[0] = _ph0_evac_slab
        _mk_ph0_evac()

        @block.sync
        def _(q):
            q.dma_start(out=IDT[:], in_=id_d).then_inc(sm["dCONST"], 16)
            q.dma_start(out=CONST[:, C_BQ:C_BQ + 16], in_=bq_d).then_inc(sm["dCONST"], 16)
            q.dma_start(out=CONST[:, C_BV:C_BV + 16], in_=bv_d).then_inc(sm["dCONST"], 16)
            q.dma_start(out=CONST[:, C_BO:C_BO + 16], in_=bo_d).then_inc(sm["dCONST"], 16)
            q.dma_start(out=CONST[:, C_DS:C_DS + 16], in_=ds_d).then_inc(sm["dCONST"], 16)
            q.dma_start(out=CONST[:, C_GS:C_GS + 44], in_=gs_d).then_inc(sm["dCONST"], 16)
            q.dma_start(out=CONST[:, C_US:C_US + 44], in_=us_d).then_inc(sm["dCONST"], 16)
            for tp in range(0, NT, 2):
                q.wait_ge(sm["vLT"], tp + 2)
                q.wait_ge(sm["aLT"], tp + 2)
                if tp < OWNT:
                    src = BIG0_16[:, :, tp * 128:tp * 128 + 256]
                else:
                    src = LTST_16
                q.dma_start(out=LTd[:, :, tp * 128:tp * 128 + 256].rearrange("c p s -> p c s"),
                            in_=src).then_inc(sm["dLTst"], 16)
            for g in range(NG):
                if g == 0:
                    q.wait_ge(sm["dQP"], 16 * 128)
                else:
                    q.wait_ge(sm["tS"], g * 64)
                q.dma_start(out=QPT[:], in_=QpTd[g]).then_inc(sm["dQG"], 16)
                for kt in range(NKT):
                    slot = kt % 2
                    if kt >= 2:
                        q.wait_ge(sm["tS"], g * 64 + 2 * kt - 2)
                    elif g > 0:
                        q.wait_ge(sm["tCk"], g * 32)
                    else:
                        q.wait_ge(sm["aP0"], 5 * NSL)
                    for hq in range(4):
                        q.dma_start(
                            out=STRM[:, slot * 8192 + hq * 4 * 512: slot * 8192 + (hq + 1) * 4 * 512]
                                .rearrange("p (c j) -> p c j", c=4),
                            in_=LTd[hq * 4:(hq + 1) * 4, :, kt * 512:(kt + 1) * 512]
                                .rearrange("c p s -> p c s"),
                        ).then_inc(sm["dLTA"], 16)
            for ih in range(2):
                q.wait_ge(sm["dCT"], 1024 * (ih + 1))
                if ih == 0:
                    q.wait_ge(sm["vPT"], 512)
                else:
                    q.wait_ge(sm["tM3"], 32)
                q.dma_start(out=BIG1[:], in_=CTd[ih]).then_inc(sm["dCTl"], 16)
                for tt in range(8):
                    T = ih * 8 + tt
                    p = T % 2
                    if T >= 2:
                        q.wait_ge(sm["aP3"], 4 * (T - 2) + 4)
                        q.wait_ge(sm["dY"], 16 * (T - 1))
                    q.dma_start(out=SMALL[:, p * 2048:(p + 1) * 2048], in_=x_v[T]).then_inc(sm["dXo"], 16)
                    q.wait_ge(sm["vP3"], 11 * T + 6)
                    q.dma_start(out=yd_v[T], in_=SMALL[:, p * 2048:(p + 1) * 2048]).then_inc(sm["dY"], 16)
            for fc in range(FCN):
                for it in range(4):
                    idx = fc * 4 + it
                    q.wait_ge(sm["vH"], idx + 1)
                    q.dma_start(out=HDNd[it][:, fc * 512:(fc + 1) * 512],
                                in_=LTST[:, (idx % 4) * 512:(idx % 4 + 1) * 512]).then_inc(sm["dH"], 16)
            for it2 in range(4):
                if it2 == 0:
                    q.wait_ge(sm["dH"], 16 * FCN * 4)
                    q.wait_ge(sm["tG"], FCN * 4)
                    q.wait_ge(sm["tU"], FCN * 4)
                    q.wait_ge(sm["vTR3"], 128)
                else:
                    q.wait_ge(sm["tD"], 16 * it2)
                for ch in range(4):
                    f0 = ch * 11
                    q.dma_start(out=BIG0[:, f0 * 512:(f0 + 11) * 512],
                                in_=HDNd[it2][:, f0 * 512:(f0 + 11) * 512]).then_inc(sm["dHl"], 16)
                for tt in range(4):
                    To = it2 * 4 + tt
                    p = To % 2
                    if To == 0:
                        q.wait_ge(sm["dY"], 16 * 16)
                        q.wait_ge(sm["aP3"], 64)
                    if To >= 2:
                        q.wait_ge(sm["dOUT"], 16 * (To - 1))
                    q.dma_start(out=SMALL[:, p * 2048:(p + 1) * 2048], in_=yd_v[To]).then_inc(sm["dYl"], 16)
                    q.wait_ge(sm["vP4"], 6 * To + 6)
                    q.dma_start(out=out_v[To], in_=SMALL[:, p * 2048:(p + 1) * 2048]).then_inc(sm["dOUT"], 16)

        @block.gpsimd
        def _(g_):
            g_.dma_start(out=WCOL[:, 0:2048], in_=wq_p[0]).then_inc(sm["dW1"], 16)
            g_.dma_start(out=WCOL[:, 2048:4096], in_=wq_p[1]).then_inc(sm["dW1"], 16)

            def _w1_load(k):
                ih, r = k // 32, k % 32
                if r < 16:
                    oc = r
                    if ih == 0 and oc < 2:
                        return
                    w = oc % 2
                    if ih == 1 and oc < 2:
                        g_.wait_ge(sm["tM2"], 32)
                    if oc >= 2:
                        g_.wait_ge(sm["tM1"], ih * 32 + 2 * oc - 2)
                    g_.dma_start(out=WCOL[:, w * 2048:(w + 1) * 2048], in_=wq_p[oc]).then_inc(sm["dW1"], 16)
                else:
                    hcp = r - 16
                    w = hcp % 2
                    g_.wait_ge(sm["tM1"], (ih + 1) * 32)
                    if hcp >= 2:
                        g_.wait_ge(sm["tM2"], ih * 32 + 2 * (hcp - 1))
                    g_.dma_start(out=WCOL[:, w * 2048:(w + 1) * 2048], in_=wk_p[hcp]).then_inc(sm["dW1"], 16)
            def _slab_load(s):
                sp = s % 2
                if s >= 2:
                    g_.wait_ge(sm["aP0"], 5 * (s - 2) + 5)
                g_.dma_start(out=STRM[:, sp * 8192:(sp + 1) * 8192]
                             .rearrange("p (c d) -> p c d", c=4),
                             in_=x_v[4 * s:4 * s + 4].rearrange("c p d -> p c d")
                             ).then_inc(sm["dXB"], 16)
            _slab_load(0)
            _slab_load(1)
            K_W = [0]
            for s in range(4):
                for i in range(4):
                    t = 4 * s + i
                    g_.wait_ge(sm["aP0"], 5 * s + i + 2)
                    g_.dma_start(out=Ld_v[t], in_=SMALLB[:, (t % 2) * 2048:(t % 2 + 1) * 2048]
                                 ).then_inc(sm["dLst"], 16)
                _slab_load(s + 2)
            while K_W[0] < 64:
                _w1_load(K_W[0])
                K_W[0] += 1
            for s in range(4, NSL):
                for i in range(4):
                    t = 4 * s + i
                    g_.wait_ge(sm["aP0"], 5 * s + i + 2)
                    g_.dma_start(out=Ld_v[t], in_=SMALLB[:, (t % 2) * 2048:(t % 2 + 1) * 2048]
                                 ).then_inc(sm["dLst"], 16)
                if s + 2 < NSL:
                    _slab_load(s + 2)
            for oc in range(16):
                g_.dma_start(out=WDB[oc], in_=wd_p[oc]).then_inc(sm["dWB"], 16)
            for g in range(NG):
                for kt in range(NKT):
                    slot = kt % 2
                    if kt >= 2:
                        g_.wait_ge(sm["tCk"], g * 32 + kt - 1)
                    else:
                        g_.wait_ge(sm["tS"], g * 64 + 62 + 2 * kt)
                    g_.dma_start(out=STRM[:, slot * 8192:(slot + 1) * 8192]
                                 .rearrange("p (c j) -> p c j", c=4),
                                 in_=Ld_v[kt * 4:(kt + 1) * 4, :, :]
                                 .rearrange("c p s -> p c s")).then_inc(sm["dLC"], 16)
                for hc in range(16):
                    g_.wait_ge(sm["vCT"], g * 16 + hc + 1)
                    g_.dma_start(out=CTd[g // 4][:, hc * 1024 + (g % 4) * 256: hc * 1024 + (g % 4) * 256 + 256],
                                 in_=FT[:, hc * 256:(hc + 1) * 256]).then_inc(sm["dCT"], 16)
            for ih in range(2):
                for oc in range(16):
                    w = oc % 2
                    if ih == 0 and oc < 2:
                        g_.wait_ge(sm["tM2"], 64)
                    if ih == 1 and oc < 2:
                        g_.wait_ge(sm["tM4"], 32)
                    if oc >= 2:
                        g_.wait_ge(sm["tM3"], ih * 32 + 2 * oc - 2)
                    g_.dma_start(out=WCOL[:, w * 2048:(w + 1) * 2048], in_=wv_p[oc]).then_inc(sm["dW3"], 16)
                for ocp in range(16):
                    w = ocp % 2
                    g_.wait_ge(sm["tM3"], (ih + 1) * 32)
                    if ocp >= 2:
                        g_.wait_ge(sm["tM4"], ih * 32 + 2 * (ocp - 1))
                    g_.dma_start(out=WCOL[:, w * 2048:(w + 1) * 2048], in_=wo_p[ocp]).then_inc(sm["dW3"], 16)
            for fc in range(FCN):
                wslot = fc % 2
                if fc < 2:
                    g_.wait_ge(sm["tM4"], 64)
                else:
                    g_.wait_ge(sm["tG"], 4 * (fc - 1))
                g_.dma_start(out=WCOL[:, wslot * 2048:(wslot + 1) * 2048], in_=wg_p[fc]).then_inc(sm["dW4"], 16)
                if fc >= 2:
                    g_.wait_ge(sm["tU"], 4 * (fc - 1))
                g_.dma_start(out=WCOL[:, 4096 + wslot * 2048: 4096 + (wslot + 1) * 2048],
                             in_=wu_p[fc]).then_inc(sm["dW4"], 16)
            for it2 in range(4):
                for oc in range(16):
                    jdx = it2 * 16 + oc
                    slot = oc % 2
                    if jdx < 2:
                        g_.wait_ge(sm["dWB"], 256)
                        g_.wait_ge(sm["tG"], FCN * 4)
                        g_.wait_ge(sm["tU"], FCN * 4)
                    else:
                        g_.wait_ge(sm["tD"], jdx - 1)
                    g_.dma_start(out=STRM[:, slot * 8192: slot * 8192 + INTER],
                                 in_=WDB[oc]).then_inc(sm["dWd"], 16)
            if dbg:
                g_.wait_ge(sm["dLst"], 16 * NT)
                g_.dma_start(out=dbgL, in_=Ld[0:256, :]).then_inc(sm["dOUT"], 16)
                g_.wait_ge(sm["dLTst"], 16 * 64)
                g_.dma_start(out=dbgLT, in_=LTd[2, 0:128, 0:512]).then_inc(sm["dOUT"], 16)
                g_.wait_ge(sm["dQP"], 16 * 128)
                g_.dma_start(out=dbgQpT, in_=QpTd[1]).then_inc(sm["dOUT"], 16)
                g_.wait_ge(sm["dCT"], 2048)
                g_.dma_start(out=dbgCT, in_=CTd[0][:, 0:2048]).then_inc(sm["dOUT"], 16)
                g_.wait_ge(sm["dY"], 256)
                g_.dma_start(out=dbgY, in_=yd[0:256, :]).then_inc(sm["dOUT"], 16)
                g_.wait_ge(sm["dH"], 16 * FCN * 4)
                g_.dma_start(out=dbgHDN, in_=HDNd[0][:, 0:2048]).then_inc(sm["dOUT"], 16)
                g_.wait_ge(sm["dOUT"], 16 * 16 + 16 * 6)
            else:
                g_.wait_ge(sm["dOUT"], 16 * 16)

        @block.vector
        def _(v):
            _ph0_evac_slab = PH0EVAC[0]
            for s in range(NSL):
                sp = s % 2
                sl = st0(sp)
                xs = STRM[:, sp * 8192:(sp + 1) * 8192]
                v.wait_ge(sm["dXB"], 16 * (s + 1))
                if s >= 2:
                    v.wait_ge(sm["aP0"], 5 * (s - 2) + 5)
                for i4 in range(4):
                    for c in range(4):
                        i = v.bn_stats(sl["bn"][:, (i4 * 4 + c) * 6:(i4 * 4 + c + 1) * 6],
                                       xs[:, i4 * 2048 + c * 512: i4 * 2048 + (c + 1) * 512])
                i.then_inc(sm["vP0"], 1)
                if s >= 1:
                    _ph0_evac_slab(v, sm["vLT"], 0, s - 1)
                v.wait_ge(sm["vP0"], 5 * s + 1)
                for i4 in range(4):
                    i = v.bn_aggr(sl["mv"][:, i4], sl["bn"][:, i4 * 24:(i4 + 1) * 24])
                i.then_inc(sm["vP0"], 1)
                v.wait_ge(sm["vP0"], 5 * s + 2)
                v.tensor_scalar_add(sl["t1"], sl["mv"][:, :, 1], EPS).then_inc(sm["vP0"], 1)
                v.wait_ge(sm["aP0"], 5 * s + 1)
                v.reciprocal(sl["rstd"], sl["t2"]).then_inc(sm["vP0"], 1)
                v.wait_ge(sm["vP0"], 5 * s + 4)
                v.scalar_tensor_tensor(sl["t3"], sl["mv"][:, :, 0], -1.0, sl["rstd"],
                                       ALU.mult, ALU.mult).then_inc(sm["vP0"], 1)
            _ph0_evac_slab(v, sm["vLT"], 0, NSL - 1)
            for g in range(NG):
                for kt in range(NKT):
                    for qi in range(2):
                        idx = g * 64 + kt * 2 + qi
                        v.wait_ge(sm["aS"], idx + 1)
                        v.reduce_max(STATS[:, ST_MX + qi * 32 + kt: ST_MX + qi * 32 + kt + 1],
                                     BIG0[:, qi * S + kt * 512: qi * S + (kt + 1) * 512],
                                     axis=AX.X).then_inc(sm["vKM"], 1)
                for qi in range(2):
                    b = g * 2 + qi
                    v.wait_ge(sm["vKM"], g * 64 + 63 + qi)
                    v.reduce_max(STATS[:, ST_NEGM + qi:ST_NEGM + qi + 1],
                                 STATS[:, ST_MX + qi * 32: ST_MX + qi * 32 + 32],
                                 axis=AX.X, negate=True).then_inc(sm["vFM"], 1)
                    v.wait_ge(sm["aEX"], 4 * b + 4)
                    v.reduce_sum(STATS[:, ST_SES + qi:ST_SES + qi + 1],
                                 STATS[:, ST_SE4 + qi * 4: ST_SE4 + qi * 4 + 4],
                                 axis=AX.X).then_inc(sm["vPS"], 1)
                    v.wait_ge(sm["vPS"], 6 * b + 1)
                    v.reciprocal(STATS[:, ST_RS + qi:ST_RS + qi + 1],
                                 STATS[:, ST_SES + qi:ST_SES + qi + 1]).then_inc(sm["vPS"], 1)
                    v.wait_ge(sm["vPS"], 6 * b + 2)
                    for cch in range(4):
                        v.tensor_single_scalar(BIG1[:, cch * 4096:(cch + 1) * 4096],
                                               BIG1[:, cch * 4096:(cch + 1) * 4096],
                                               STATS[:, ST_RS + qi:ST_RS + qi + 1],
                                               ALU.mult).then_inc(sm["vPS"], 1)
                    for jg in range(32):
                        gpt = b * 32 + jg
                        v.wait_ge(sm["tTP"], gpt * 4 + 4)
                        v.tensor_copy(BIG0[:, qi * S + jg * 512: qi * S + (jg + 1) * 512],
                                      bb[gpt % 2][:, 0:512]).then_inc(sm["vPT"], 1)
                for hc in range(16):
                    v.wait_ge(sm["tCk"], g * 32 + 32)
                    if hc == 0:
                        if g == 0:
                            v.wait_ge(sm["dQP"], 16 * 128)
                        else:
                            v.wait_ge(sm["dCT"], 256 * g)
                    v.tensor_copy(FT[:, hc * 256:(hc + 1) * 256],
                                  pb[hc // 2][:, (hc % 2) * 256:(hc % 2) * 256 + 256]
                                  ).then_inc(sm["vCT"], 1)
            for ih in range(2):
                for tt in range(8):
                    T = ih * 8 + tt
                    p = T % 2
                    for kg in range(4):
                        uix = T * 8 + kg
                        if uix == 0:
                            v.wait_ge(sm["dLst"], 16 * NT)
                        v.wait_ge(sm["tTR3"], (T * 32) + 4 * kg + 4)
                        v.tensor_copy(SMALLB[:, p * 2048 + kg * 512: p * 2048 + (kg + 1) * 512],
                                      bb[uix % 2][:, 0:512]).then_inc(sm["vTR3"], 1)
                    slA = st3(ST_P3A, p)
                    v.wait_ge(sm["vTR3"], T * 8 + 4)
                    ats = SMALLB[:, p * 2048:(p + 1) * 2048]
                    for c in range(4):
                        i = v.bn_stats(slA["bn"][:, c * 6:(c + 1) * 6], ats[:, c * 512:(c + 1) * 512])
                    i.then_inc(sm["vP3"], 1)
                    v.wait_ge(sm["vP3"], 11 * T + 1)
                    v.bn_aggr(slA["mv"], slA["bn"]).then_inc(sm["vP3"], 1)
                    v.wait_ge(sm["vP3"], 11 * T + 2)
                    v.tensor_scalar_add(slA["t1"], slA["mv"][:, 1:2], EPS).then_inc(sm["vP3"], 1)
                    v.wait_ge(sm["aP3"], 4 * T + 1)
                    v.reciprocal(slA["rstd"], slA["t2"]).then_inc(sm["vP3"], 1)
                    v.wait_ge(sm["vP3"], 11 * T + 4)
                    v.scalar_tensor_tensor(slA["t3"], slA["mv"][:, 0:1], -1.0, slA["rstd"],
                                           ALU.mult, ALU.mult).then_inc(sm["vP3"], 1)
                    xs3 = SMALL[:, p * 2048:(p + 1) * 2048]
                    v.wait_ge(sm["aP3"], 4 * T + 2)
                    v.wait_ge(sm["dXo"], 16 * (T + 1))
                    v.tensor_tensor(xs3, xs3, SMALL[:, 4096:6144], ALU.add).then_inc(sm["vP3"], 1)
                    slB = st3(ST_P3B, p)
                    v.wait_ge(sm["vP3"], 11 * T + 6)
                    for c in range(4):
                        i = v.bn_stats(slB["bn"][:, c * 6:(c + 1) * 6], xs3[:, c * 512:(c + 1) * 512])
                    i.then_inc(sm["vP3"], 1)
                    v.wait_ge(sm["vP3"], 11 * T + 7)
                    v.bn_aggr(slB["mv"], slB["bn"]).then_inc(sm["vP3"], 1)
                    v.wait_ge(sm["vP3"], 11 * T + 8)
                    v.tensor_scalar_add(slB["t1"], slB["mv"][:, 1:2], EPS).then_inc(sm["vP3"], 1)
                    v.wait_ge(sm["aP3"], 4 * T + 3)
                    v.reciprocal(slB["rstd"], slB["t2"]).then_inc(sm["vP3"], 1)
                    v.wait_ge(sm["vP3"], 11 * T + 10)
                    v.scalar_tensor_tensor(slB["t3"], slB["mv"][:, 0:1], -1.0, slB["rstd"],
                                           ALU.mult, ALU.mult).then_inc(sm["vP3"], 1)
                    for kg in range(4):
                        uix = T * 8 + 4 + kg
                        v.wait_ge(sm["tTR3"], (T * 32) + 16 + 4 * kg + 4)
                        if ih == 0:
                            dest = STRM[:].rearrange("p (c i) -> p c i", c=16)[
                                :, kg * 4:(kg + 1) * 4, T * 128:T * 128 + 128]
                        else:
                            dest = BIG1[:].rearrange("p (c i) -> p c i", c=16)[
                                :, kg * 4:(kg + 1) * 4, (T - 8) * 128:(T - 8) * 128 + 128]
                        v.tensor_copy(dest, bb[uix % 2][:, 0:512].rearrange("p (a b) -> p a b", a=4)
                                      ).then_inc(sm["vTR3"], 1)
            for fc in range(FCN):
                for it in range(4):
                    idx = fc * 4 + it
                    v.wait_ge(sm["aG"], idx + 1)
                    v.wait_ge(sm["aU"], idx + 1)
                    if idx == 0:
                        v.wait_ge(sm["dLTst"], 16 * 64)
                    if idx >= 4:
                        v.wait_ge(sm["dH"], 16 * (idx - 3))
                    v.tensor_mul(LTST[:, (idx % 4) * 512:(idx % 4 + 1) * 512],
                                 SMALLB[:, (idx % 2) * 512:(idx % 2 + 1) * 512],
                                 SMALLB[:, 1024 + (idx % 2) * 512: 1024 + (idx % 2 + 1) * 512]
                                 ).then_inc(sm["vH"], 1)
            for it2 in range(4):
                for tt in range(4):
                    To = it2 * 4 + tt
                    p = To % 2
                    for og in range(4):
                        fgi = To * 4 + og
                        v.wait_ge(sm["tTF"], fgi * 4 + 4)
                        if og == 0:
                            if To == 0:
                                v.wait_ge(sm["aG"], FCN * 4)
                                v.wait_ge(sm["aU"], FCN * 4)
                            if To >= 2:
                                v.wait_ge(sm["aP4"], 2 * To - 2)
                        v.tensor_copy(SMALLB[:, p * 2048 + og * 512: p * 2048 + (og + 1) * 512],
                                      bb[fgi % 2][:, 0:512]).then_inc(sm["vTF"], 1)
                    sl4 = st3(ST_P4, p)
                    v.wait_ge(sm["vTF"], To * 4 + 4)
                    fts = SMALLB[:, p * 2048:(p + 1) * 2048]
                    for c in range(4):
                        i = v.bn_stats(sl4["bn"][:, c * 6:(c + 1) * 6], fts[:, c * 512:(c + 1) * 512])
                    i.then_inc(sm["vP4"], 1)
                    v.wait_ge(sm["vP4"], 6 * To + 1)
                    v.bn_aggr(sl4["mv"], sl4["bn"]).then_inc(sm["vP4"], 1)
                    v.wait_ge(sm["vP4"], 6 * To + 2)
                    v.tensor_scalar_add(sl4["t1"], sl4["mv"][:, 1:2], EPS).then_inc(sm["vP4"], 1)
                    v.wait_ge(sm["aP4"], 2 * To + 1)
                    v.reciprocal(sl4["rstd"], sl4["t2"]).then_inc(sm["vP4"], 1)
                    v.wait_ge(sm["vP4"], 6 * To + 4)
                    v.scalar_tensor_tensor(sl4["t3"], sl4["mv"][:, 0:1], -1.0, sl4["rstd"],
                                           ALU.mult, ALU.mult).then_inc(sm["vP4"], 1)
                    xs4 = SMALL[:, p * 2048:(p + 1) * 2048]
                    v.wait_ge(sm["aP4"], 2 * To + 2)
                    v.wait_ge(sm["dYl"], 16 * (To + 1))
                    v.tensor_tensor(xs4, xs4, SMALL[:, 4096:6144], ALU.add).then_inc(sm["vP4"], 1)

        @block.scalar
        def _(s):
            _ph0_evac_tile = PH0EVAC[1]
            s.wait_ge(sm["dCONST"], 16 * 7)
            K_E = [0]

            def _p1_evac(e):
                ih, r = e // 64, e % 64
                if r < 32:
                    oc, it = r // 2, r % 2
                    g1 = ih * 32 + r
                    s.wait_ge(sm["tM1"], g1 + 1)
                    if ih == 1 and g1 == 32:
                        s.wait_ge(sm["tM2"], 32)
                    s.activation(BIG1[:, oc * 1024 + it * 512: oc * 1024 + (it + 1) * 512],
                                 pb[4 + g1 % 4][:],
                                 AF.Identity, bias=CONST[:, C_BQ + oc:C_BQ + oc + 1]).then_inc(sm["aE1"], 1)
                else:
                    hcp, it = (r - 32) // 2, (r - 32) % 2
                    g2 = ih * 32 + (r - 32)
                    rr = g2 % 4
                    s.wait_ge(sm["tM2"], g2 + 1)
                    if g2 >= 4:
                        s.wait_ge(sm["dQP"], 16 * (2 * g2 - 6))
                    s.activation(FT[:, rr * 512:(rr + 1) * 512], pb[4 + g2 % 4][:],
                                 AF.Identity, scale=RSCALE).then_inc(sm["aE2"], 1)
                    for half in range(2):
                        gabs = ih * 4 + it * 2 + half
                        s.dma_start(out=QpTd[gabs][:, hcp * 256:(hcp + 1) * 256],
                                    in_=FT[:, rr * 512 + half * 256: rr * 512 + (half + 1) * 256]
                                    ).then_inc(sm["dQP"], 16)
            for sab in range(NSL):
                sp = sab % 2
                sl = st0(sp)
                s.wait_ge(sm["vP0"], 5 * sab + 3)
                s.activation(sl["t2"], sl["t1"], AF.Sqrt).then_inc(sm["aP0"], 1)
                for i4 in range(4):
                    t = 4 * sab + i4
                    if t >= 2 and t not in (16, 17):
                        _ph0_evac_tile(s, sm["aLT"], 1, t - 2)
                    s.wait_ge(sm["vP0"], 5 * sab + 5)
                    if t >= 2:
                        s.wait_ge(sm["tTR"], 16 * (t - 1))
                        s.wait_ge(sm["dLst"], 16 * (t - 1))
                    s.activation(SMALLB[:, (t % 2) * 2048:(t % 2 + 1) * 2048],
                                 STRM[:, sp * 8192 + i4 * 2048: sp * 8192 + (i4 + 1) * 2048],
                                 AF.Identity, bias=sl["t3"][:, i4:i4 + 1],
                                 scale=sl["rstd"][:, i4:i4 + 1]).then_inc(sm["aP0"], 1)
                if sab == 3:
                    _ph0_evac_tile(s, sm["aLT"], 1, 14)
                    _ph0_evac_tile(s, sm["aLT"], 1, 15)
                    while K_E[0] < 128:
                        _p1_evac(K_E[0])
                        K_E[0] += 1
            while K_E[0] < 128:
                _p1_evac(K_E[0])
                K_E[0] += 1
            _ph0_evac_tile(s, sm["aLT"], 1, NT - 2)
            _ph0_evac_tile(s, sm["aLT"], 1, NT - 1)
            for g in range(NG):
                for kt in range(NKT):
                    for qi in range(2):
                        idx = g * 64 + kt * 2 + qi
                        s.wait_ge(sm["tS"], idx + 1)
                        if g > 0 and kt == 0:
                            s.wait_ge(sm["tCk"], g * 32)
                        s.activation(BIG0[:, qi * S + kt * 512: qi * S + (kt + 1) * 512],
                                     pb[(kt * 2 + qi) % 2][:], AF.Copy).then_inc(sm["aS"], 1)
                for qi in range(2):
                    b = g * 2 + qi
                    for cch in range(4):
                        if cch == 0:
                            s.wait_ge(sm["vFM"], b + 1)
                        if b >= 1:
                            s.wait_ge(sm["vPT"], (b - 1) * 32 + 8 * (cch + 1))
                        s.activation(BIG1[:, cch * 4096:(cch + 1) * 4096],
                                     BIG0[:, qi * S + cch * 4096: qi * S + (cch + 1) * 4096],
                                     AF.Exp,
                                     bias=STATS[:, ST_NEGM + qi:ST_NEGM + qi + 1], scale=1.0,
                                     accum_out=STATS[:, ST_SE4 + qi * 4 + cch: ST_SE4 + qi * 4 + cch + 1]
                                     ).then_inc(sm["aEX"], 1)
            for ih in range(2):
                for oc in range(16):
                    for it in range(2):
                        g3 = ih * 32 + oc * 2 + it
                        s.wait_ge(sm["tM3"], g3 + 1)
                        s.activation(BIG0[:, oc * 1024 + it * 512: oc * 1024 + (it + 1) * 512],
                                     pb[4 + g3 % 4][:], AF.Identity,
                                     bias=CONST[:, C_BV + oc:C_BV + oc + 1]).then_inc(sm["aE3"], 1)
                for ocp in range(16):
                    for it in range(2):
                        g4 = ih * 32 + ocp * 2 + it
                        s.wait_ge(sm["tM4"], g4 + 1)
                        s.activation(BIG0[:, S + ocp * 1024 + it * 512: S + ocp * 1024 + (it + 1) * 512],
                                     pb[4 + g4 % 4][:], AF.Identity,
                                     bias=CONST[:, C_BO + ocp:C_BO + ocp + 1]).then_inc(sm["aE4"], 1)
                for tt in range(8):
                    T = ih * 8 + tt
                    p = T % 2
                    slA = st3(ST_P3A, p)
                    slB = st3(ST_P3B, p)
                    s.wait_ge(sm["vP3"], 11 * T + 3)
                    s.activation(slA["t2"], slA["t1"], AF.Sqrt).then_inc(sm["aP3"], 1)
                    s.wait_ge(sm["vP3"], 11 * T + 5)
                    s.activation(SMALL[:, 4096:6144], SMALLB[:, p * 2048:(p + 1) * 2048],
                                 AF.Identity, bias=slA["t3"], scale=slA["rstd"]).then_inc(sm["aP3"], 1)
                    s.wait_ge(sm["vP3"], 11 * T + 9)
                    s.activation(slB["t2"], slB["t1"], AF.Sqrt).then_inc(sm["aP3"], 1)
                    s.wait_ge(sm["vP3"], 11 * T + 11)
                    s.activation(SMALLB[:, p * 2048:(p + 1) * 2048], SMALL[:, p * 2048:(p + 1) * 2048],
                                 AF.Identity, bias=slB["t3"], scale=slB["rstd"]).then_inc(sm["aP3"], 1)
            for fc in range(FCN):
                for it in range(4):
                    idx = fc * 4 + it
                    s.wait_ge(sm["tG"], idx + 1)
                    if idx == 0:
                        s.wait_ge(sm["vTR3"], 128)
                    if idx >= 2:
                        s.wait_ge(sm["vH"], idx - 1)
                    s.activation(SMALLB[:, (idx % 2) * 512:(idx % 2 + 1) * 512],
                                 pb[4 + idx % 2][:], AF.Silu,
                                 scale=CONST[:, C_GS + fc:C_GS + fc + 1]).then_inc(sm["aG"], 1)
                    s.wait_ge(sm["tU"], idx + 1)
                    s.activation(SMALLB[:, 1024 + (idx % 2) * 512: 1024 + (idx % 2 + 1) * 512],
                                 pb[6 + idx % 2][:], AF.Identity,
                                 scale=CONST[:, C_US + fc:C_US + fc + 1]).then_inc(sm["aU"], 1)
            for it2 in range(4):
                for oc in range(16):
                    jdx = it2 * 16 + oc
                    s.wait_ge(sm["tD"], jdx + 1)
                    if jdx == 0:
                        s.wait_ge(sm["vTR3"], 128)
                    if oc == 0 and it2 >= 1:
                        s.wait_ge(sm["tTF"], it2 * 64)
                    s.activation(BIG0[:, 24576 + oc * 512: 24576 + (oc + 1) * 512],
                                 pb[4 + jdx % 2][:], AF.Identity,
                                 scale=CONST[:, C_DS + oc:C_DS + oc + 1]).then_inc(sm["aD"], 1)
                for tt in range(4):
                    To = it2 * 4 + tt
                    p = To % 2
                    sl4 = st3(ST_P4, p)
                    s.wait_ge(sm["vP4"], 6 * To + 3)
                    s.activation(sl4["t2"], sl4["t1"], AF.Sqrt).then_inc(sm["aP4"], 1)
                    s.wait_ge(sm["vP4"], 6 * To + 5)
                    s.activation(SMALL[:, 4096:6144], SMALLB[:, p * 2048:(p + 1) * 2048],
                                 AF.Identity, bias=sl4["t3"], scale=sl4["rstd"]).then_inc(sm["aP4"], 1)

        @block.tensor
        def _(t_):
            t_.wait_ge(sm["dCONST"], 16)
            def _ph0_T(t):
                t_.wait_ge(sm["aP0"], 5 * (t // 4) + (t % 4) + 2)
                for half in range(2):
                    if t >= 1:
                        t_.wait_ge(sm["vLT"] if half == 0 else sm["aLT"], t)
                    for k in range(8):
                        j = half * 8 + k
                        t_.transpose(bb[half][:, k * 128:(k + 1) * 128],
                                     SMALLB[:, (t % 2) * 2048 + j * 128: (t % 2) * 2048 + (j + 1) * 128],
                                     IDT[:]).then_inc(sm["tTR"], 1)
            for t in range(OWNT):
                _ph0_T(t)
            t_.wait_ge(sm["vLT"], 16)
            t_.wait_ge(sm["aLT"], 16)
            for ih in range(2):
                for oc in range(16):
                    for it in range(2):
                        g1 = ih * 32 + oc * 2 + it
                        t_.wait_ge(sm["dW1"], 16 * (ih * 32 + oc + 1))
                        if g1 >= 4:
                            t_.wait_ge(sm["aE1"], g1 - 3)
                        w = oc % 2
                        for hc in range(16):
                            i = t_.matmul(pb[4 + g1 % 4][:],
                                          WCOL[:, w * 2048 + hc * 128: w * 2048 + (hc + 1) * 128],
                                          BIG0[:, hc * 2048 + ih * 1024 + it * 512:
                                               hc * 2048 + ih * 1024 + (it + 1) * 512],
                                          start=(hc == 0), stop=(hc == 15))
                        i.then_inc(sm["tM1"], 1)
                for hcp in range(16):
                    for it in range(2):
                        g2 = ih * 32 + hcp * 2 + it
                        t_.wait_ge(sm["dW1"], 16 * (ih * 32 + 16 + hcp + 1))
                        if g2 < ih * 32 + 4:
                            t_.wait_ge(sm["aE1"], (ih + 1) * 32)
                        if g2 >= 4:
                            t_.wait_ge(sm["aE2"], g2 - 3)
                        w = hcp % 2
                        for oc in range(16):
                            i = t_.matmul(pb[4 + g2 % 4][:],
                                          WCOL[:, w * 2048 + oc * 128: w * 2048 + (oc + 1) * 128],
                                          BIG1[:, oc * 1024 + it * 512: oc * 1024 + (it + 1) * 512],
                                          start=(oc == 0), stop=(oc == 15))
                        i.then_inc(sm["tM2"], 1)
            for t in range(OWNT, NT):
                _ph0_T(t)
            for g in range(NG):
                t_.wait_ge(sm["dQG"], 16 * (g + 1))
                for kt in range(NKT):
                    slot = kt % 2
                    t_.wait_ge(sm["dLTA"], 64 * (g * 32 + kt + 1))
                    for qi in range(2):
                        idx = g * 64 + kt * 2 + qi
                        if g == 0 and idx < 2:
                            t_.wait_ge(sm["vLT"], 128)
                            t_.wait_ge(sm["aLT"], 128)
                        if g > 0 and kt == 0:
                            t_.wait_ge(sm["vCT"], (g - 1) * 16 + 4)
                        if idx - g * 64 >= 2:
                            t_.wait_ge(sm["aS"], idx - 1)
                        for hc in range(16):
                            i = t_.matmul(pb[(kt * 2 + qi) % 2][:],
                                          QPT[:, hc * 256 + qi * 128: hc * 256 + qi * 128 + 128],
                                          STRM[:, slot * 8192 + hc * 512: slot * 8192 + (hc + 1) * 512],
                                          start=(hc == 0), stop=(hc == 15))
                        i.then_inc(sm["tS"], 1)
                for qi in range(2):
                    b = g * 2 + qi
                    for jg in range(32):
                        gpt = b * 32 + jg
                        t_.wait_ge(sm["vPS"], 6 * b + 3 + jg // 8)
                        if qi == 0 and jg < 2:
                            t_.wait_ge(sm["aS"], g * 64 + 64)
                        if gpt >= 2:
                            t_.wait_ge(sm["vPT"], gpt - 1)
                        for k in range(4):
                            j = jg * 4 + k
                            t_.transpose(bb[gpt % 2][:, k * 128:(k + 1) * 128],
                                         BIG1[:, j * 128:(j + 1) * 128], IDT[:]).then_inc(sm["tTP"], 1)
                for kt in range(NKT):
                    slot = kt % 2
                    t_.wait_ge(sm["dLC"], 16 * (g * 32 + kt + 1))
                    if kt == 0:
                        t_.wait_ge(sm["vPT"], g * 64 + 64)
                        if g > 0:
                            t_.wait_ge(sm["vCT"], g * 16)
                    for jb in range(4):
                        rhs = BIG0q[:, :, (kt * 4 + jb) * 128:(kt * 4 + jb + 1) * 128]
                        for hc in range(16):
                            i = t_.matmul(
                                pb[hc // 2][:, (hc % 2) * 256:(hc % 2) * 256 + 256]
                                .rearrange("p (a b) -> p a b", a=2),
                                STRM[:, slot * 8192 + jb * 2048 + hc * 128:
                                     slot * 8192 + jb * 2048 + (hc + 1) * 128],
                                rhs,
                                start=(kt == 0 and jb == 0 and hc % 2 == 0),
                                stop=(kt == NKT - 1 and jb == 3 and hc % 2 == 1),
                                skip_group_check=True)
                    i.then_inc(sm["tCk"], 1)
            for ih in range(2):
                t_.wait_ge(sm["dCTl"], 16 * (ih + 1))
                for oc in range(16):
                    for it in range(2):
                        g3 = ih * 32 + oc * 2 + it
                        t_.wait_ge(sm["dW3"], 16 * (ih * 32 + oc + 1))
                        if g3 == 0:
                            t_.wait_ge(sm["vCT"], 128)
                        if g3 >= 4:
                            t_.wait_ge(sm["aE3"], g3 - 3)
                        w = oc % 2
                        for hc in range(16):
                            i = t_.matmul(pb[4 + g3 % 4][:],
                                          WCOL[:, w * 2048 + hc * 128: w * 2048 + (hc + 1) * 128],
                                          BIG1[:, hc * 1024 + it * 512: hc * 1024 + (it + 1) * 512],
                                          start=(hc == 0), stop=(hc == 15))
                        i.then_inc(sm["tM3"], 1)
                for ocp in range(16):
                    for it in range(2):
                        g4 = ih * 32 + ocp * 2 + it
                        t_.wait_ge(sm["dW3"], 16 * (ih * 32 + 16 + ocp + 1))
                        if g4 < ih * 32 + 4:
                            t_.wait_ge(sm["aE3"], ih * 32 + 32)
                        if g4 >= 4:
                            t_.wait_ge(sm["aE4"], g4 - 3)
                        w = ocp % 2
                        for oc in range(16):
                            i = t_.matmul(pb[4 + g4 % 4][:],
                                          WCOL[:, w * 2048 + oc * 128: w * 2048 + (oc + 1) * 128],
                                          BIG0[:, oc * 1024 + it * 512: oc * 1024 + (it + 1) * 512],
                                          start=(oc == 0), stop=(oc == 15))
                        i.then_inc(sm["tM4"], 1)
                for tt in range(8):
                    T = ih * 8 + tt
                    p = T % 2
                    for kg in range(4):
                        uix = T * 8 + kg
                        if uix < 2:
                            t_.wait_ge(sm["vCT"], 128)
                        if kg == 0:
                            t_.wait_ge(sm["aE4"], ih * 32 + 32)
                        if uix >= 2:
                            t_.wait_ge(sm["vTR3"], uix - 1)
                        for k4 in range(4):
                            k = kg * 4 + k4
                            t_.transpose(bb[uix % 2][:, k4 * 128:(k4 + 1) * 128],
                                         BIG0[:, S + k * 1024 + tt * 128: S + k * 1024 + tt * 128 + 128],
                                         IDT[:]).then_inc(sm["tTR3"], 1)
                    for kg in range(4):
                        uix = T * 8 + 4 + kg
                        if kg == 0:
                            t_.wait_ge(sm["aP3"], 4 * T + 4)
                        if uix >= 2:
                            t_.wait_ge(sm["vTR3"], uix - 1)
                        for k4 in range(4):
                            k = kg * 4 + k4
                            t_.transpose(bb[uix % 2][:, k4 * 128:(k4 + 1) * 128],
                                         SMALLB[:, p * 2048 + k * 128: p * 2048 + (k + 1) * 128],
                                         IDT[:]).then_inc(sm["tTR3"], 1)
            for fc in range(FCN):
                for it in range(4):
                    idx = fc * 4 + it
                    t_.wait_ge(sm["dW4"], 32 * fc + 16)
                    if idx == 0:
                        t_.wait_ge(sm["aE3"], 64)
                        t_.wait_ge(sm["vTR3"], 128)
                    if idx >= 2:
                        t_.wait_ge(sm["aG"], idx - 1)
                    wslot = fc % 2
                    rhsbuf = STRM if it < 2 else BIG1
                    itl = it if it < 2 else it - 2
                    for hc in range(16):
                        i = t_.matmul(pb[4 + idx % 2][:],
                                      WCOL[:, wslot * 2048 + hc * 128: wslot * 2048 + (hc + 1) * 128],
                                      rhsbuf[:, hc * 1024 + itl * 512: hc * 1024 + (itl + 1) * 512],
                                      start=(hc == 0), stop=(hc == 15))
                    i.then_inc(sm["tG"], 1)
                    t_.wait_ge(sm["dW4"], 32 * fc + 32)
                    if idx == 0:
                        t_.wait_ge(sm["aE4"], 64)
                    if idx >= 2:
                        t_.wait_ge(sm["aU"], idx - 1)
                    for hc in range(16):
                        i = t_.matmul(pb[6 + idx % 2][:],
                                      WCOL[:, 4096 + wslot * 2048 + hc * 128:
                                           4096 + wslot * 2048 + (hc + 1) * 128],
                                      rhsbuf[:, hc * 1024 + itl * 512: hc * 1024 + (itl + 1) * 512],
                                      start=(hc == 0), stop=(hc == 15))
                    i.then_inc(sm["tU"], 1)
            for it2 in range(4):
                for oc in range(16):
                    jdx = it2 * 16 + oc
                    t_.wait_ge(sm["dWd"], 16 * (jdx + 1))
                    if jdx == 0:
                        t_.wait_ge(sm["aG"], FCN * 4)
                        t_.wait_ge(sm["aU"], FCN * 4)
                    if jdx >= 2:
                        t_.wait_ge(sm["aD"], jdx - 1)
                    slot = oc % 2
                    for fc in range(FCN):
                        if oc == 0 and fc % 11 == 0:
                            t_.wait_ge(sm["dHl"], 16 * (it2 * 4 + fc // 11 + 1))
                        i = t_.matmul(pb[4 + jdx % 2][:],
                                      STRM[:, slot * 8192 + fc * 128: slot * 8192 + (fc + 1) * 128],
                                      BIG0[:, fc * 512:(fc + 1) * 512],
                                      start=(fc == 0), stop=(fc == FCN - 1))
                    i.then_inc(sm["tD"], 1)
                t_.wait_ge(sm["aD"], it2 * 16 + 16)
                for tt in range(4):
                    To = it2 * 4 + tt
                    for og in range(4):
                        fgi = To * 4 + og
                        if fgi < 2:
                            t_.wait_ge(sm["vTR3"], 128)
                        if fgi >= 2:
                            t_.wait_ge(sm["vTF"], fgi - 1)
                        for k4 in range(4):
                            oc = og * 4 + k4
                            t_.transpose(bb[fgi % 2][:, k4 * 128:(k4 + 1) * 128],
                                         BIG0[:, 24576 + oc * 512 + tt * 128:
                                              24576 + oc * 512 + tt * 128 + 128],
                                         IDT[:]).then_inc(sm["tTF"], 1)

    return nc


def _pack(Wp):
    K, M = Wp.shape
    R, C = K // 128, M // 128
    return np.ascontiguousarray(
        Wp.reshape(R, 128, C, 128).transpose(2, 1, 0, 3).reshape(C, 128, R * 128))


def _prep_shared(i):
    wqkv = np.asarray(i["L0_attn_w_qkv"], np.float32)
    m = {
        "wq_p": _pack(np.ascontiguousarray(wqkv[0:2048].T)),
        "wk_p": _pack(np.ascontiguousarray(wqkv[2048:4096])),
        "wv_p": _pack(np.ascontiguousarray(wqkv[4096:6144].T)),
        "wo_p": _pack(np.ascontiguousarray(np.asarray(i["L0_attn_w_out"], np.float32).T)),
        "wg_p": _pack(np.ascontiguousarray(np.asarray(i["L0_ffn_gate_w"], np.float32).T)),
        "wu_p": _pack(np.ascontiguousarray(np.asarray(i["L0_ffn_up_w"], np.float32).T)),
        "wd_p": _pack(np.ascontiguousarray(np.asarray(i["L0_ffn_down_w"], np.float32).T)),
        "bq": np.ascontiguousarray(np.asarray(i["L0_attn_b_qkv"], np.float32)[0:2048].reshape(16, 128).T),
        "bv": np.ascontiguousarray(np.asarray(i["L0_attn_b_qkv"], np.float32)[4096:6144].reshape(16, 128).T),
        "bo": np.ascontiguousarray(np.asarray(i["L0_attn_b_out"], np.float32).reshape(16, 128).T),
        "ds": np.ascontiguousarray(np.asarray(i["L0_ffn_down_s"], np.float32).reshape(16, 128).T),
        "gs": np.ascontiguousarray(np.asarray(i["L0_ffn_gate_s"], np.float32).reshape(44, 128).T),
        "us": np.ascontiguousarray(np.asarray(i["L0_ffn_up_s"], np.float32).reshape(44, 128).T),
        "ident": np.eye(128, dtype=ml_dtypes.bfloat16),
    }
    return m


def kernel(**inputs):
    x = np.asarray(inputs["L0_attn_hidden_states"], np.float32).reshape(S, D)
    shared = _prep_shared(inputs)
    nc = build(dbg=False)
    in_maps = []
    for c in range(8):
        m = dict(shared)
        m["x"] = np.ascontiguousarray(np.roll(x, -2048 * c, axis=0))
        in_maps.append(m)
    res = run_bass_kernel_spmd(nc, in_maps, core_ids=list(range(8)))
    out = np.concatenate([np.asarray(res.results[c]["out"]) for c in range(8)], axis=0)
    return out.reshape(1, S, D).astype(np.float32)


# version 07 (final)
# speedup vs baseline: 1.4426x; 1.1803x over previous
import contextlib
import numpy as np
import ml_dtypes
import concourse.bass as bass
import concourse.mybir as mybir
from concourse.bass_utils import run_bass_kernel_spmd

F32 = mybir.dt.float32
BF16 = mybir.dt.bfloat16
AF = mybir.ActivationFunctionType
ALU = mybir.AluOpType
AX = mybir.AxisListType

S = 16384
D = 2048
NT = 128
NSL = 64
OWNT = 16
OWN = 2048
NKT = 32
INTER = 5632
FCN = 44
EPS = 1e-5
RSCALE = float(np.float32(1.0) / np.float32(np.sqrt(np.float32(2048.0))))

C_BQ = 0
C_BV = 16
C_BO = 32
C_DS = 48
C_GS = 64
C_US = 108

ST_NM0 = 128
ST_NM1 = 136
ST_NMP = 144
ST_DD = 152
ST_CORR = 160
ST_SUM = 168
ST_SPART = 176
ST_RS = 184
ST_P3A = 200
ST_P3B = 260
ST_P4 = 320


def build(cut=3, dbg=False):
    nc = bass.Bass("TRN2", target_bir_lowering=False)

    x_d = nc.dram_tensor("x", [S, D], F32, kind="ExternalInput").ap()
    wq_p = nc.dram_tensor("wq_p", [16, 128, D], F32, kind="ExternalInput").ap()
    wk_p = nc.dram_tensor("wk_p", [16, 128, D], F32, kind="ExternalInput").ap()
    wv_p = nc.dram_tensor("wv_p", [16, 128, D], F32, kind="ExternalInput").ap()
    wo_p = nc.dram_tensor("wo_p", [16, 128, D], F32, kind="ExternalInput").ap()
    wg_p = nc.dram_tensor("wg_p", [44, 128, D], F32, kind="ExternalInput").ap()
    wu_p = nc.dram_tensor("wu_p", [44, 128, D], F32, kind="ExternalInput").ap()
    wd_p = nc.dram_tensor("wd_p", [16, 128, INTER], F32, kind="ExternalInput").ap()
    bq_d = nc.dram_tensor("bq", [128, 16], F32, kind="ExternalInput").ap()
    bv_d = nc.dram_tensor("bv", [128, 16], F32, kind="ExternalInput").ap()
    bo_d = nc.dram_tensor("bo", [128, 16], F32, kind="ExternalInput").ap()
    ds_d = nc.dram_tensor("ds", [128, 16], F32, kind="ExternalInput").ap()
    gs_d = nc.dram_tensor("gs", [128, 44], F32, kind="ExternalInput").ap()
    us_d = nc.dram_tensor("us", [128, 44], F32, kind="ExternalInput").ap()
    id_d = nc.dram_tensor("ident", [128, 128], BF16, kind="ExternalInput").ap()
    out_d = nc.dram_tensor("out", [OWN, D], F32, kind="ExternalOutput").ap()

    Ld = nc.dram_tensor("Ld", [S, D], BF16, kind="Internal").ap()
    LTd = nc.dram_tensor("LTd", [16, 128, S], BF16, kind="Internal").ap()
    QTd = nc.dram_tensor("QTd", [2, 128, 16384], BF16, kind="Internal").ap()
    CTd = nc.dram_tensor("CTd", [2, 128, 16384], BF16, kind="Internal").ap()
    HDNd = nc.dram_tensor("HDNd", [4, 128, FCN * 512], BF16, kind="Internal").ap()
    WDB = nc.dram_tensor("WDB", [16, 128, INTER], BF16, kind="Internal").ap()
    yd = nc.dram_tensor("yd", [OWN, D], F32, kind="Internal").ap()

    if dbg:
        dL = nc.dram_tensor("dL", [256, D], BF16, kind="ExternalOutput").ap()
        dLT = nc.dram_tensor("dLT", [128, 2048], BF16, kind="ExternalOutput").ap()
        dQT = nc.dram_tensor("dQT", [2, 128, 2048], BF16, kind="ExternalOutput").ap()
        dCT = nc.dram_tensor("dCT", [2, 128, 4096], BF16, kind="ExternalOutput").ap()
        dY = nc.dram_tensor("dY", [256, D], F32, kind="ExternalOutput").ap()

    Ld_v = Ld.rearrange("(k p) d -> k p d", p=128)
    x_v = x_d.rearrange("(k p) d -> k p d", p=128)
    yd_v = yd.rearrange("(k p) d -> k p d", p=128)
    out_v = out_d.rearrange("(k p) d -> k p d", p=128)
    CTd_c = [CTd[ih].rearrange("p (c u) -> p c u", c=16) for ih in range(2)]

    es = contextlib.ExitStack()
    with es:
        E = es.enter_context
        BIG0 = E(nc.sbuf_tensor("BIG0", [128, 32768], BF16))
        BIG1 = E(nc.sbuf_tensor("BIG1", [128, 16384], BF16))
        STRM = E(nc.sbuf_tensor("STRM", [128, 16384], BF16))
        QPT = E(nc.sbuf_tensor("QPT", [128, 16384], BF16))
        WCOL = E(nc.sbuf_tensor("WCOL", [128, 8192], BF16))
        FT = E(nc.sbuf_tensor("FT", [128, 4096], BF16))
        LTST = E(nc.sbuf_tensor("LTST", [128, 4096], BF16))
        SMALLB = E(nc.sbuf_tensor("SMALLB", [128, 4096], BF16))
        STATS = E(nc.sbuf_tensor("STATS", [128, 400], F32))
        CONST = E(nc.sbuf_tensor("CONST", [128, 256], F32))
        IDT = E(nc.sbuf_tensor("IDT", [128, 128], BF16))
        pb = [E(nc.psum_tensor("pb%d" % i, [128, 512], F32)) for i in range(8)]

        sem_names = [
            "dCONST", "dXB", "vP0", "aP0", "tTR", "vLT", "aLT", "dLst", "dLTst",
            "dW1", "tM1", "aE1", "tM2", "aE2", "dQP", "dWB",
            "dQT", "dLT0", "dLT1", "dLL", "tSC", "vSM", "aSM", "aEX", "tPT", "vPT",
            "tCX", "vCA", "aFN", "tCT", "vCT", "dCT",
            "dCTl", "dW3", "tM3", "aE3", "tM4", "aE4", "tTR3", "vTR3",
            "vP3", "aP3", "dXo", "dY",
            "dW4", "tG", "tU", "aG", "aU", "vH", "dH",
            "dWd", "dHl", "tD", "aD", "tTF", "vTF", "vP4", "aP4", "dYl", "dOUT",
        ]
        sm = {n: E(nc.semaphore(n)) for n in sem_names}
        block = E(nc.Block())

        bb = [pb[i][:].bitcast(BF16) for i in range(8)]
        BIG0_16 = BIG0[:].rearrange("p (c i) -> p c i", c=16)
        BIG0f = BIG0[:].bitcast(F32)
        LTST_16 = LTST[:].rearrange("p (c j) -> p c j", c=16)
        STRMf = STRM[:].bitcast(F32)
        SMALLv = QPT[:].bitcast(F32)

        def st3(base, p):
            o = base + p * 30
            return dict(bn=STATS[:, o:o + 24], mv=STATS[:, o + 24:o + 26],
                        t1=STATS[:, o + 26:o + 27], t2=STATS[:, o + 27:o + 28],
                        rstd=STATS[:, o + 28:o + 29], t3=STATS[:, o + 29:o + 30])

        def st0(sp):
            b = sp * 64
            mv = STATS[:, b + 48:b + 52].rearrange("p (t two) -> p t two", two=2)
            return dict(bn=STATS[:, b:b + 48],
                        mv=mv,
                        t1=STATS[:, b + 52:b + 54], t2=STATS[:, b + 54:b + 56],
                        rstd=STATS[:, b + 56:b + 58], t3=STATS[:, b + 58:b + 60])

        def col(base, qt):
            return STATS[:, base + qt:base + qt + 1]

        def _ph0_evac_tile(eng, sem, half, t):
            eng.wait_ge(sm["tTR"], 16 * t + 8 * half + 8)
            srcv = bb[half][:, 0:1024].rearrange("p (a b) -> p a b", a=8)
            if t < OWNT:
                dest = BIG0_16[:, half * 8:(half + 1) * 8, t * 128:t * 128 + 128]
            else:
                if t >= 18:
                    eng.wait_ge(sm["dLTst"], 16 * (8 + (t - 18) // 2 + 1))
                dest = LTST_16[:, half * 8:(half + 1) * 8, (t % 2) * 128:(t % 2) * 128 + 128]
            if hasattr(eng, "tensor_copy"):
                eng.tensor_copy(dest, srcv).then_inc(sem, 1)
            else:
                eng.copy(dest, srcv).then_inc(sem, 1)

        @block.sync
        def _(q):
            q.dma_start(out=IDT[:], in_=id_d).then_inc(sm["dCONST"], 16)
            q.dma_start(out=CONST[:, C_BQ:C_BQ + 16], in_=bq_d).then_inc(sm["dCONST"], 16)
            q.dma_start(out=CONST[:, C_BV:C_BV + 16], in_=bv_d).then_inc(sm["dCONST"], 16)
            q.dma_start(out=CONST[:, C_BO:C_BO + 16], in_=bo_d).then_inc(sm["dCONST"], 16)
            q.dma_start(out=CONST[:, C_DS:C_DS + 16], in_=ds_d).then_inc(sm["dCONST"], 16)
            q.dma_start(out=CONST[:, C_GS:C_GS + 44], in_=gs_d).then_inc(sm["dCONST"], 16)
            q.dma_start(out=CONST[:, C_US:C_US + 44], in_=us_d).then_inc(sm["dCONST"], 16)
            if cut == 1:
                q.wait_ge(sm["dQP"], 16 * 64)
                q.wait_ge(sm["dLst"], 16 * NT)
                q.wait_ge(sm["dLTst"], 16 * NSL)
                q.wait_ge(sm["dW1"], 16 * 64)
                q.dma_start(out=out_v[0][:, 0:400], in_=STATS[:]).then_inc(sm["dOUT"], 16)
                return
            for h in range(2):
                if h == 0:
                    q.wait_ge(sm["dQP"], 16 * 64)
                else:
                    q.wait_ge(sm["tSC"], 256)
                q.dma_start(out=QPT[:], in_=QTd[h]).then_inc(sm["dQT"], 16)
                for kt in range(NKT):
                    slot = kt % 2
                    tb = 256 * h
                    if kt >= 2:
                        q.wait_ge(sm["tSC"], tb + 8 * (kt - 1))
                    elif h == 1:
                        q.wait_ge(sm["tSC"], 248 + 8 * kt)
                    if h == 0:
                        if kt < 2:
                            q.wait_ge(sm["aP0"], 189 + 3 * kt)
                        q.wait_ge(sm["dLTst"], 16 * NSL)
                        q.wait_ge(sm["dLst"], 16 * NT)
                    for hq in range(4):
                        q.dma_start(
                            out=STRM[:, slot * 8192 + hq * 2048: slot * 8192 + (hq + 1) * 2048]
                                .rearrange("p (c j) -> p c j", c=4),
                            in_=LTd[hq * 4:(hq + 1) * 4, :, kt * 512:(kt + 1) * 512]
                                .rearrange("c p s -> p c s"),
                        ).then_inc(sm["dLT0" if slot == 0 else "dLT1"], 16)
                    if h == 0 and kt == 0:
                        q.wait_ge(sm["tM2"], 64)
                    else:
                        q.wait_ge(sm["tCX"], 1024 * h + 32 * kt)
                    q.dma_start(out=BIG1[:, 8192:16384].rearrange("p (c j) -> p c j", c=4),
                                in_=Ld_v[kt * 4:(kt + 1) * 4].rearrange("c p d -> p c d")
                                ).then_inc(sm["dLL"], 16)
            if cut == 2:
                q.wait_ge(sm["dCT"], 256)
                q.wait_ge(sm["dLst"], 16 * NT)
                q.dma_start(out=out_v[0][:, 0:400], in_=STATS[:]).then_inc(sm["dOUT"], 16)
                return
            for ih in range(2):
                q.wait_ge(sm["dCT"], 128 * (ih + 1))
                if ih == 0:
                    q.wait_ge(sm["tPT"], 2048)
                    q.wait_ge(sm["tCX"], 2048)
                    q.wait_ge(sm["aEX"], 512)
                else:
                    q.wait_ge(sm["tM3"], 32)
                q.dma_start(out=BIG1[:], in_=CTd[ih]).then_inc(sm["dCTl"], 16)
                for tt in range(8):
                    T = ih * 8 + tt
                    p = T % 2
                    if T == 0:
                        q.wait_ge(sm["tSC"], 512)
                    if T >= 2:
                        q.wait_ge(sm["aP3"], 4 * (T - 2) + 4)
                        q.wait_ge(sm["dY"], 16 * (T - 1))
                    q.dma_start(out=SMALLv[:, p * 2048:(p + 1) * 2048], in_=x_v[T]).then_inc(sm["dXo"], 16)
                    q.wait_ge(sm["vP3"], 11 * T + 6)
                    q.dma_start(out=yd_v[T], in_=SMALLv[:, p * 2048:(p + 1) * 2048]).then_inc(sm["dY"], 16)
            for fc in range(FCN):
                for it in range(4):
                    idx = fc * 4 + it
                    q.wait_ge(sm["vH"], idx + 1)
                    q.dma_start(out=HDNd[it][:, fc * 512:(fc + 1) * 512],
                                in_=LTST[:, (idx % 4) * 512:(idx % 4 + 1) * 512]).then_inc(sm["dH"], 16)
            for it2 in range(4):
                if it2 == 0:
                    q.wait_ge(sm["dH"], 16 * FCN * 4)
                    q.wait_ge(sm["tG"], FCN * 4)
                    q.wait_ge(sm["tU"], FCN * 4)
                    q.wait_ge(sm["vTR3"], 128)
                else:
                    q.wait_ge(sm["tD"], 16 * it2)
                for ch in range(4):
                    f0 = ch * 11
                    q.dma_start(out=BIG0[:, f0 * 512:(f0 + 11) * 512],
                                in_=HDNd[it2][:, f0 * 512:(f0 + 11) * 512]).then_inc(sm["dHl"], 16)
                for tt in range(4):
                    To = it2 * 4 + tt
                    p = To % 2
                    if To == 0:
                        q.wait_ge(sm["dY"], 16 * 16)
                        q.wait_ge(sm["aP3"], 64)
                    if To >= 2:
                        q.wait_ge(sm["dOUT"], 16 * (To - 1))
                    q.dma_start(out=SMALLv[:, p * 2048:(p + 1) * 2048], in_=yd_v[To]).then_inc(sm["dYl"], 16)
                    q.wait_ge(sm["vP4"], 6 * To + 6)
                    q.dma_start(out=out_v[To], in_=SMALLv[:, p * 2048:(p + 1) * 2048]).then_inc(sm["dOUT"], 16)

        @block.gpsimd
        def _(g_):
            def _slab_load(s):
                sp = s % 2
                if s >= 2:
                    g_.wait_ge(sm["aP0"], 3 * (s - 2) + 3)
                g_.dma_start(out=STRMf[:, sp * 4096:(sp + 1) * 4096]
                             .rearrange("p (c d) -> p c d", c=2),
                             in_=x_v[2 * s:2 * s + 2].rearrange("c p d -> p c d")
                             ).then_inc(sm["dXB"], 16)
            def _pair_store(s):
                g_.wait_ge(sm["vLT"], 2 * s + 2)
                g_.wait_ge(sm["aLT"], 2 * s + 2)
                if s < 8:
                    src = BIG0_16[:, :, s * 256:s * 256 + 256]
                else:
                    src = LTST_16
                g_.dma_start(out=LTd[:, :, s * 256:s * 256 + 256].rearrange("c p s -> p c s"),
                             in_=src).then_inc(sm["dLTst"], 16)

            def _w1_load(k):
                ih, r = k // 32, k % 32
                if r < 16:
                    oc = r
                    w = oc % 2
                    if ih == 1 and oc < 2:
                        g_.wait_ge(sm["tM2"], 32)
                    if oc >= 2:
                        g_.wait_ge(sm["tM1"], ih * 32 + 2 * oc - 2)
                    g_.dma_start(out=WCOL[:, w * 2048:(w + 1) * 2048], in_=wq_p[oc]).then_inc(sm["dW1"], 16)
                else:
                    hcp = r - 16
                    w = hcp % 2
                    g_.wait_ge(sm["tM1"], (ih + 1) * 32)
                    if hcp >= 2:
                        g_.wait_ge(sm["tM2"], ih * 32 + 2 * (hcp - 1))
                    g_.dma_start(out=WCOL[:, w * 2048:(w + 1) * 2048], in_=wk_p[hcp]).then_inc(sm["dW1"], 16)

            def _w1_bound(k):
                ih, r = k // 32, k % 32
                gneeds = []

                def tm1(v):
                    gneeds.append(v - 1 if v <= 32 else 31 + v)

                def tm2(v):
                    gneeds.append(31 + v if v <= 32 else 63 + v)
                if r < 16:
                    oc = r
                    if ih == 1 and oc < 2:
                        tm2(32)
                    if oc >= 2:
                        tm1(ih * 32 + 2 * oc - 2)
                else:
                    hcp = r - 16
                    tm1((ih + 1) * 32)
                    if hcp >= 2:
                        tm2(ih * 32 + 2 * (hcp - 1))
                b = 10
                for G in gneeds:
                    b = max(b, (G + 15) // 2 - 1)
                    if G >= 4:
                        b = max(b, 10 + (G - 4) // 2)
                return b

            def _w1_upper(k):
                ih, r = k // 32, k % 32
                G = ih * 64 + (2 * r if r < 16 else 32 + 2 * (r - 16))
                return 8 + G // 2
            W1S = [-1, -1]
            prev = 10
            for k in range(2, 64):
                lo = max(_w1_bound(k), prev)
                up = _w1_upper(k)
                assert lo <= up, (k, lo, up)
                W1S.append(lo)
                prev = lo
            _slab_load(0)
            _slab_load(1)
            _w1_load(0)
            _w1_load(1)
            for s in range(NSL):
                for i in range(2):
                    t = 2 * s + i
                    g_.wait_ge(sm["aP0"], 3 * s + i + 2)
                    g_.dma_start(out=Ld_v[t], in_=SMALLB[:, (t % 2) * 2048:(t % 2 + 1) * 2048]
                                 ).then_inc(sm["dLst"], 16)
                if s + 2 < NSL:
                    _slab_load(s + 2)
                if s >= 1:
                    _pair_store(s - 1)
                for k in range(2, 64):
                    if W1S[k] == s:
                        _w1_load(k)
            _pair_store(NSL - 1)
            for k in range(2, 64):
                if W1S[k] > NSL - 1:
                    _w1_load(k)
            if cut < 3:
                g_.wait_ge(sm["dOUT"], 16)
                return
            for oc in range(16):
                g_.dma_start(out=WDB[oc], in_=wd_p[oc]).then_inc(sm["dWB"], 16)
            for ih in range(2):
                for oc in range(16):
                    w = oc % 2
                    if ih == 0 and oc < 2:
                        g_.wait_ge(sm["tM2"], 64)
                    if ih == 1 and oc < 2:
                        g_.wait_ge(sm["tM4"], 32)
                    if oc >= 2:
                        g_.wait_ge(sm["tM3"], ih * 32 + 2 * oc - 2)
                    g_.dma_start(out=WCOL[:, w * 2048:(w + 1) * 2048], in_=wv_p[oc]).then_inc(sm["dW3"], 16)
                for ocp in range(16):
                    w = ocp % 2
                    g_.wait_ge(sm["tM3"], (ih + 1) * 32)
                    if ocp >= 2:
                        g_.wait_ge(sm["tM4"], ih * 32 + 2 * (ocp - 1))
                    g_.dma_start(out=WCOL[:, w * 2048:(w + 1) * 2048], in_=wo_p[ocp]).then_inc(sm["dW3"], 16)
            for fc in range(FCN):
                wslot = fc % 2
                if fc < 2:
                    g_.wait_ge(sm["tM4"], 64)
                else:
                    g_.wait_ge(sm["tG"], 4 * (fc - 1))
                g_.dma_start(out=WCOL[:, wslot * 2048:(wslot + 1) * 2048], in_=wg_p[fc]).then_inc(sm["dW4"], 16)
                if fc >= 2:
                    g_.wait_ge(sm["tU"], 4 * (fc - 1))
                g_.dma_start(out=WCOL[:, 4096 + wslot * 2048: 4096 + (wslot + 1) * 2048],
                             in_=wu_p[fc]).then_inc(sm["dW4"], 16)
            for it2 in range(4):
                for oc in range(16):
                    jdx = it2 * 16 + oc
                    slot = oc % 2
                    if jdx < 2:
                        g_.wait_ge(sm["dWB"], 256)
                        g_.wait_ge(sm["tG"], FCN * 4)
                        g_.wait_ge(sm["tU"], FCN * 4)
                    else:
                        g_.wait_ge(sm["tD"], jdx - 1)
                    g_.dma_start(out=STRM[:, slot * 8192: slot * 8192 + INTER],
                                 in_=WDB[oc]).then_inc(sm["dWd"], 16)
            g_.wait_ge(sm["dOUT"], 16 * 16)
            if dbg:
                g_.dma_start(out=dL, in_=Ld[0:256, :]).then_inc(sm["dXB"], 16)
                g_.dma_start(out=dLT, in_=LTd[2, :, 0:2048]).then_inc(sm["dXB"], 16)
                for hh in range(2):
                    g_.dma_start(out=dQT[hh], in_=QTd[hh][:, 0:2048]).then_inc(sm["dXB"], 16)
                    g_.dma_start(out=dCT[hh], in_=CTd[hh][:, 0:4096]).then_inc(sm["dXB"], 16)
                g_.dma_start(out=dY, in_=yd[0:256, :]).then_inc(sm["dXB"], 16)
                g_.wait_ge(sm["dXB"], 16 * 64 + 16 * 7)

        @block.vector
        def _(v):
            for s in range(NSL):
                sp = s % 2
                sl = st0(sp)
                xs = STRMf[:, sp * 4096:(sp + 1) * 4096]
                v.wait_ge(sm["dXB"], 16 * (s + 1))
                if s >= 2:
                    v.wait_ge(sm["aP0"], 3 * (s - 2) + 3)
                for ti in range(2):
                    for c in range(4):
                        i = v.bn_stats(sl["bn"][:, (ti * 4 + c) * 6:(ti * 4 + c + 1) * 6],
                                       xs[:, ti * 2048 + c * 512: ti * 2048 + (c + 1) * 512])
                i.then_inc(sm["vP0"], 1)
                v.wait_ge(sm["vP0"], 5 * s + 1)
                for ti in range(2):
                    i = v.bn_aggr(sl["mv"][:, ti], sl["bn"][:, ti * 24:(ti + 1) * 24])
                i.then_inc(sm["vP0"], 1)
                v.wait_ge(sm["vP0"], 5 * s + 2)
                v.tensor_scalar_add(sl["t1"], sl["mv"][:, :, 1], EPS).then_inc(sm["vP0"], 1)
                v.wait_ge(sm["aP0"], 3 * s + 1)
                v.reciprocal(sl["rstd"], sl["t2"]).then_inc(sm["vP0"], 1)
                v.wait_ge(sm["vP0"], 5 * s + 4)
                v.scalar_tensor_tensor(sl["t3"], sl["mv"][:, :, 0], -1.0, sl["rstd"],
                                       ALU.mult, ALU.mult).then_inc(sm["vP0"], 1)
                if s >= 1:
                    _ph0_evac_tile(v, sm["vLT"], 0, 2 * s - 2)
                    _ph0_evac_tile(v, sm["vLT"], 0, 2 * s - 1)
            _ph0_evac_tile(v, sm["vLT"], 0, NT - 2)
            _ph0_evac_tile(v, sm["vLT"], 0, NT - 1)
            if cut == 1:
                return
            for K in range(64):
                h, kt = K // 32, K % 32
                p = kt % 2
                NMn = ST_NM0 if p == 0 else ST_NM1
                NMo = ST_NM1 if p == 0 else ST_NM0
                SMB = 1032 * h
                for qt in range(8):
                    i = 8 * K + qt
                    v.wait_ge(sm["tSC"], i + 1)
                    v.reduce_max(col(ST_NMP, qt), pb[i % 2][:], axis=AX.X,
                                 negate=True).then_inc(sm["vSM"], 1)
                    v.wait_ge(sm["vSM"], SMB + 32 * kt + 3 * qt + 1)
                    if kt == 0:
                        v.tensor_copy(col(NMn, qt), col(ST_NMP, qt)).then_inc(sm["vSM"], 1)
                    else:
                        v.tensor_tensor(col(NMn, qt), col(NMo, qt), col(ST_NMP, qt),
                                        ALU.min).then_inc(sm["vSM"], 1)
                    v.wait_ge(sm["vSM"], SMB + 32 * kt + 3 * qt + 2)
                    v.tensor_tensor(col(ST_DD, qt), col(NMn, qt), col(NMo, qt),
                                    ALU.subtract).then_inc(sm["vSM"], 1)
                pslot = FT if p == 0 else LTST
                for qt in range(8):
                    v.wait_ge(sm["tPT"], 32 * K + 4 * (qt + 1))
                    if K >= 2:
                        v.wait_ge(sm["tCX"], 4 * (8 * (K - 2) + qt) + 4)
                    if K == 0 and qt == 0:
                        v.wait_ge(sm["dQP"], 16 * 64)
                    if K == 1 and qt == 0:
                        v.wait_ge(sm["dLTst"], 16 * NSL)
                    if K == 32 and qt == 0:
                        v.wait_ge(sm["dCT"], 128)
                    v.tensor_copy(pslot[:, qt * 512:(qt + 1) * 512],
                                  bb[2 + qt % 2][:, 0:512]).then_inc(sm["vPT"], 1)
                for qt in range(8):
                    v.wait_ge(sm["aEX"], 8 * K + qt + 1)
                    v.wait_ge(sm["vSM"], SMB + 32 * kt + 24 + qt)
                    if kt == 0:
                        i = v.tensor_copy(col(ST_SUM, qt), col(ST_SPART, qt))
                    else:
                        v.wait_ge(sm["aSM"], 8 * K + qt + 1)
                        i = v.scalar_tensor_tensor(col(ST_SUM, qt), col(ST_SUM, qt),
                                                   col(ST_CORR, qt), col(ST_SPART, qt),
                                                   ALU.mult, ALU.add)
                    i.then_inc(sm["vSM"], 1)
                for qt in range(8):
                    w = 8 * K + qt
                    for dc in range(4):
                        v.wait_ge(sm["tCX"], 4 * w + dc + 1)
                        v.wait_ge(sm["vCA"], 4 * w + dc)
                        dst = BIG0f[:, qt * 2048 + dc * 512: qt * 2048 + (dc + 1) * 512]
                        if kt == 0:
                            if h == 1 and qt == 0 and dc == 0:
                                v.wait_ge(sm["aFN"], 8)
                            i = v.tensor_copy(dst, pb[4 + dc][:])
                        else:
                            i = v.scalar_tensor_tensor(dst, dst, col(ST_CORR, qt),
                                                       pb[4 + dc][:], ALU.mult, ALU.add)
                        i.then_inc(sm["vCA"], 1)
                if kt == NKT - 1:
                    v.wait_ge(sm["vSM"], SMB + 1024)
                    for qt in range(8):
                        v.reciprocal(col(ST_RS, qt), col(ST_SUM, qt)).then_inc(sm["vSM"], 1)
                    for qt in range(8):
                        v.wait_ge(sm["tCT"], 8 * h + qt + 1)
                        if 8 * h + qt >= 2:
                            v.wait_ge(sm["dCT"], 16 * (8 * h + qt - 1))
                        v.tensor_copy(FT[:, (qt % 2) * 2048: (qt % 2) * 2048 + 1024],
                                      bb[2][:, 0:1024]).then_inc(sm["vCT"], 1)
                        v.tensor_copy(FT[:, (qt % 2) * 2048 + 1024: (qt % 2) * 2048 + 2048],
                                      bb[3][:, 0:1024]).then_inc(sm["vCT"], 1)
            if cut == 2:
                return
            for ih in range(2):
                for tt in range(8):
                    T = ih * 8 + tt
                    p = T % 2
                    for kg in range(4):
                        uix = T * 8 + kg
                        if uix == 0:
                            v.wait_ge(sm["dLst"], 16 * NT)
                        v.wait_ge(sm["tTR3"], (T * 32) + 4 * kg + 4)
                        v.tensor_copy(SMALLB[:, p * 2048 + kg * 512: p * 2048 + (kg + 1) * 512],
                                      bb[uix % 2][:, 0:512]).then_inc(sm["vTR3"], 1)
                    slA = st3(ST_P3A, p)
                    v.wait_ge(sm["vTR3"], T * 8 + 4)
                    ats = SMALLB[:, p * 2048:(p + 1) * 2048]
                    for c in range(4):
                        i = v.bn_stats(slA["bn"][:, c * 6:(c + 1) * 6], ats[:, c * 512:(c + 1) * 512])
                    i.then_inc(sm["vP3"], 1)
                    v.wait_ge(sm["vP3"], 11 * T + 1)
                    v.bn_aggr(slA["mv"], slA["bn"]).then_inc(sm["vP3"], 1)
                    v.wait_ge(sm["vP3"], 11 * T + 2)
                    v.tensor_scalar_add(slA["t1"], slA["mv"][:, 1:2], EPS).then_inc(sm["vP3"], 1)
                    v.wait_ge(sm["aP3"], 4 * T + 1)
                    v.reciprocal(slA["rstd"], slA["t2"]).then_inc(sm["vP3"], 1)
                    v.wait_ge(sm["vP3"], 11 * T + 4)
                    v.scalar_tensor_tensor(slA["t3"], slA["mv"][:, 0:1], -1.0, slA["rstd"],
                                           ALU.mult, ALU.mult).then_inc(sm["vP3"], 1)
                    xs3 = SMALLv[:, p * 2048:(p + 1) * 2048]
                    v.wait_ge(sm["aP3"], 4 * T + 2)
                    v.wait_ge(sm["dXo"], 16 * (T + 1))
                    v.tensor_tensor(xs3, xs3, SMALLv[:, 4096:6144], ALU.add).then_inc(sm["vP3"], 1)
                    slB = st3(ST_P3B, p)
                    v.wait_ge(sm["vP3"], 11 * T + 6)
                    for c in range(4):
                        i = v.bn_stats(slB["bn"][:, c * 6:(c + 1) * 6], xs3[:, c * 512:(c + 1) * 512])
                    i.then_inc(sm["vP3"], 1)
                    v.wait_ge(sm["vP3"], 11 * T + 7)
                    v.bn_aggr(slB["mv"], slB["bn"]).then_inc(sm["vP3"], 1)
                    v.wait_ge(sm["vP3"], 11 * T + 8)
                    v.tensor_scalar_add(slB["t1"], slB["mv"][:, 1:2], EPS).then_inc(sm["vP3"], 1)
                    v.wait_ge(sm["aP3"], 4 * T + 3)
                    v.reciprocal(slB["rstd"], slB["t2"]).then_inc(sm["vP3"], 1)
                    v.wait_ge(sm["vP3"], 11 * T + 10)
                    v.scalar_tensor_tensor(slB["t3"], slB["mv"][:, 0:1], -1.0, slB["rstd"],
                                           ALU.mult, ALU.mult).then_inc(sm["vP3"], 1)
                    for kg in range(4):
                        uix = T * 8 + 4 + kg
                        v.wait_ge(sm["tTR3"], (T * 32) + 16 + 4 * kg + 4)
                        if ih == 0:
                            dest = STRM[:].rearrange("p (c i) -> p c i", c=16)[
                                :, kg * 4:(kg + 1) * 4, T * 128:T * 128 + 128]
                        else:
                            dest = BIG1[:].rearrange("p (c i) -> p c i", c=16)[
                                :, kg * 4:(kg + 1) * 4, (T - 8) * 128:(T - 8) * 128 + 128]
                        v.tensor_copy(dest, bb[uix % 2][:, 0:512].rearrange("p (a b) -> p a b", a=4)
                                      ).then_inc(sm["vTR3"], 1)
            for fc in range(FCN):
                for it in range(4):
                    idx = fc * 4 + it
                    v.wait_ge(sm["aG"], idx + 1)
                    v.wait_ge(sm["aU"], idx + 1)
                    if idx == 0:
                        v.wait_ge(sm["dLTst"], 16 * NSL)
                        v.wait_ge(sm["tCX"], 2048)
                    if idx >= 4:
                        v.wait_ge(sm["dH"], 16 * (idx - 3))
                    v.tensor_mul(LTST[:, (idx % 4) * 512:(idx % 4 + 1) * 512],
                                 SMALLB[:, (idx % 2) * 512:(idx % 2 + 1) * 512],
                                 SMALLB[:, 1024 + (idx % 2) * 512: 1024 + (idx % 2 + 1) * 512]
                                 ).then_inc(sm["vH"], 1)
            for it2 in range(4):
                for tt in range(4):
                    To = it2 * 4 + tt
                    p = To % 2
                    for og in range(4):
                        fgi = To * 4 + og
                        v.wait_ge(sm["tTF"], fgi * 4 + 4)
                        if og == 0:
                            if To == 0:
                                v.wait_ge(sm["aG"], FCN * 4)
                                v.wait_ge(sm["aU"], FCN * 4)
                            if To >= 2:
                                v.wait_ge(sm["aP4"], 2 * To - 2)
                        v.tensor_copy(SMALLB[:, p * 2048 + og * 512: p * 2048 + (og + 1) * 512],
                                      bb[fgi % 2][:, 0:512]).then_inc(sm["vTF"], 1)
                    sl4 = st3(ST_P4, p)
                    v.wait_ge(sm["vTF"], To * 4 + 4)
                    fts = SMALLB[:, p * 2048:(p + 1) * 2048]
                    for c in range(4):
                        i = v.bn_stats(sl4["bn"][:, c * 6:(c + 1) * 6], fts[:, c * 512:(c + 1) * 512])
                    i.then_inc(sm["vP4"], 1)
                    v.wait_ge(sm["vP4"], 6 * To + 1)
                    v.bn_aggr(sl4["mv"], sl4["bn"]).then_inc(sm["vP4"], 1)
                    v.wait_ge(sm["vP4"], 6 * To + 2)
                    v.tensor_scalar_add(sl4["t1"], sl4["mv"][:, 1:2], EPS).then_inc(sm["vP4"], 1)
                    v.wait_ge(sm["aP4"], 2 * To + 1)
                    v.reciprocal(sl4["rstd"], sl4["t2"]).then_inc(sm["vP4"], 1)
                    v.wait_ge(sm["vP4"], 6 * To + 4)
                    v.scalar_tensor_tensor(sl4["t3"], sl4["mv"][:, 0:1], -1.0, sl4["rstd"],
                                           ALU.mult, ALU.mult).then_inc(sm["vP4"], 1)
                    xs4 = SMALLv[:, p * 2048:(p + 1) * 2048]
                    v.wait_ge(sm["aP4"], 2 * To + 2)
                    v.wait_ge(sm["dYl"], 16 * (To + 1))
                    v.tensor_tensor(xs4, xs4, SMALLv[:, 4096:6144], ALU.add).then_inc(sm["vP4"], 1)

        @block.scalar
        def _(s):
            s.wait_ge(sm["dCONST"], 16 * 7)
            K_E = [0]

            def _p1_evac(e):
                ih, r = e // 64, e % 64
                if r < 32:
                    oc, it = r // 2, r % 2
                    g1 = ih * 32 + r
                    s.wait_ge(sm["tM1"], g1 + 1)
                    if ih == 1 and g1 == 32:
                        s.wait_ge(sm["tM2"], 32)
                    s.activation(BIG1[:, oc * 1024 + it * 512: oc * 1024 + (it + 1) * 512],
                                 pb[4 + g1 % 4][:],
                                 AF.Identity, bias=CONST[:, C_BQ + oc:C_BQ + oc + 1]).then_inc(sm["aE1"], 1)
                else:
                    hcp, it = (r - 32) // 2, (r - 32) % 2
                    g2 = ih * 32 + (r - 32)
                    rr = g2 % 4
                    s.wait_ge(sm["tM2"], g2 + 1)
                    if g2 >= 4:
                        s.wait_ge(sm["dQP"], 16 * (g2 - 3))
                    s.activation(FT[:, rr * 512:(rr + 1) * 512], pb[4 + g2 % 4][:],
                                 AF.Identity, scale=RSCALE).then_inc(sm["aE2"], 1)
                    s.dma_start(out=QTd[ih][:, hcp * 1024 + it * 512: hcp * 1024 + (it + 1) * 512],
                                in_=FT[:, rr * 512:(rr + 1) * 512]).then_inc(sm["dQP"], 16)
            for sab in range(NSL):
                sp = sab % 2
                sl = st0(sp)
                s.wait_ge(sm["vP0"], 5 * sab + 3)
                s.activation(sl["t2"], sl["t1"], AF.Sqrt).then_inc(sm["aP0"], 1)
                if sab >= 10:
                    lim = min(128, (sab - 9) * 2)
                    while K_E[0] < lim:
                        _p1_evac(K_E[0])
                        K_E[0] += 1
                for i in range(2):
                    t = 2 * sab + i
                    if t >= 2:
                        _ph0_evac_tile(s, sm["aLT"], 1, t - 2)
                    s.wait_ge(sm["vP0"], 5 * sab + 5)
                    if t >= 2:
                        s.wait_ge(sm["tTR"], 16 * (t - 1))
                        s.wait_ge(sm["dLst"], 16 * (t - 1))
                    s.activation(SMALLB[:, (t % 2) * 2048:(t % 2 + 1) * 2048],
                                 STRMf[:, sp * 4096 + i * 2048: sp * 4096 + (i + 1) * 2048],
                                 AF.Identity, bias=sl["t3"][:, i:i + 1],
                                 scale=sl["rstd"][:, i:i + 1]).then_inc(sm["aP0"], 1)
            _ph0_evac_tile(s, sm["aLT"], 1, NT - 2)
            _ph0_evac_tile(s, sm["aLT"], 1, NT - 1)
            while K_E[0] < 128:
                _p1_evac(K_E[0])
                K_E[0] += 1
            if cut == 1:
                return
            for K in range(64):
                h, kt = K // 32, K % 32
                p = kt % 2
                NMn = ST_NM0 if p == 0 else ST_NM1
                SMB = 1032 * h
                for qt in range(8):
                    i = 8 * K + qt
                    s.wait_ge(sm["vSM"], SMB + 32 * kt + 3 * qt + 3)
                    if kt >= 1:
                        s.wait_ge(sm["vSM"], SMB + 32 * (kt - 1) + 24 + qt + 1)
                        s.wait_ge(sm["vCA"], 4 * (8 * (K - 1) + qt) + 4)
                    elif h == 1:
                        s.wait_ge(sm["vSM"], 32 * 31 + 24 + qt + 1)
                        s.wait_ge(sm["vCA"], 4 * (8 * (K - 1) + qt) + 4)
                    s.activation(col(ST_CORR, qt), col(ST_DD, qt), AF.Exp).then_inc(sm["aSM"], 1)
                    s.wait_ge(sm["tSC"], i + 1)
                    if K >= 2:
                        s.wait_ge(sm["tPT"], 32 * (K - 2) + 4 * (qt + 1))
                    s.activation(BIG1[:, qt * 1024 + p * 512: qt * 1024 + p * 512 + 512],
                                 pb[i % 2][:], AF.Exp,
                                 bias=col(NMn, qt), scale=1.0,
                                 accum_out=col(ST_SPART, qt)).then_inc(sm["aEX"], 1)
                if kt == NKT - 1:
                    def _ct_store(qt):
                        s.wait_ge(sm["vCT"], 2 * (8 * h + qt) + 2)
                        s.dma_start(out=CTd_c[h][:, :, qt * 128:(qt + 1) * 128],
                                    in_=FT[:, (qt % 2) * 2048:(qt % 2) * 2048 + 2048]
                                    .rearrange("p (c u) -> p c u", c=16)).then_inc(sm["dCT"], 16)
                    for qt in range(8):
                        s.wait_ge(sm["vSM"], SMB + 1024 + qt + 1)
                        s.wait_ge(sm["vCA"], 4 * (8 * K + qt) + 4)
                        if qt == 0:
                            s.wait_ge(sm["dLst"], 16 * NT)
                            s.wait_ge(sm["tTR"], 16 * NT)
                        if qt >= 2:
                            s.wait_ge(sm["tCT"], 8 * h + qt - 1)
                        s.activation(SMALLB[:, (qt % 2) * 2048:(qt % 2) * 2048 + 2048],
                                     BIG0f[:, qt * 2048:(qt + 1) * 2048],
                                     AF.Identity, scale=col(ST_RS, qt)).then_inc(sm["aFN"], 1)
                        if qt >= 2:
                            _ct_store(qt - 2)
                    _ct_store(6)
                    _ct_store(7)
            if cut == 2:
                return
            for ih in range(2):
                for oc in range(16):
                    for it in range(2):
                        g3 = ih * 32 + oc * 2 + it
                        s.wait_ge(sm["tM3"], g3 + 1)
                        s.activation(BIG0[:, oc * 1024 + it * 512: oc * 1024 + (it + 1) * 512],
                                     pb[4 + g3 % 4][:], AF.Identity,
                                     bias=CONST[:, C_BV + oc:C_BV + oc + 1]).then_inc(sm["aE3"], 1)
                for ocp in range(16):
                    for it in range(2):
                        g4 = ih * 32 + ocp * 2 + it
                        s.wait_ge(sm["tM4"], g4 + 1)
                        s.activation(BIG0[:, S + ocp * 1024 + it * 512: S + ocp * 1024 + (it + 1) * 512],
                                     pb[4 + g4 % 4][:], AF.Identity,
                                     bias=CONST[:, C_BO + ocp:C_BO + ocp + 1]).then_inc(sm["aE4"], 1)
                for tt in range(8):
                    T = ih * 8 + tt
                    p = T % 2
                    slA = st3(ST_P3A, p)
                    slB = st3(ST_P3B, p)
                    s.wait_ge(sm["vP3"], 11 * T + 3)
                    s.activation(slA["t2"], slA["t1"], AF.Sqrt).then_inc(sm["aP3"], 1)
                    s.wait_ge(sm["vP3"], 11 * T + 5)
                    s.activation(SMALLv[:, 4096:6144], SMALLB[:, p * 2048:(p + 1) * 2048],
                                 AF.Identity, bias=slA["t3"], scale=slA["rstd"]).then_inc(sm["aP3"], 1)
                    s.wait_ge(sm["vP3"], 11 * T + 9)
                    s.activation(slB["t2"], slB["t1"], AF.Sqrt).then_inc(sm["aP3"], 1)
                    s.wait_ge(sm["vP3"], 11 * T + 11)
                    s.activation(SMALLB[:, p * 2048:(p + 1) * 2048], SMALLv[:, p * 2048:(p + 1) * 2048],
                                 AF.Identity, bias=slB["t3"], scale=slB["rstd"]).then_inc(sm["aP3"], 1)
            for fc in range(FCN):
                for it in range(4):
                    idx = fc * 4 + it
                    s.wait_ge(sm["tG"], idx + 1)
                    if idx == 0:
                        s.wait_ge(sm["vTR3"], 128)
                    if idx >= 2:
                        s.wait_ge(sm["vH"], idx - 1)
                    s.activation(SMALLB[:, (idx % 2) * 512:(idx % 2 + 1) * 512],
                                 pb[4 + idx % 2][:], AF.Silu,
                                 scale=CONST[:, C_GS + fc:C_GS + fc + 1]).then_inc(sm["aG"], 1)
                    s.wait_ge(sm["tU"], idx + 1)
                    s.activation(SMALLB[:, 1024 + (idx % 2) * 512: 1024 + (idx % 2 + 1) * 512],
                                 pb[6 + idx % 2][:], AF.Identity,
                                 scale=CONST[:, C_US + fc:C_US + fc + 1]).then_inc(sm["aU"], 1)
            for it2 in range(4):
                for oc in range(16):
                    jdx = it2 * 16 + oc
                    s.wait_ge(sm["tD"], jdx + 1)
                    if jdx == 0:
                        s.wait_ge(sm["vTR3"], 128)
                    if oc == 0 and it2 >= 1:
                        s.wait_ge(sm["tTF"], it2 * 64)
                    s.activation(BIG0[:, 24576 + oc * 512: 24576 + (oc + 1) * 512],
                                 pb[4 + jdx % 2][:], AF.Identity,
                                 scale=CONST[:, C_DS + oc:C_DS + oc + 1]).then_inc(sm["aD"], 1)
                for tt in range(4):
                    To = it2 * 4 + tt
                    p = To % 2
                    sl4 = st3(ST_P4, p)
                    s.wait_ge(sm["vP4"], 6 * To + 3)
                    s.activation(sl4["t2"], sl4["t1"], AF.Sqrt).then_inc(sm["aP4"], 1)
                    s.wait_ge(sm["vP4"], 6 * To + 5)
                    s.activation(SMALLv[:, 4096:6144], SMALLB[:, p * 2048:(p + 1) * 2048],
                                 AF.Identity, bias=sl4["t3"], scale=sl4["rstd"]).then_inc(sm["aP4"], 1)

        @block.tensor
        def _(t_):
            t_.wait_ge(sm["dCONST"], 16 * 7)

            def _ph0_T(t):
                t_.wait_ge(sm["aP0"], 3 * (t // 2) + (t % 2) + 2)
                for half in range(2):
                    if t >= 1:
                        t_.wait_ge(sm["vLT"] if half == 0 else sm["aLT"], t)
                    for k in range(8):
                        j = half * 8 + k
                        t_.transpose(bb[half][:, k * 128:(k + 1) * 128],
                                     SMALLB[:, (t % 2) * 2048 + j * 128: (t % 2) * 2048 + (j + 1) * 128],
                                     IDT[:]).then_inc(sm["tTR"], 1)
            for t in range(OWNT):
                _ph0_T(t)
            t_.wait_ge(sm["vLT"], 16)
            t_.wait_ge(sm["aLT"], 16)
            TAIL = [0]

            def _tail_T():
                if OWNT + TAIL[0] < NT:
                    _ph0_T(OWNT + TAIL[0])
                    TAIL[0] += 1
            for ih in range(2):
                for oc in range(16):
                    for it in range(2):
                        g1 = ih * 32 + oc * 2 + it
                        t_.wait_ge(sm["dW1"], 16 * (ih * 32 + oc + 1))
                        if g1 >= 4:
                            t_.wait_ge(sm["aE1"], g1 - 3)
                        w = oc % 2
                        for hc in range(16):
                            i = t_.matmul(pb[4 + g1 % 4][:],
                                          WCOL[:, w * 2048 + hc * 128: w * 2048 + (hc + 1) * 128],
                                          BIG0[:, hc * 2048 + ih * 1024 + it * 512:
                                               hc * 2048 + ih * 1024 + (it + 1) * 512],
                                          start=(hc == 0), stop=(hc == 15))
                        i.then_inc(sm["tM1"], 1)
                        _tail_T()
                for hcp in range(16):
                    for it in range(2):
                        g2 = ih * 32 + hcp * 2 + it
                        t_.wait_ge(sm["dW1"], 16 * (ih * 32 + 16 + hcp + 1))
                        if g2 < ih * 32 + 4:
                            t_.wait_ge(sm["aE1"], (ih + 1) * 32)
                        if g2 >= 4:
                            t_.wait_ge(sm["aE2"], g2 - 3)
                        w = hcp % 2
                        for oc in range(16):
                            i = t_.matmul(pb[4 + g2 % 4][:],
                                          WCOL[:, w * 2048 + oc * 128: w * 2048 + (oc + 1) * 128],
                                          BIG1[:, oc * 1024 + it * 512: oc * 1024 + (it + 1) * 512],
                                          start=(oc == 0), stop=(oc == 15))
                        i.then_inc(sm["tM2"], 1)
                        _tail_T()
            while OWNT + TAIL[0] < NT:
                _ph0_T(OWNT + TAIL[0])
                TAIL[0] += 1
            if cut == 1:
                return
            for K in range(64):
                h, kt = K // 32, K % 32
                p = kt % 2
                slot = kt % 2
                for qt in range(8):
                    i = 8 * K + qt
                    if K == 0 and qt == 0:
                        t_.wait_ge(sm["dQT"], 16)
                        t_.wait_ge(sm["vLT"], 128)
                        t_.wait_ge(sm["aLT"], 128)
                    if K == 32 and qt == 0:
                        t_.wait_ge(sm["dQT"], 32)
                    t_.wait_ge(sm["dLT0" if kt % 2 == 0 else "dLT1"],
                               1024 * h + 64 * (kt // 2 + 1))
                    if i >= 2:
                        t_.wait_ge(sm["aEX"], i - 1)
                    for hc in range(16):
                        i_ = t_.matmul(pb[i % 2][:],
                                       QPT[:, hc * 1024 + qt * 128: hc * 1024 + qt * 128 + 128],
                                       STRM[:, slot * 8192 + hc * 512: slot * 8192 + (hc + 1) * 512],
                                       start=(hc == 0), stop=(hc == 15))
                    i_.then_inc(sm["tSC"], 1)
                for qt in range(8):
                    t_.wait_ge(sm["aEX"], 8 * K + qt + 1)
                    if K == 32 and qt < 2:
                        t_.wait_ge(sm["vCT"], 16)
                    if 8 * K + qt >= 2:
                        t_.wait_ge(sm["vPT"], 8 * K + qt - 1)
                    for kc in range(4):
                        t_.transpose(bb[2 + qt % 2][:, kc * 128:(kc + 1) * 128],
                                     BIG1[:, qt * 1024 + p * 512 + kc * 128:
                                          qt * 1024 + p * 512 + (kc + 1) * 128],
                                     IDT[:]).then_inc(sm["tPT"], 1)
                pslot = FT if p == 0 else LTST
                for qt in range(8):
                    w = 8 * K + qt
                    t_.wait_ge(sm["vPT"], w + 1)
                    t_.wait_ge(sm["dLL"], 512 * h + 16 * (kt + 1))
                    if K == 0 and qt == 0:
                        t_.wait_ge(sm["aE2"], 64)
                    for dc in range(4):
                        if w >= 1:
                            t_.wait_ge(sm["vCA"], 4 * (w - 1) + dc + 1)
                        for kc in range(4):
                            i_ = t_.matmul(pb[4 + dc][:],
                                           pslot[:, qt * 512 + kc * 128: qt * 512 + (kc + 1) * 128],
                                           BIG1[:, 8192 + kc * 2048 + dc * 512:
                                                8192 + kc * 2048 + (dc + 1) * 512],
                                           start=(kc == 0), stop=(kc == 3))
                        i_.then_inc(sm["tCX"], 1)
                if kt == NKT - 1:
                    for qt in range(8):
                        t_.wait_ge(sm["aFN"], 8 * h + qt + 1)
                        if qt == 0:
                            t_.wait_ge(sm["vPT"], 256 * (h + 1))
                        if 8 * h + qt > 0:
                            t_.wait_ge(sm["vCT"], 2 * (8 * h + qt))
                        for c in range(16):
                            i_ = t_.transpose(bb[2 + c // 8][:, (c % 8) * 128:(c % 8 + 1) * 128],
                                              SMALLB[:, (qt % 2) * 2048 + c * 128:
                                                     (qt % 2) * 2048 + (c + 1) * 128],
                                              IDT[:])
                        i_.then_inc(sm["tCT"], 1)
            if cut == 2:
                return
            for ih in range(2):
                t_.wait_ge(sm["dCTl"], 16 * (ih + 1))
                for oc in range(16):
                    for it in range(2):
                        g3 = ih * 32 + oc * 2 + it
                        t_.wait_ge(sm["dW3"], 16 * (ih * 32 + oc + 1))
                        if g3 == 0:
                            t_.wait_ge(sm["vCA"], 2048)
                        if g3 >= 4:
                            t_.wait_ge(sm["aE3"], g3 - 3)
                        w = oc % 2
                        for hc in range(16):
                            i = t_.matmul(pb[4 + g3 % 4][:],
                                          WCOL[:, w * 2048 + hc * 128: w * 2048 + (hc + 1) * 128],
                                          BIG1[:, hc * 1024 + it * 512: hc * 1024 + (it + 1) * 512],
                                          start=(hc == 0), stop=(hc == 15))
                        i.then_inc(sm["tM3"], 1)
                for ocp in range(16):
                    for it in range(2):
                        g4 = ih * 32 + ocp * 2 + it
                        t_.wait_ge(sm["dW3"], 16 * (ih * 32 + 16 + ocp + 1))
                        if g4 < ih * 32 + 4:
                            t_.wait_ge(sm["aE3"], ih * 32 + 32)
                        if g4 >= 4:
                            t_.wait_ge(sm["aE4"], g4 - 3)
                        w = ocp % 2
                        for oc in range(16):
                            i = t_.matmul(pb[4 + g4 % 4][:],
                                          WCOL[:, w * 2048 + oc * 128: w * 2048 + (oc + 1) * 128],
                                          BIG0[:, oc * 1024 + it * 512: oc * 1024 + (it + 1) * 512],
                                          start=(oc == 0), stop=(oc == 15))
                        i.then_inc(sm["tM4"], 1)
                for tt in range(8):
                    T = ih * 8 + tt
                    p = T % 2
                    for kg in range(4):
                        uix = T * 8 + kg
                        if uix < 2:
                            t_.wait_ge(sm["aEX"], 512)
                        if kg == 0:
                            t_.wait_ge(sm["aE4"], ih * 32 + 32)
                        if uix >= 2:
                            t_.wait_ge(sm["vTR3"], uix - 1)
                        for k4 in range(4):
                            k = kg * 4 + k4
                            t_.transpose(bb[uix % 2][:, k4 * 128:(k4 + 1) * 128],
                                         BIG0[:, S + k * 1024 + tt * 128: S + k * 1024 + tt * 128 + 128],
                                         IDT[:]).then_inc(sm["tTR3"], 1)
                    for kg in range(4):
                        uix = T * 8 + 4 + kg
                        if kg == 0:
                            t_.wait_ge(sm["aP3"], 4 * T + 4)
                        if uix >= 2:
                            t_.wait_ge(sm["vTR3"], uix - 1)
                        for k4 in range(4):
                            k = kg * 4 + k4
                            t_.transpose(bb[uix % 2][:, k4 * 128:(k4 + 1) * 128],
                                         SMALLB[:, p * 2048 + k * 128: p * 2048 + (k + 1) * 128],
                                         IDT[:]).then_inc(sm["tTR3"], 1)
            for fc in range(FCN):
                for it in range(4):
                    idx = fc * 4 + it
                    t_.wait_ge(sm["dW4"], 32 * fc + 16)
                    if idx == 0:
                        t_.wait_ge(sm["aE3"], 64)
                        t_.wait_ge(sm["vTR3"], 128)
                    if idx >= 2:
                        t_.wait_ge(sm["aG"], idx - 1)
                    wslot = fc % 2
                    rhsbuf = STRM if it < 2 else BIG1
                    itl = it if it < 2 else it - 2
                    for hc in range(16):
                        i = t_.matmul(pb[4 + idx % 2][:],
                                      WCOL[:, wslot * 2048 + hc * 128: wslot * 2048 + (hc + 1) * 128],
                                      rhsbuf[:, hc * 1024 + itl * 512: hc * 1024 + (itl + 1) * 512],
                                      start=(hc == 0), stop=(hc == 15))
                    i.then_inc(sm["tG"], 1)
                    t_.wait_ge(sm["dW4"], 32 * fc + 32)
                    if idx == 0:
                        t_.wait_ge(sm["aE4"], 64)
                    if idx >= 2:
                        t_.wait_ge(sm["aU"], idx - 1)
                    for hc in range(16):
                        i = t_.matmul(pb[6 + idx % 2][:],
                                      WCOL[:, 4096 + wslot * 2048 + hc * 128:
                                           4096 + wslot * 2048 + (hc + 1) * 128],
                                      rhsbuf[:, hc * 1024 + itl * 512: hc * 1024 + (itl + 1) * 512],
                                      start=(hc == 0), stop=(hc == 15))
                    i.then_inc(sm["tU"], 1)
            for it2 in range(4):
                for oc in range(16):
                    jdx = it2 * 16 + oc
                    t_.wait_ge(sm["dWd"], 16 * (jdx + 1))
                    if jdx == 0:
                        t_.wait_ge(sm["aG"], FCN * 4)
                        t_.wait_ge(sm["aU"], FCN * 4)
                    if jdx >= 2:
                        t_.wait_ge(sm["aD"], jdx - 1)
                    slot = oc % 2
                    for fc in range(FCN):
                        if oc == 0 and fc % 11 == 0:
                            t_.wait_ge(sm["dHl"], 16 * (it2 * 4 + fc // 11 + 1))
                        i = t_.matmul(pb[4 + jdx % 2][:],
                                      STRM[:, slot * 8192 + fc * 128: slot * 8192 + (fc + 1) * 128],
                                      BIG0[:, fc * 512:(fc + 1) * 512],
                                      start=(fc == 0), stop=(fc == FCN - 1))
                    i.then_inc(sm["tD"], 1)
                t_.wait_ge(sm["aD"], it2 * 16 + 16)
                for tt in range(4):
                    To = it2 * 4 + tt
                    for og in range(4):
                        fgi = To * 4 + og
                        if fgi < 2:
                            t_.wait_ge(sm["vTR3"], 128)
                        if fgi >= 2:
                            t_.wait_ge(sm["vTF"], fgi - 1)
                        for k4 in range(4):
                            oc = og * 4 + k4
                            t_.transpose(bb[fgi % 2][:, k4 * 128:(k4 + 1) * 128],
                                         BIG0[:, 24576 + oc * 512 + tt * 128:
                                              24576 + oc * 512 + tt * 128 + 128],
                                         IDT[:]).then_inc(sm["tTF"], 1)

    return nc


def _pack(Wp):
    K, M = Wp.shape
    R, C = K // 128, M // 128
    return np.ascontiguousarray(
        Wp.reshape(R, 128, C, 128).transpose(2, 1, 0, 3).reshape(C, 128, R * 128))


def _prep_shared(i):
    wqkv = np.asarray(i["L0_attn_w_qkv"], np.float32)
    m = {
        "wq_p": _pack(np.ascontiguousarray(wqkv[0:2048].T)),
        "wk_p": _pack(np.ascontiguousarray(wqkv[2048:4096])),
        "wv_p": _pack(np.ascontiguousarray(wqkv[4096:6144].T)),
        "wo_p": _pack(np.ascontiguousarray(np.asarray(i["L0_attn_w_out"], np.float32).T)),
        "wg_p": _pack(np.ascontiguousarray(np.asarray(i["L0_ffn_gate_w"], np.float32).T)),
        "wu_p": _pack(np.ascontiguousarray(np.asarray(i["L0_ffn_up_w"], np.float32).T)),
        "wd_p": _pack(np.ascontiguousarray(np.asarray(i["L0_ffn_down_w"], np.float32).T)),
        "bq": np.ascontiguousarray(np.asarray(i["L0_attn_b_qkv"], np.float32)[0:2048].reshape(16, 128).T),
        "bv": np.ascontiguousarray(np.asarray(i["L0_attn_b_qkv"], np.float32)[4096:6144].reshape(16, 128).T),
        "bo": np.ascontiguousarray(np.asarray(i["L0_attn_b_out"], np.float32).reshape(16, 128).T),
        "ds": np.ascontiguousarray(np.asarray(i["L0_ffn_down_s"], np.float32).reshape(16, 128).T),
        "gs": np.ascontiguousarray(np.asarray(i["L0_ffn_gate_s"], np.float32).reshape(44, 128).T),
        "us": np.ascontiguousarray(np.asarray(i["L0_ffn_up_s"], np.float32).reshape(44, 128).T),
        "ident": np.eye(128, dtype=ml_dtypes.bfloat16),
    }
    return m


def kernel(**inputs):
    x = np.asarray(inputs["L0_attn_hidden_states"], np.float32).reshape(S, D)
    shared = _prep_shared(inputs)
    nc = build()
    in_maps = []
    for c in range(8):
        m = dict(shared)
        m["x"] = np.ascontiguousarray(np.roll(x, -2048 * c, axis=0))
        in_maps.append(m)
    res = run_bass_kernel_spmd(nc, in_maps, core_ids=list(range(8)))
    out = np.concatenate([np.asarray(res.results[c]["out"]) for c in range(8)], axis=0)
    return out.reshape(1, S, D).astype(np.float32)
